# version 08
# baseline (speedup 1.0000x reference)
import numpy as np
import ml_dtypes
from contextlib import ExitStack

import concourse.bass as bass
import concourse.mybir as mybir
from concourse.bass import IndirectOffsetOnAxis
from concourse.bass_utils import run_bass_kernel_spmd

F32 = mybir.dt.float32
BF16 = mybir.dt.bfloat16
I32 = mybir.dt.int32
AF = mybir.ActivationFunctionType
ALU = mybir.AluOpType
AX = mybir.AxisListType

NCORES = 8
RG = [[0, 1, 2, 3, 4, 5, 6, 7]]
T = 2048
D = 1024
DC = 8
TW = 512
NW = T // TW
NT = T // 128
LN_EPS = 1e-5
FFN_H = 4096
HC = 512
NHC = FFN_H // HC
L1_NORM = 1.0 / (64.0 ** 0.25)
F_EXP = 512
F_SH = 1024



class Sched:

    ENGINES = ("sync", "gpsimd", "tensor", "vector", "scalar")
    SEMS = ("ld", "wt", "mm", "ev", "sc", "gp", "st")

    DEBUG = False

    def __init__(self, nc, stack):
        self.nc = nc
        self.stack = stack
        self.q = {e: [] for e in self.ENGINES}
        self.sems = {n: stack.enter_context(nc.semaphore(n)) for n in self.SEMS}
        self.cnt = {n: 0 for n in self.SEMS}
        self.floor = {e: {n: 0 for n in self.SEMS} for e in self.ENGINES}
        self.log = []

    def new_sem(self, name):
        self.sems[name] = self.stack.enter_context(self.nc.semaphore(name))
        self.cnt[name] = 0
        for e in self.ENGINES:
            self.floor[e][name] = 0
        return name

    def emit(self, eng, fn, waits=(), inc=None, dma=False, inc_n=None):
        n = inc_n if inc_n is not None else (16 if dma else 1)
        fl = self.floor[eng]
        real_waits = []
        for s_, v in waits:
            if v > fl[s_]:
                real_waits.append((s_, v))
                fl[s_] = v
        target = None
        if inc is not None:
            self.cnt[inc] += n
            target = self.cnt[inc]
            if not dma and inc_n is None:
                fl[inc] = max(fl[inc], target)
        if Sched.DEBUG:
            import sys as _s
            self.log.append((eng, _s._getframe(1).f_lineno,
                             tuple(real_waits), inc, target))
        sems = self.sems

        def cl(e, fn=fn, rw=tuple(real_waits), inc=inc, n=n):
            for s_, v in rw:
                e.wait_ge(sems[s_], v)
            ins = fn(e)
            if inc is not None:
                ins.then_inc(sems[inc], n)
        self.q[eng].append(cl)
        return target

    def drain(self, eng):
        self.q[eng].append(lambda e: e.drain())

    def c(self, name):
        return self.cnt[name]

    def finish(self):
        with self.nc.Block() as block:
            for engname in self.ENGINES:
                ops = self.q[engname]
                if not ops:
                    continue
                getattr(block, engname)(
                    lambda e, ops=ops: [f(e) for f in ops] and None
                )


def _alloc(stack, nc, name, shape, dtype):
    return stack.enter_context(nc.sbuf_tensor(name, shape, dtype))


def _psum(stack, nc, name, shape, dtype=F32):
    return stack.enter_context(nc.psum_tensor(name, shape, dtype))



def emit_layernorm(s, x_sb, xn_sb, aux, colsum_to=None):
    ps_sum = aux["ps_stat0"]
    ps_sq = aux["ps_stat1"]
    ps_m = aux["ps_bc0"]
    ps_r = aux["ps_bc1"]
    sq = aux["sq_scr"]
    rows = aux["rows"]
    ones_col = aux["ones_col"]
    ones_row = aux["ones_row"]
    x_ready = aux["x_ready"]
    pre = tuple(aux.get("pre_waits", ()))

    sq_gate = [0, 0]
    st = dict(stats_read=0,
              bc_read=0,
              sc_sd=0)
    ev_r = [0, 0]
    bc_done = [0, 0]

    def stage_a(w):
        p = w % 2
        last = None
        for c in range(DC):
            xin = x_sb[:, c * T + w * TW: c * T + (w + 1) * TW]
            sqv = sq[:, (c % 2) * TW: (c % 2 + 1) * TW]
            sc_done = s.emit(
                "scalar",
                lambda e, xin=xin, sqv=sqv: e.square(sqv, xin),
                waits=(*pre, x_ready, ("mm", sq_gate[c % 2])),
                inc="sc")
            s.emit(
                "tensor",
                lambda e, xin=xin, c=c: e.matmul(
                    ps_sum, ones_col, xin, start=(c == 0), stop=(c == DC - 1)),
                waits=(*pre, x_ready, ("ev", st["stats_read"])))
            mm_i = s.emit(
                "tensor",
                lambda e, sqv=sqv, c=c: e.matmul(
                    ps_sq, ones_col, sqv, start=(c == 0), stop=(c == DC - 1)),
                waits=(("sc", sc_done),),
                inc="mm")
            sq_gate[c % 2] = mm_i
            last = mm_i
        m_row = rows[:, p * TW: (p + 1) * TW]
        r_row = rows[:, (2 + p) * TW: (3 + p) * TW]
        t4 = rows[:, 4 * TW: 5 * TW]
        t5 = rows[:, 5 * TW: 6 * TW]
        s.emit("vector",
               lambda e, m_row=m_row: e.tensor_scalar_mul(m_row, ps_sum, 1.0 / D),
               waits=(("mm", last), ("mm", bc_done[p])))
        st["stats_read"] = s.emit(
            "vector", lambda e: e.tensor_scalar_mul(t4, ps_sq, 1.0 / D),
            waits=(("sc", st["sc_sd"]),), inc="ev")
        s.drain("vector")
        s.emit("vector", lambda e, m_row=m_row: e.tensor_mul(t5, m_row, m_row))
        s.drain("vector")
        ev_var = s.emit("vector", lambda e: e.tensor_sub(t4, t4, t5),
                        inc="ev")
        st["sc_sd"] = s.emit(
            "scalar",
            lambda e: e.activation(t4, t4, AF.Sqrt, bias=aux["eps_row"]),
            waits=(("ev", ev_var),), inc="sc")
        ev_r[p] = s.emit("vector",
                         lambda e, r_row=r_row: e.reciprocal(r_row, t4),
                         waits=(("sc", st["sc_sd"]),), inc="ev")

    def stage_b(w):
        p = w % 2
        m_row = rows[:, p * TW: (p + 1) * TW]
        r_row = rows[:, (2 + p) * TW: (3 + p) * TW]
        mm_bc = s.emit(
            "tensor",
            lambda e, m_row=m_row, r_row=r_row: [
                e.matmul(ps_m, ones_row[:1, 0:128], m_row, start=True, stop=True),
                e.matmul(ps_r, ones_row[:1, 0:128], r_row, start=True, stop=True),
            ][-1],
            waits=(("ev", ev_r[p]), ("ev", st["bc_read"])), inc="mm")
        bc_done[p] = mm_bc
        diff = aux["diff_scr"]
        ev_xn = 0

        def em_diff(c):
            xin = x_sb[:, c * T + w * TW: c * T + (w + 1) * TW]
            dv = diff[:, (c % 2) * TW: (c % 2 + 1) * TW]
            s.emit("vector",
                   lambda e, xin=xin, dv=dv: e.tensor_sub(dv, xin, ps_m),
                   waits=(("mm", mm_bc),))

        def em_xn(c):
            nonlocal ev_xn
            xno = xn_sb[:, c * T + w * TW: c * T + (w + 1) * TW]
            dv = diff[:, (c % 2) * TW: (c % 2 + 1) * TW]
            if colsum_to is not None:
                cs = colsum_to[:, c * NW + w: c * NW + w + 1]
                ev_xn = s.emit(
                    "vector",
                    lambda e, dv=dv, xno=xno, cs=cs: e.scalar_tensor_tensor(
                        xno, dv, 1.0, ps_r, ALU.mult, ALU.mult, accum_out=cs),
                    inc="ev")
            else:
                ev_xn = s.emit(
                    "vector",
                    lambda e, dv=dv, xno=xno: e.scalar_tensor_tensor(
                        xno, dv, 1.0, ps_r, ALU.mult, ALU.mult),
                    inc="ev")

        em_diff(0)
        em_diff(1)
        for c in range(DC):
            s.drain("vector")
            em_xn(c)
            if c + 2 < DC:
                em_diff(c + 2)
        st["bc_read"] = ev_xn
        aux["xn_done"][w] = ev_xn

    stage_a(0)
    for w in range(NW):
        if w + 1 < NW:
            stage_a(w + 1)
        stage_b(w)



def build_fused():
    nc = bass.Bass("TRN2", target_bir_lowering=False)
    dr = {}

    def din(name, shape, dtype=F32):
        dr[name] = nc.dram_tensor(name, shape, dtype, kind="ExternalInput").ap()
        return dr[name]

    x_dr = din("x_fm", [D, T])
    wkT_dr = din("wkT", [D, D])
    wvT_dr = din("wvT", [D, D])
    bk_dr = din("bk", [1, D])
    bv_dr = din("bv", [1, D])
    alpha_dr = din("alpha", [1, 1])
    wgate_dr = din("wgate", [D, 8])
    bgate_dr = din("bgate", [1, 8])
    w1m_dr = din("W1m", [D, FFN_H])
    w2m_dr = din("W2m", [FFN_H, D])
    b2m_dr = din("b2m", [1, D])
    w1e_dr = din("W1e", [8 * D, FFN_H])
    b1e_dr = din("b1e", [8, FFN_H])
    w2e_dr = din("W2e", [8 * FFN_H, D])
    b2e_dr = din("b2e", [8, D])
    wq_dr = din("wqT", [D, D])
    wo_dr = din("woT", [D, D])
    boc_dr = din("bo_cols", [128, 8])
    wk1_dr = din("wk1T", [D, D])
    wv1_dr = din("wv1T", [D, D])
    bv1_dr = din("bv1", [1, D])
    pt_dr = din("PT", [64, 64])
    id_dr = din("ident", [128, 128])
    b1mc_dr = din("b1m_cols", [128, 32])
    bqc_dr = din("bq_cols", [128, 8])
    bk1c_dr = din("bk1_cols", [64, 16])
    wq1_dr = din("wq1T", [D, D])
    woP_dr = din("wo1P", [128, 8 * D])
    bo1c_dr = din("bo1_cols", [128, 8])
    bq1c_dr = din("bq1_cols", [64, 16])
    rw_dr = din("router_w", [D, 16])
    gw_dr = din("gate_w", [16 * D, F_EXP])
    uw_dr = din("up_w", [16 * D, F_EXP])
    dw_dr = din("down_w", [16 * F_EXP, D])
    sg_dr = din("sgate_w", [D, F_SH])
    su_dr = din("sup_w", [D, F_SH])
    sd_dr = din("sdown_w", [F_SH, D])

    out_dr = nc.dram_tensor("out_fm", [D, T], F32, kind="ExternalOutput").ap()

    cc_cs_i = nc.dram_tensor("cc_cs_i", [128, 8], F32)
    cc_cs_o = nc.dram_tensor("cc_cs_o", [128, 8], F32)
    cc_m_i = nc.dram_tensor("cc_m_i", [128, 1024], F32)
    cc_m_o = nc.dram_tensor("cc_m_o", [128, 1024], F32)
    cc_kv_i = nc.dram_tensor("cc_kv_i", [64, 16 * 65], F32)
    cc_kv_o = nc.dram_tensor("cc_kv_o", [64, 16 * 65], F32)
    x1_dram = nc.dram_tensor("x1b", [D, T], F32)
    x1_dr = x1_dram.ap()

    with ExitStack() as stack:
        s = Sched(nc, stack)
        a_sb = _alloc(stack, nc, "a_sb", [128, DC * T], F32)
        xn_sb = _alloc(stack, nc, "xn_sb", [128, DC * T], BF16)
        g_sb = _alloc(stack, nc, "g_sb", [128, DC * T], BF16)
        w_sb = _alloc(stack, nc, "w_sb", [128, DC * T], BF16)
        gf = g_sb[:].bitcast(F32)
        m32_sb = _alloc(stack, nc, "m32_sb", [128, 8 * 128], F32)
        m_sb = _alloc(stack, nc, "m_sb", [128, 8 * 128], BF16)
        o_sb = _alloc(stack, nc, "o_sb", [128, 8 * 512], BF16)
        of32 = o_sb[:].bitcast(F32)
        sq_q = _alloc(stack, nc, "sq_q", [128, 2 * 512], BF16)
        t_scr = _alloc(stack, nc, "t_scr", [128, 2 * 512], F32)
        brow_sb = _alloc(stack, nc, "brow_sb", [1, 4096], F32)
        bv_row_sb = _alloc(stack, nc, "bv_row", [1, D], F32)
        kvb_sb = _alloc(stack, nc, "kvb_sb", [64, 16 * 65], BF16)
        cs_w = _alloc(stack, nc, "cs_w", [128, DC * NW], F32)
        cs8 = _alloc(stack, nc, "cs8", [128, 8], F32)
        wg_sb = _alloc(stack, nc, "wg_sb", [128, 8 * 8], F32)
        cols_sb = _alloc(stack, nc, "cols_sb", [128, 16], F32)
        idx_sb = _alloc(stack, nc, "idx_sb", [128, 8], I32)
        bidx_sb = _alloc(stack, nc, "bidx_sb", [1, 2], I32)
        iota_sb = _alloc(stack, nc, "iota_sb", [128, 1], F32)
        cvec_sb = _alloc(stack, nc, "cvec_sb", [128, 8], F32)
        rrow_sb = _alloc(stack, nc, "rrow_sb", [1, 64], F32)
        b1mc_sb = _alloc(stack, nc, "b1mc_sb", [128, 32], F32)
        b1c_sb = _alloc(stack, nc, "b1c_sb", [128, 8], F32)
        bqc_sb = _alloc(stack, nc, "bqc_sb", [128, 8], F32)
        boc_sb = _alloc(stack, nc, "boc_sb", [128, 8], F32)
        bo1c_sb = _alloc(stack, nc, "bo1c_sb", [128, 8], F32)
        bk1c_sb = _alloc(stack, nc, "bk1c_sb", [64, 16], F32)
        bq1c_sb = _alloc(stack, nc, "bq1c_sb", [64, 16], F32)
        eps_ln = _alloc(stack, nc, "eps_ln", [1, 1], F32)
        eps_r6 = _alloc(stack, nc, "eps_r6", [1, 1], F32)
        eps_col = _alloc(stack, nc, "eps_col", [128, 1], F32)
        ones_row = _alloc(stack, nc, "ones_row", [1, TW], F32)
        ones_col = _alloc(stack, nc, "ones_col", [128, 1], F32)
        onesb_col = _alloc(stack, nc, "onesb_col", [128, 1], BF16)
        ptb_sb = _alloc(stack, nc, "ptb_sb", [64, 64], BF16)
        id_sb = _alloc(stack, nc, "id_sb", [128, 128], F32)
        ct_sb = _alloc(stack, nc, "ct_sb", [128, 16 * 16], F32)
        pr_sb = _alloc(stack, nc, "pr_sb", [128, 16], F32)
        rcol_sb = _alloc(stack, nc, "rcol_sb", [128, 4], F32)
        rw_sb = _alloc(stack, nc, "rw_sb", [128, 8 * 16], BF16)

        b1st_sb = gf[0:2, 4096:5120]
        b2st_sb = gf[0:2, 5120:6144]
        kvf_sb = gf[0:64, 7152:8192]
        crow_sb = brow_sb[:1, 2048:3072]

        ps = [_psum(stack, nc, f"ps{i}", [128, 512], F32) for i in range(8)]

        pre = s.new_sem("pre")
        preg = s.new_sem("preg")
        c1i = s.new_sem("c1i"); c1c = s.new_sem("c1c"); c1o = s.new_sem("c1o")
        c2i = s.new_sem("c2i"); c2c = s.new_sem("c2c"); c2o = s.new_sem("c2o")
        c3i = s.new_sem("c3i"); c3c = s.new_sem("c3c"); c3o = s.new_sem("c3o")
        w1s = [s.new_sem("w1A"), s.new_sem("w1B")]
        w2s = [s.new_sem("w2A"), s.new_sem("w2B")]
        b2ch = s.new_sem("b2ch")
        b2g = s.new_sem("b2g")
        dch = s.new_sem("dch")
        dchx = s.new_sem("dchx")
        dg = s.new_sem("dg")
        fch = s.new_sem("fch")
        fg = s.new_sem("fg")
        slch = s.new_sem("slch")
        d3ch = s.new_sem("d3ch")
        d3chx = s.new_sem("d3chx")
        d3g = s.new_sem("d3g")

        init = s.emit("vector", lambda e: [
            e.memset(ones_row[:], 1.0),
            e.memset(ones_col[:], 1.0),
            e.memset(onesb_col[:], 1.0),
            e.memset(eps_ln[:], LN_EPS),
            e.memset(eps_r6[:], 1e-6),
            e.memset(eps_col[:], 1e-6),
        ][-1], inc="ev")
        gp_iota = s.emit("gpsimd", lambda e: [
            e.iota(iota_sb[:, 0:1], [[0, 1]], channel_multiplier=1,
                   allow_small_or_imprecise_dtypes=True),
            e.iota(rrow_sb[:1, 48:56], [[1, 8]], channel_multiplier=0,
                   allow_small_or_imprecise_dtypes=True),
        ][-1], inc="gp")

        s.emit("sync", lambda e: e.dma_start(
            out=a_sb[:].rearrange("p (c t) -> p c t", c=DC),
            in_=x_dr.rearrange("(c p) t -> p c t", p=128)), inc=pre, dma=True)
        s.emit("gpsimd", lambda e: e.dma_start(
            out=w_sb[:, 0:8192].rearrange("p (c j) -> p c j", c=DC),
            in_=wkT_dr.rearrange("(c p) j -> p c j", p=128)), inc=preg, dma=True)
        s.emit("gpsimd", lambda e: e.dma_start(
            out=w_sb[:, 8192:16384].rearrange("p (c j) -> p c j", c=DC),
            in_=wvT_dr.rearrange("(c p) j -> p c j", p=128)), inc=preg, dma=True)
        s.emit("sync", lambda e: e.dma_start(out=brow_sb[:1, 3072:4096], in_=bk_dr),
               inc=pre, dma=True)
        s.emit("sync", lambda e: e.dma_start(out=bv_row_sb[:], in_=bv_dr),
               inc=pre, dma=True)
        s.emit("sync", lambda e: e.dma_start(
            out=wg_sb[:].rearrange("p (c g) -> p c g", c=DC),
            in_=wgate_dr.rearrange("(c p) g -> p c g", p=128)), inc=pre, dma=True)
        s.emit("sync", lambda e: e.dma_start(out=rrow_sb[:1, 32:40], in_=bgate_dr),
               inc=pre, dma=True)
        s.emit("sync", lambda e: e.dma_start(out=rrow_sb[:1, 40:41], in_=alpha_dr),
               inc=pre, dma=True)
        s.emit("sync", lambda e: e.dma_start(out=b1mc_sb[:], in_=b1mc_dr),
               inc=pre, dma=True)
        s.emit("sync", lambda e: e.dma_start(out=bqc_sb[:], in_=bqc_dr),
               inc=pre, dma=True)
        s.emit("sync", lambda e: e.dma_start(out=boc_sb[:], in_=boc_dr),
               inc=pre, dma=True)
        s.emit("sync", lambda e: e.dma_start(out=bo1c_sb[:], in_=bo1c_dr),
               inc=pre, dma=True)
        s.emit("sync", lambda e: e.dma_start(out=bk1c_sb[:], in_=bk1c_dr),
               inc=pre, dma=True)
        s.emit("sync", lambda e: e.dma_start(out=bq1c_sb[:], in_=bq1c_dr),
               inc=pre, dma=True)
        s.emit("sync", lambda e: e.dma_start(out=id_sb[:], in_=id_dr),
               inc=pre, dma=True)
        s.emit("gpsimd", lambda e: e.dma_start(out=ptb_sb[:], in_=pt_dr),
               inc=preg, dma=True)
        s.emit("gpsimd", lambda e: e.dma_start(
            out=rw_sb[:].rearrange("p (c g) -> p c g", c=DC),
            in_=rw_dr.rearrange("(c p) g -> p c g", p=128)), inc=preg, dma=True)
        pre_t = s.c(pre)
        preg_t = s.c(preg)

        aux0 = dict(
            eps_row=eps_ln[:1, :1],
            ps_stat0=ps[0][0:1, :], ps_stat1=ps[1][0:1, :],
            ps_bc0=ps[2][:], ps_bc1=ps[3][:],
            sq_scr=of32[:, 0:1024], diff_scr=of32[:, 1024:2048],
            rows=brow_sb[:1, 0:3072],
            ones_col=ones_col[:], ones_row=ones_row,
            x_ready=(pre, pre_t),
            pre_waits=(("ev", init),),
            xn_done=[0] * NW,
        )
        emit_layernorm(s, a_sb, xn_sb, aux0, colsum_to=cs_w)
        ln0_done = aux0["xn_done"]

        ev_cs = 0
        for c in range(DC):
            ev_cs = s.emit("vector", lambda e, c=c: e.reduce_sum(
                cs8[:, c:c + 1],
                cs_w[:].rearrange("p (c w) -> p c w", c=DC)[:, c, :],
                axis=AX.X), inc="ev" if c == DC - 1 else None)
        s.emit("sync", lambda e: e.dma_start(out=cc_cs_i[:, :], in_=cs8[:]),
               waits=(("ev", ev_cs),), inc=c1i, dma=True)
        s.emit("gpsimd", lambda e: e.collective_compute(
            "AllReduce", ALU.add, replica_groups=RG,
            ins=[cc_cs_i.ap().opt()], outs=[cc_cs_o.ap().opt()]),
            waits=((c1i, 16),), inc=c1c, dma=True, inc_n=1)
        s.emit("sync", lambda e: e.dma_start(out=cs8[:], in_=cc_cs_o[:, :]),
               waits=((c1c, 1),), inc=c1o, dma=True)
        c1o_t = s.c(c1o)

        kg_sb = g_sb[:, 0:8192]
        vg_sb = g_sb[:, 8192:16384]
        wk_v = w_sb[:, 0:8192]
        wv_v = w_sb[:, 8192:16384]
        bk_row = brow_sb[:1, 3072:4096]
        evk_gate = [0, 0]
        scv_last = 0
        kv_used = 0
        ev_m = 0
        m_ps_free = 0
        mm_p1w = 0
        ev_kg_last = 0
        scv_p1_last = 0
        for g in range(2):
            ev_k = sc_v = 0
            for tt in range(NT):
                slot = tt % 2
                kg = kg_sb[:, tt * 512:(tt + 1) * 512]
                vg = vg_sb[:, tt * 512:(tt + 1) * 512]
                kgp = ps[4 + slot][:, 0:512]
                vgp = ps[6][:, 0:512]
                w = tt // (NT // NW)
                xw = ("ev", ln0_done[w])

                def proj(e, g=g, tt=tt, kgp=kgp, vgp=vgp):
                    for c in range(DC):
                        xsl = xn_sb[:, c * T + tt * 128: c * T + (tt + 1) * 128]
                        wks = wk_v[:, c * D + g * 512: c * D + (g + 1) * 512]
                        wvs = wv_v[:, c * D + g * 512: c * D + (g + 1) * 512]
                        e.matmul(kgp, xsl, wks, start=(c == 0), stop=False)
                        e.matmul(vgp, xsl, wvs, start=(c == 0), stop=False)
                    e.matmul(kgp, ones_row[:1, 0:128],
                             bk_row[:1, g * 512:(g + 1) * 512], start=False, stop=True)
                    return e.matmul(vgp, ones_row[:1, 0:128],
                                    bv_row_sb[:1, g * 512:(g + 1) * 512],
                                    start=False, stop=True)
                mm_proj = s.emit(
                    "tensor", proj,
                    waits=(xw, (pre, pre_t), (preg, preg_t),
                           ("ev", evk_gate[slot]), ("sc", scv_last),
                           ("mm", kv_used)),
                    inc="mm")
                mm_p1w = mm_proj
                ev_k = s.emit("vector",
                              lambda e, kg=kg, kgp=kgp: e.tensor_copy(kg, kgp),
                              waits=(("mm", mm_proj),), inc="ev")
                sc_v = s.emit("scalar",
                              lambda e, vg=vg, vgp=vgp: e.copy(vg, vgp),
                              waits=(("mm", mm_proj),), inc="sc")
                evk_gate[slot] = ev_k
                scv_last = sc_v
            ev_kg_last = ev_k
            scv_p1_last = sc_v
            mm_m = 0
            for hh in range(4):
                def macc(e, hh=hh):
                    last = None
                    for tt in range(NT):
                        last = e.matmul(
                            ps[7][:, hh * 128:(hh + 1) * 128],
                            kg_sb[:, tt * 512 + hh * 128: tt * 512 + (hh + 1) * 128],
                            vg_sb[:, tt * 512 + hh * 128: tt * 512 + (hh + 1) * 128],
                            start=(tt == 0), stop=(tt == NT - 1))
                    return last
                mm_m = s.emit("tensor", macc,
                              waits=(("ev", ev_k), ("sc", sc_v),
                                     ("ev", m_ps_free)), inc="mm")
            for hh in range(4):
                h = g * 4 + hh
                ev_m = s.emit("vector", lambda e, h=h, hh=hh: e.tensor_copy(
                    m32_sb[:, h * 128:(h + 1) * 128], ps[7][:, hh * 128:(hh + 1) * 128]),
                    waits=(("mm", mm_m),), inc="ev" if hh == 3 else None)
            kv_used = mm_m
            m_ps_free = ev_m
        mm_p1g = kv_used

        s.emit("sync", lambda e: e.dma_start(out=cc_m_i[:, :], in_=m32_sb[:]),
               waits=(("ev", ev_m),), inc=c2i, dma=True)
        s.emit("gpsimd", lambda e: e.collective_compute(
            "AllReduce", ALU.add, replica_groups=RG,
            ins=[cc_m_i.ap().opt()], outs=[cc_m_o.ap().opt()]),
            waits=((c2i, 16),), inc=c2c, dma=True, inc_n=1)
        s.emit("sync", lambda e: e.dma_start(out=m32_sb[:], in_=cc_m_o[:, :]),
               waits=((c2c, 1),), inc=c2o, dma=True)
        c2o_t = s.c(c2o)

        R = rrow_sb
        lg = R[:1, 0:8]
        scr = R[:1, 8:16]
        probs = R[:1, 16:24]
        scr2 = R[:1, 24:32]
        bgate = R[:1, 32:40]
        iota8 = R[:1, 48:56]
        mm_lg = s.emit("tensor", lambda e: [
            e.matmul(ps[0][0:1, 0:8], cs8[:, c:c + 1], wg_sb[:, c * 8:(c + 1) * 8],
                     start=(c == 0), stop=(c == DC - 1))
            for c in range(DC)][-1],
            waits=((c1o, c1o_t), (pre, pre_t), ("ev", ev_cs)), inc="mm")

        def rt(e):
            e.tensor_scalar(lg, ps[0][0:1, 0:8], 1.0 / 16384.0, None, ALU.mult)
            e.drain()
            e.tensor_add(lg, lg, bgate)
            e.drain()
            e.reduce_max(R[:1, 41:42], lg, axis=AX.X)
            e.drain()
            return e.tensor_scalar(R[:1, 42:43], R[:1, 41:42], -1.0, None, ALU.mult)
        ev_rt1 = s.emit("vector", rt, waits=(("mm", mm_lg), (pre, pre_t),
                                             ("ev", init)), inc="ev")
        sc_exp = s.emit(
            "scalar",
            lambda e: e.activation(scr, lg, AF.Exp, bias=R[:1, 42:43]),
            waits=(("ev", ev_rt1),), inc="sc")

        def rt2(e):
            e.reduce_sum(R[:1, 44:45], scr, axis=AX.X)
            e.drain()
            e.reciprocal(R[:1, 45:46], R[:1, 44:45])
            e.drain()
            e.tensor_scalar(probs, scr, R[:1, 45:46], None, ALU.mult)
            e.drain()
            e.reduce_max(R[:1, 56:57], probs, axis=AX.X)
            e.drain()
            e.tensor_scalar(scr2, probs, R[:1, 56:57], None, ALU.is_ge)
            e.drain()
            e.tensor_scalar(scr, scr2, -9.0, 9.0, ALU.mult, ALU.add)
            e.drain()
            e.tensor_add(scr, scr, iota8)
            e.drain()
            e.tensor_reduce(R[:1, 57:58], scr, AX.X, ALU.min)
            e.drain()
            e.scalar_tensor_tensor(scr, scr2, -2.0, probs, ALU.mult, ALU.add)
            e.drain()
            e.reduce_max(R[:1, 58:59], scr, axis=AX.X)
            e.drain()
            e.tensor_scalar(scr2, scr, R[:1, 58:59], None, ALU.is_ge)
            e.drain()
            e.tensor_scalar(scr2, scr2, -9.0, 9.0, ALU.mult, ALU.add)
            e.drain()
            e.tensor_add(scr2, scr2, iota8)
            e.drain()
            e.tensor_reduce(R[:1, 59:60], scr2, AX.X, ALU.min)
            e.drain()
            e.tensor_add(R[:1, 60:61], R[:1, 56:57], R[:1, 58:59])
            e.drain()
            e.reciprocal(R[:1, 61:62], R[:1, 60:61])
            e.drain()
            e.tensor_mul(R[:1, 62:63], R[:1, 56:57], R[:1, 61:62])
            e.tensor_mul(R[:1, 63:64], R[:1, 58:59], R[:1, 61:62])
            e.drain()
            e.tensor_scalar(R[:1, 8:9], R[:1, 40:41], 0.1, 1.0, ALU.max, ALU.min)
            e.drain()
            e.tensor_scalar(R[:1, 24:25], R[:1, 8:9], -1.0, 1.0, ALU.mult, ALU.add)
            e.drain()
            e.tensor_mul(R[:1, 9:10], R[:1, 24:25], R[:1, 62:63])
            e.tensor_mul(R[:1, 10:11], R[:1, 24:25], R[:1, 63:64])
            e.tensor_scalar(R[:1, 11:12], R[:1, 57:58], 1024.0, None, ALU.mult)
            e.tensor_scalar(R[:1, 12:13], R[:1, 59:60], 1024.0, None, ALU.mult)
            e.tensor_scalar(R[:1, 13:14], R[:1, 57:58], 4096.0, None, ALU.mult)
            e.tensor_scalar(R[:1, 14:15], R[:1, 59:60], 4096.0, None, ALU.mult)
            e.tensor_copy(R[:1, 15:16], R[:1, 57:58])
            e.tensor_copy(R[:1, 16:17], R[:1, 59:60])
            e.drain()
            e.tensor_copy(bidx_sb[:1, 0:1], R[:1, 57:58])
            return e.tensor_copy(bidx_sb[:1, 1:2], R[:1, 59:60])
        ev_rt2 = s.emit("vector", rt2, waits=(("sc", sc_exp), ("gp", gp_iota)),
                        inc="ev")
        mm_bcr = s.emit("tensor", lambda e: e.matmul(
            ps[2][:, 0:9], ones_row[:1, 0:128], R[:1, 8:17], start=True, stop=True),
            waits=(("ev", ev_rt2),), inc="mm")

        def mkcols(e):
            e.tensor_copy(cols_sb[:, 0:9], ps[2][:, 0:9])
            e.drain()
            e.tensor_scalar(cols_sb[:, 9:13], cols_sb[:, 3:7], iota_sb[:, 0:1],
                            None, ALU.add)
            e.drain()
            e.tensor_copy(idx_sb[:, 0:4], cols_sb[:, 9:13])
            return e.tensor_copy(idx_sb[:, 4:6], cols_sb[:, 7:9])
        routing_done = s.emit("vector", mkcols,
                              waits=(("mm", mm_bcr), ("gp", gp_iota)), inc="ev")
        a_col = cols_sb[:, 0:1]
        c1_col = cols_sb[:, 1:2]
        c2_col = cols_sb[:, 2:3]

        s.emit("sync", lambda e: e.dma_start(
            out=brow_sb[:1, D:2 * D], in_=b2m_dr),
            waits=(("mm", mm_bcr),), inc=b2ch, dma=True)
        s.emit("gpsimd", lambda e: e.indirect_dma_start(
            out=b2st_sb[0:2, :], out_offset=None, in_=b2e_dr,
            in_offset=IndirectOffsetOnAxis(idx_sb[0:2, 4:5], 0)),
            waits=(("ev", routing_done), ("mm", mm_p1g)), inc=b2g, dma=True)
        wt_b2a = s.c(b2g)
        ev_b2a = s.emit("vector", lambda e: e.tensor_copy(
            brow_sb[:1, 2 * D:3 * D], b2st_sb[0:1, :]),
            waits=((b2g, wt_b2a), ("mm", mm_bcr)), inc="ev")
        s.emit("gpsimd", lambda e: e.indirect_dma_start(
            out=b2st_sb[0:2, :], out_offset=None, in_=b2e_dr,
            in_offset=IndirectOffsetOnAxis(idx_sb[0:2, 5:6], 0)),
            waits=(("ev", ev_b2a),), inc=b2g, dma=True)
        wt_b2b = s.c(b2g)
        ev_b2b = s.emit("vector", lambda e: e.tensor_copy(
            brow_sb[:1, 3 * D:4 * D], b2st_sb[0:1, :]),
            waits=((b2g, wt_b2b), (b2ch, s.c(b2ch)), ("mm", mm_p1w)), inc="ev")

        def cvec(e):
            cv = brow_sb[:1, 0:D]
            tv = brow_sb[:1, D:2 * D]
            e.tensor_scalar(cv, brow_sb[:1, D:2 * D], R[:1, 8:9], None, ALU.mult)
            e.drain()
            e.tensor_scalar(tv, brow_sb[:1, 2 * D:3 * D], R[:1, 9:10], None, ALU.mult)
            e.drain()
            e.tensor_add(cv, cv, tv)
            e.drain()
            e.tensor_scalar(tv, brow_sb[:1, 3 * D:4 * D], R[:1, 10:11], None, ALU.mult)
            e.drain()
            return e.tensor_add(cv, cv, tv)
        ev_cvec = s.emit("vector", cvec, waits=(("ev", ev_b2b),), inc="ev")

        slab = 0
        w1_freed = [mm_p1w, mm_p1w]
        w2_freed = [mm_p1w, mm_p1w]
        b1_freed = [mm_p1g, mm_p1g]
        hid_ps_gate = [0, 0]
        b1c_gate = [scv_p1_last, scv_p1_last]
        o_ps_gate = [0, 0]
        hid_used = mm_p1g
        sc_gelu = 0
        ev_ffn = 0
        first_hid = [True, True]
        first_out = [True, True]
        acc_last = {}
        hid_w_last = {}

        for p in range(3):
            sc_col = (a_col, c1_col, c2_col)[p]
            for hc in range(NHC):
                sl = slab % 2
                w1v = w_sb[:, sl * 4096:(sl + 1) * 4096]
                w2v = w_sb[:, 8192 + sl * 4096: 8192 + (sl + 1) * 4096]
                b1v = b1st_sb[0:1, sl * HC:(sl + 1) * HC]
                if p == 0:
                    s.emit("gpsimd", lambda e, hc=hc, w1v=w1v: e.dma_start(
                        out=w1v.rearrange("p (c h) -> p c h", c=DC),
                        in_=w1m_dr.rearrange("(c p) h -> p c h", p=128)
                        [:, :, hc * HC:(hc + 1) * HC]),
                        waits=(("mm", w1_freed[sl]),), inc=w1s[sl], dma=True)
                    wt1 = (w1s[sl], s.c(w1s[sl]))
                    s.emit("gpsimd", lambda e, hc=hc, w2v=w2v: e.dma_start(
                        out=w2v.rearrange("p (q f) -> p q f", q=4),
                        in_=w2m_dr.rearrange("(q p) f -> p q f", p=128)
                        [:, hc * 4:(hc + 1) * 4, :]),
                        waits=(("mm", w2_freed[sl]),), inc=w2s[sl], dma=True)
                    wt2 = (w2s[sl], s.c(w2s[sl]))
                    bt = (pre, pre_t)
                else:
                    iw1 = idx_sb[:, (p - 1):p]
                    iw2 = idx_sb[:, (1 + p):(2 + p)]
                    for c in range(DC):
                        s.emit("gpsimd", lambda e, c=c, hc=hc, w1v=w1v, iw1=iw1: e.indirect_dma_start(
                            out=w1v.rearrange("p (c h) -> p c h", c=DC)[:, c, :],
                            out_offset=None, in_=w1e_dr,
                            in_offset=IndirectOffsetOnAxis(iw1, 0),
                            element_offset=(c * 128) * FFN_H + hc * HC),
                            waits=(("mm", w1_freed[sl]), ("ev", routing_done)),
                            inc=w1s[sl], dma=True)
                    s.emit("gpsimd", lambda e, p=p, hc=hc, sl=sl: e.indirect_dma_start(
                        out=b1st_sb[0:2, sl * HC:(sl + 1) * HC], out_offset=None,
                        in_=b1e_dr,
                        in_offset=IndirectOffsetOnAxis(idx_sb[0:2, (3 + p):(4 + p)], 0),
                        element_offset=hc * HC),
                        waits=(("ev", routing_done), ("mm", b1_freed[sl])),
                        inc=w1s[sl], dma=True)
                    wt1 = (w1s[sl], s.c(w1s[sl]))
                    bt = wt1
                    for qq in range(4):
                        s.emit("gpsimd", lambda e, qq=qq, hc=hc, w2v=w2v, iw2=iw2: e.indirect_dma_start(
                            out=w2v.rearrange("p (q f) -> p q f", q=4)[:, qq, :],
                            out_offset=None, in_=w2e_dr,
                            in_offset=IndirectOffsetOnAxis(iw2, 0),
                            element_offset=(hc * 4 + qq) * 128 * D),
                            waits=(("mm", w2_freed[sl]), ("ev", routing_done)),
                            inc=w2s[sl], dma=True)
                    wt2 = (w2s[sl], s.c(w2s[sl]))
                if p == 0:
                    b1cols = b1mc_sb[:, hc * 4: hc * 4 + 4]
                    b1_ready = bt
                else:
                    mm_b1t = s.emit("tensor", lambda e, b1v=b1v: [
                        e.transpose(ps[6][:, 8 + it: 9 + it],
                                    b1v[:1, it * 128:(it + 1) * 128], ones_row[:1, 0:1])
                        for it in range(4)][-1],
                        waits=(bt, (pre, pre_t), ("sc", b1c_gate[sl])),
                        inc="mm")
                    ev_b1c = s.emit("vector", lambda e, sl=sl: e.tensor_copy(
                        b1c_sb[:, sl * 4:(sl + 1) * 4], ps[6][:, 8:12]),
                        waits=(("mm", mm_b1t),), inc="ev")
                    b1cols = b1c_sb[:, sl * 4: sl * 4 + 4]
                    b1_ready = ("ev", ev_b1c)
                for q in range(4):
                    for w in range(NW):
                        pr = (q * NW + w) % 2
                        pi = ps[pr]
                        def hmm(e, q=q, w=w, pi=pi, w1v=w1v):
                            last = None
                            for c in range(DC):
                                last = e.matmul(
                                    pi[:],
                                    w1v[:, c * HC + q * 128: c * HC + (q + 1) * 128],
                                    xn_sb[:, c * T + w * TW: c * T + (w + 1) * TW],
                                    start=(c == 0), stop=(c == DC - 1))
                            return last
                        extra = ()
                        if first_hid[pr]:
                            extra = (("ev", routing_done),)
                            first_hid[pr] = False
                        mm_h = s.emit("tensor", hmm,
                                      waits=(wt1, ("ev", ln0_done[w]),
                                             ("sc", hid_ps_gate[pr]), *extra),
                                      inc="mm")
                        hv = g_sb[:, q * T + w * TW: q * T + (w + 1) * TW]
                        sc_gelu = s.emit(
                            "scalar",
                            lambda e, hv=hv, pi=pi, b1cols=b1cols, q=q:
                            e.activation(hv, pi[:], AF.Gelu_apprx_tanh,
                                         bias=b1cols[:, q:q + 1]),
                            waits=(("mm", mm_h), ("mm", hid_used), b1_ready),
                            inc="sc")
                        hid_ps_gate[pr] = sc_gelu
                if p != 0:
                    b1c_gate[sl] = sc_gelu
                mm_o_last = 0
                for fc in range(DC):
                    for w in range(NW):
                        pr = (fc * NW + w) % 2
                        po = ps[2 + pr]
                        def omm(e, fc=fc, w=w, po=po, w2v=w2v):
                            last = None
                            for q in range(4):
                                last = e.matmul(
                                    po[:],
                                    w2v[:, q * 1024 + fc * 128: q * 1024 + (fc + 1) * 128],
                                    g_sb[:, q * T + w * TW: q * T + (w + 1) * TW],
                                    start=(q == 0), stop=(q == 3))
                            return last
                        extra = ()
                        if first_out[pr]:
                            extra = (("ev", routing_done),)
                            first_out[pr] = False
                        mm_o = s.emit("tensor", omm,
                                      waits=(wt2, ("sc", sc_gelu),
                                             ("ev", o_ps_gate[pr]), *extra),
                                      inc="mm")
                        av = a_sb[:, fc * T + w * TW: fc * T + (w + 1) * TW]
                        if p == 0 and hc == 0:
                            ev_ffn = s.emit(
                                "vector",
                                lambda e, av=av, po=po: e.tensor_scalar(
                                    av, po[:], a_col, None, ALU.mult),
                                waits=(("mm", mm_o), ("ev", routing_done)),
                                inc="ev")
                        else:
                            ev_ffn = s.emit(
                                "vector",
                                lambda e, av=av, po=po, sc_col=sc_col: e.scalar_tensor_tensor(
                                    av, po[:], sc_col, av, ALU.mult, ALU.add),
                                waits=(("mm", mm_o),), inc="ev")
                        o_ps_gate[pr] = ev_ffn
                        mm_o_last = mm_o
                        if p == 2 and hc == NHC - 1:
                            acc_last[(fc, w)] = ev_ffn
                            if fc == DC - 1:
                                hid_w_last[w] = mm_o
                w1_freed[sl] = s.c("mm")
                w2_freed[sl] = s.c("mm")
                b1_freed[sl] = s.c("mm")
                hid_used = mm_o_last
                slab += 1
        ffn_done_ev = s.c("ev")

        mm_cvt = s.emit("tensor", lambda e: [
            e.transpose(ps[6][:, fc:fc + 1], brow_sb[:1, fc * 128:(fc + 1) * 128],
                        ones_row[:1, 0:1])
            for fc in range(DC)][-1],
            waits=(("ev", ev_cvec), ("sc", scv_p1_last), (pre, pre_t)), inc="mm")
        ev_cvc = s.emit("vector",
                        lambda e: e.tensor_copy(cvec_sb[:, 0:8], ps[6][:, 0:8]),
                        waits=(("mm", mm_cvt),), inc="ev")
        sil_done = 0
        for fc in range(DC):
            for w in range(NW):
                gv = g_sb[:, fc * T + w * TW: fc * T + (w + 1) * TW]
                av = a_sb[:, fc * T + w * TW: fc * T + (w + 1) * TW]
                sil_done = s.emit(
                    "scalar",
                    lambda e, gv=gv, av=av, fc=fc: e.activation(
                        gv, av, AF.Silu, bias=cvec_sb[:, fc:fc + 1]),
                    waits=(("ev", ev_cvc), ("ev", acc_last[(fc, w)]),
                           ("mm", hid_w_last[w] if fc < 4 else mm_p1g)),
                    inc="sc")

        for hh in range(2):
            s.emit("gpsimd", lambda e, hh=hh: e.dma_start(
                out=w_sb[:, hh * 4096:(hh + 1) * 4096].rearrange(
                    "p (c j) -> p c j", c=4),
                in_=wq_dr[hh * 512:(hh + 1) * 512, :].rearrange(
                    "(c p) j -> p c j", p=128)),
                waits=(("mm", w1_freed[hh]),), inc=dg, dma=True)
            s.emit("gpsimd", lambda e, hh=hh: e.dma_start(
                out=w_sb[:, 8192 + hh * 4096: 8192 + (hh + 1) * 4096].rearrange(
                    "p (c j) -> p c j", c=4),
                in_=wo_dr[hh * 512:(hh + 1) * 512, :].rearrange(
                    "(c p) j -> p c j", p=128)),
                waits=(("mm", w2_freed[hh]),), inc=dg, dma=True)
        s.emit("sync", lambda e: e.dma_start(
            out=a_sb[:].rearrange("p (c t) -> p c t", c=DC),
            in_=x_dr.rearrange("(c p) t -> p c t", p=128)),
            waits=(("sc", sil_done),), inc=dchx, dma=True)
        dch_t = s.c(dch)
        dchx_t = s.c(dchx)
        dg_t = s.c(dg)

        ev_mbf = s.emit("vector", lambda e: e.tensor_copy(m_sb[:], m32_sb[:]),
                        waits=((c2o, c2o_t),), inc="ev")

        x1_done = 0
        o_used = 0
        q_used = [0, 0]
        y_evac = [ev_kg_last, ev_kg_last]
        for w in range(NW):
            sc_o = 0
            for h in range(8):
                pq = ps[h % 2]
                def qmm(e, h=h, w=w, pq=pq):
                    last = None
                    for c in range(DC):
                        last = e.matmul(
                            pq[:],
                            w_sb[:, c * 1024 + h * 128: c * 1024 + (h + 1) * 128],
                            xn_sb[:, c * T + w * TW: c * T + (w + 1) * TW],
                            start=(c == 0), stop=(c == DC - 1))
                    return last
                mm_q = s.emit("tensor", qmm,
                              waits=((dch, dch_t), (dg, dg_t), ("sc", sil_done),
                                     ("mm", q_used[h % 2])),
                              inc="mm")
                qv = sq_q[:, (h % 2) * 512:(h % 2 + 1) * 512]
                ev_q = s.emit("scalar",
                              lambda e, qv=qv, pq=pq, h=h: e.activation(
                                  qv, pq[:], AF.Identity, bias=bqc_sb[:, h:h + 1]),
                              waits=(("mm", mm_q),), inc="sc")
                po = ps[2 + h % 2]
                mm_oh = s.emit("tensor",
                               lambda e, h=h, qv=qv, po=po: e.matmul(
                                   po[:], m_sb[:, h * 128:(h + 1) * 128], qv,
                                   start=True, stop=True),
                               waits=(("sc", ev_q), ("ev", ev_mbf),
                                      ("sc", sc_o), ("mm", o_used)), inc="mm")
                q_used[h % 2] = mm_oh
                ov = o_sb[:, h * 512:(h + 1) * 512]
                sc_o = s.emit("scalar",
                              lambda e, ov=ov, po=po: e.copy(ov, po[:]),
                              waits=(("mm", mm_oh),), inc="sc")
            for ic in range(DC):
                pr = ic % 2
                py = ps[4 + pr]
                def ymm(e, ic=ic, py=py):
                    last = None
                    for cc in range(DC):
                        last = e.matmul(
                            py[:],
                            w_sb[:, 8192 + cc * 1024 + ic * 128:
                                 8192 + cc * 1024 + (ic + 1) * 128],
                            o_sb[:, cc * 512:(cc + 1) * 512],
                            start=(cc == 0), stop=(cc == DC - 1))
                    return last
                mm_y = s.emit("tensor", ymm,
                              waits=((dch, dch_t), (dg, dg_t), ("sc", sc_o),
                                     ("ev", y_evac[pr])), inc="mm")
                o_used = mm_y
                tv = t_scr[:, pr * 512:(pr + 1) * 512]
                gv = g_sb[:, ic * T + w * TW: ic * T + (w + 1) * TW]
                av = a_sb[:, ic * T + w * TW: ic * T + (w + 1) * TW]
                y_evac[pr] = s.emit(
                    "vector",
                    lambda e, tv=tv, gv=gv, py=py, ic=ic: e.scalar_tensor_tensor(
                        tv, py[:], boc_sb[:, ic:ic + 1], gv, ALU.add, ALU.mult),
                    waits=(("mm", mm_y),), inc="ev")
                s.drain("vector")
                x1_done = s.emit("vector",
                                 lambda e, av=av, tv=tv: e.tensor_add(av, av, tv),
                                 waits=((dchx, dchx_t),), inc="ev")
        st_x1 = s.emit("sync", lambda e: e.dma_start(
            out=x1_dr.rearrange("(c p) t -> p c t", p=128),
            in_=a_sb[:].rearrange("p (c t) -> p c t", c=DC)),
            waits=(("ev", x1_done),), inc="st", dma=True)
        st_x1_t = s.c("st")

        aux1 = dict(
            eps_row=eps_ln[:1, :1],
            ps_stat0=ps[0][0:1, :], ps_stat1=ps[1][0:1, :],
            ps_bc0=ps[2][:], ps_bc1=ps[3][:],
            sq_scr=gf[:, 0:1024], diff_scr=gf[:, 1024:2048],
            rows=gf[0:1, 2048:5120],
            ones_col=ones_col[:], ones_row=ones_row,
            x_ready=("ev", x1_done),
            pre_waits=(("ev", x1_done),),
            xn_done=[0] * NW,
        )
        emit_layernorm(s, a_sb, xn_sb, aux1, colsum_to=None)
        xn1_done = aux1["xn_done"]

        mm_all_d = s.c("mm")
        fgk = s.new_sem("fgk")
        s.emit("gpsimd", lambda e: e.dma_start(
            out=w_sb[:, 0:8192].rearrange("p (c j) -> p c j", c=DC),
            in_=wk1_dr.rearrange("(c p) j -> p c j", p=128)),
            waits=(("mm", max(q_used)),), inc=fgk, dma=True)
        s.emit("gpsimd", lambda e: e.dma_start(
            out=w_sb[:, 8192:16384].rearrange("p (c j) -> p c j", c=DC),
            in_=wv1_dr.rearrange("(c p) j -> p c j", p=128)),
            waits=(("mm", mm_all_d),), inc=fg, dma=True)
        s.emit("sync",
               lambda e: e.dma_start(out=brow_sb[:1, 1024:2048], in_=bv1_dr),
               waits=(("mm", mm_all_d),), inc=fch, dma=True)
        fch_t = s.c(fch)
        fg_t = s.c(fg)
        fgk_t = s.c(fgk)
        bv1_row = brow_sb[:1, 1024:2048]
        v_all = o_sb[:, 0:2048]
        kf_all = o_sb[:, 2048:3072]
        kv_sb = kvf_sb
        ln1_all = ("ev", xn1_done[NW - 1])

        kv_done = 0
        kv_done_par = [0, 0]
        v_used = 0
        kf_used = 0
        pv_gate = [0, 0]
        pk_gate = [0, 0]
        pf_gate = [0, 0]
        first_kv = True
        sc_bvt = 0
        ev_v_prev = x1_done
        bvt_sb = t_scr[:, 0:128]
        for hp in range(8):
            mm_bv = s.emit("tensor", lambda e, hp=hp: e.matmul(
                ps[7][:, 0:128], ones_row[:1, 0:128],
                bv1_row[:1, hp * 128:(hp + 1) * 128], start=True, stop=True),
                waits=((fch, fch_t), ("sc", sc_bvt), ("ev", ev_m)), inc="mm")
            sc_bvt = s.emit("scalar", lambda e: e.copy(bvt_sb, ps[7][:, 0:128]),
                            waits=(("mm", mm_bv), ("ev", ev_v_prev)), inc="sc")
            ev_v = 0
            for tt in range(NT):
                pv = ps[2 + tt % 2]
                def vmm(e, hp=hp, tt=tt, pv=pv):
                    last = None
                    for c in range(DC):
                        last = e.matmul(
                            pv[:, 0:128],
                            xn_sb[:, c * T + tt * 128: c * T + (tt + 1) * 128],
                            w_sb[:, 8192 + c * 1024 + hp * 128:
                                 8192 + c * 1024 + (hp + 1) * 128],
                            start=(c == 0), stop=(c == DC - 1))
                    return last
                mm_v = s.emit("tensor", vmm,
                              waits=((fg, fg_t), ln1_all,
                                     ("mm", v_used), ("ev", pv_gate[tt % 2])),
                              inc="mm")
                ev_v = s.emit("vector", lambda e, tt=tt, pv=pv: e.tensor_add(
                    v_all[:, tt * 128:(tt + 1) * 128], pv[:, 0:128], bvt_sb),
                    waits=(("mm", mm_v), ("mm", v_used), ("sc", sc_bvt)),
                    inc="ev")
                pv_gate[tt % 2] = ev_v
                ev_v_prev = ev_v
            for half in range(2):
                h = 2 * hp + half
                ev_kf = 0
                for w in range(NW):
                    pk = ps[w % 2]
                    def kfmm(e, h=h, w=w, pk=pk):
                        last = None
                        for c in range(DC):
                            last = e.matmul(
                                pk[0:64, :],
                                w_sb[:, c * 1024 + h * 64: c * 1024 + (h + 1) * 64],
                                xn_sb[:, c * T + w * TW: c * T + (w + 1) * TW],
                                start=(c == 0), stop=(c == DC - 1))
                        return last
                    mm_kf = s.emit("tensor", kfmm,
                                   waits=((fgk, fgk_t), ln1_all,
                                          ("mm", kf_used), ("sc", pk_gate[w % 2])),
                                   inc="mm")
                    kfv = sq_q[0:64, (w % 2) * 512:(w % 2 + 1) * 512]
                    ev_kfm = s.emit("scalar",
                                    lambda e, kfv=kfv, pk=pk, h=h: e.activation(
                                        kfv, pk[0:64, :], AF.Identity,
                                        bias=bk1c_sb[:, h:h + 1]),
                                    waits=(("mm", mm_kf),), inc="sc")
                    pk_gate[w % 2] = ev_kfm
                    for it in range(4):
                        tt = w * 4 + it
                        pf = ps[4 + tt % 2]
                        mm_f = s.emit(
                            "tensor",
                            lambda e, it=it, kfv=kfv, pf=pf: e.matmul(
                                pf[:, 0:64], kfv[:, it * 128:(it + 1) * 128],
                                ptb_sb[0:64, :], start=True, stop=True),
                            waits=(("sc", ev_kfm),
                                   ("sc", pf_gate[tt % 2])), inc="mm")
                        ev_kf = s.emit(
                            "scalar", lambda e, tt=tt, pf=pf: e.activation(
                                kf_all[:, tt * 64:(tt + 1) * 64], pf[:, 0:64],
                                AF.Relu, scale=L1_NORM, bias=eps_col[:, 0:1]),
                            waits=(("mm", mm_f), ("mm", kf_used)), inc="sc")
                        pf_gate[tt % 2] = ev_kf
                cb = (h % 2) * 128
                def kvmm(e, half=half, cb=cb):
                    base = half * 64
                    for tt in range(NT):
                        e.matmul(ps[6][0:64, cb:cb + 64],
                                 kf_all[:, tt * 64:(tt + 1) * 64],
                                 v_all[:, tt * 128 + base: tt * 128 + base + 64],
                                 start=(tt == 0), stop=(tt == NT - 1))
                    for tt in range(NT):
                        last = e.matmul(ps[6][0:64, cb + 64:cb + 65],
                                        kf_all[:, tt * 64:(tt + 1) * 64],
                                        onesb_col[:],
                                        start=(tt == 0), stop=(tt == NT - 1))
                    return last
                extra = ()
                if first_kv:
                    extra = (("ev", ev_cvc),)
                    first_kv = False
                else:
                    extra = (("ev", kv_done_par[h % 2]),)
                mm_kv = s.emit("tensor", kvmm,
                               waits=(("sc", ev_kf), ("ev", ev_v), *extra),
                               inc="mm")
                kf_used = mm_kv
                if half == 1:
                    v_used = mm_kv
                kv_done = s.emit("vector", lambda e, h=h, cb=cb: e.tensor_copy(
                    kv_sb[0:64, h * 65:(h + 1) * 65], ps[6][0:64, cb:cb + 65]),
                    waits=(("mm", mm_kv),), inc="ev")
                kv_done_par[h % 2] = kv_done

        s.emit("sync", lambda e: e.dma_start(out=cc_kv_i[:, :], in_=kv_sb[0:64, :]),
               waits=(("ev", kv_done),), inc=c3i, dma=True)
        c3i_t = s.c(c3i)
        s.emit("gpsimd", lambda e: e.collective_compute(
            "AllReduce", ALU.add, replica_groups=RG,
            ins=[cc_kv_i.ap().opt()], outs=[cc_kv_o.ap().opt()]),
            waits=((c3i, 16),), inc=c3c, dma=True, inc_n=1)
        s.emit("sync", lambda e: e.dma_start(out=kv_sb[0:64, :], in_=cc_kv_o[:, :]),
               waits=((c3c, 1),), inc=c3o, dma=True)
        c3o_t = s.c(c3o)

        B3mm = s.c("mm")
        B3ev = s.c("ev")
        B3sc = s.c("sc")
        s.emit("tensor", lambda e: e.nop(), waits=(("ev", B3ev), ("sc", B3sc)))
        s.emit("vector", lambda e: e.nop(), waits=(("mm", B3mm), ("sc", B3sc)))
        s.emit("scalar", lambda e: e.nop(), waits=(("mm", B3mm), ("ev", B3ev)))

        lg_read = 0
        ev_ct = 0
        for w in range(NW):
            mm_lg3 = s.emit("tensor", lambda e, w=w: [
                e.matmul(ps[0][0:16, :], rw_sb[:, c * 16:(c + 1) * 16],
                         xn_sb[:, c * T + w * TW: c * T + (w + 1) * TW],
                         start=(c == 0), stop=(c == DC - 1))
                for c in range(DC)][-1],
                waits=((preg, preg_t), ("ev", xn1_done[w]), ("ev", lg_read)),
                inc="mm")
            ev_lg = s.emit("vector", lambda e: e.tensor_copy(
                t_scr[0:16, 0:512], ps[0][0:16, :]),
                waits=(("mm", mm_lg3),), inc="ev")
            lg_read = ev_lg
            for it in range(4):
                tt = w * 4 + it
                mm_tr = s.emit("tensor", lambda e, it=it: e.transpose(
                    ps[1][:, 0:16], t_scr[0:16, it * 128:(it + 1) * 128],
                    id_sb[0:16, 0:16]),
                    waits=(("ev", ev_lg), ("ev", ev_ct)), inc="mm")

                def sfm(e):
                    e.reduce_max(rcol_sb[:, 0:1], ps[1][:, 0:16], axis=AX.X)
                    e.drain()
                    return e.tensor_scalar(rcol_sb[:, 1:2], rcol_sb[:, 0:1],
                                           -1.0, None, ALU.mult)
                ev_s1 = s.emit("vector", sfm, waits=(("mm", mm_tr),), inc="ev")
                sc_e = s.emit("scalar", lambda e: e.activation(
                    pr_sb[:, 0:16], ps[1][:, 0:16], AF.Exp,
                    bias=rcol_sb[:, 1:2]),
                    waits=(("ev", ev_s1),), inc="sc")

                def top2(e, tt=tt):
                    ctv = ct_sb[:, tt * 16:(tt + 1) * 16]
                    e.reduce_sum(rcol_sb[:, 0:1], pr_sb[:, 0:16], axis=AX.X)
                    e.drain()
                    e.reciprocal(rcol_sb[:, 1:2], rcol_sb[:, 0:1])
                    e.drain()
                    e.tensor_scalar(pr_sb[:, 0:16], pr_sb[:, 0:16],
                                    rcol_sb[:, 1:2], None, ALU.mult)
                    e.drain()
                    e.reduce_max(rcol_sb[:, 0:1], pr_sb[:, 0:16], axis=AX.X)
                    e.drain()
                    e.tensor_scalar(ctv, pr_sb[:, 0:16],
                                    rcol_sb[:, 0:1], None, ALU.is_ge)
                    e.drain()
                    e.scalar_tensor_tensor(ctv, ctv, -2.0, pr_sb[:, 0:16],
                                           ALU.mult, ALU.add)
                    e.drain()
                    e.reduce_max(rcol_sb[:, 2:3], ctv, axis=AX.X)
                    e.drain()
                    e.tensor_scalar(ctv, pr_sb[:, 0:16],
                                    rcol_sb[:, 2:3], None, ALU.is_ge)
                    e.drain()
                    return e.tensor_mul(ctv, ctv, pr_sb[:, 0:16])
                ev_ct = s.emit("vector", top2, waits=(("sc", sc_e),), inc="ev")
        coeff_done = ev_ct

        kv_ready = s.emit("vector", lambda e: e.tensor_copy(
            kvb_sb[:], kv_sb[0:64, :]), waits=((c3o, c3o_t),), inc="ev")

        wd_slab = g_sb[:, 8192:12288]
        gu_freed = [B3mm, B3mm]
        dn_freed = B3mm
        slchb = [s.new_sem("slA3"), s.new_sem("slB3")]
        dnch = s.new_sem("dnch")
        hid_ps_gate = [0, 0]
        up_ps_gate = [0, 0]
        o_ps_gate = [0, 0]
        cb_gate = 0
        sil_free = [0, 0]
        crow_gate = 0
        inter_used = 0
        ev_ffn = 0
        sc_sil = 0
        first_acc = True
        acc3_last = {}
        hid3_w_last = {}
        for pe in range(-2, 16):
            soff = (pe + 2) * 512
            b = (pe + 2) % 2
            wg_slab = w_sb[:, b * 8192: b * 8192 + 4096]
            wu_slab = w_sb[:, b * 8192 + 4096: (b + 1) * 8192]
            if pe < 0:
                s.emit("gpsimd", lambda e, soff=soff, wg_slab=wg_slab: e.dma_start(
                    out=wg_slab.rearrange("p (c h) -> p c h", c=DC),
                    in_=sg_dr.rearrange("(c p) h -> p c h", p=128)
                    [:, :, soff:soff + 512]),
                    waits=(("mm", gu_freed[b]),), inc=slchb[b], dma=True)
                s.emit("gpsimd", lambda e, soff=soff, wu_slab=wu_slab: e.dma_start(
                    out=wu_slab.rearrange("p (c h) -> p c h", c=DC),
                    in_=su_dr.rearrange("(c p) h -> p c h", p=128)
                    [:, :, soff:soff + 512]),
                    waits=(("mm", gu_freed[b]),), inc=slchb[b], dma=True)
                s.emit("gpsimd", lambda e, soff=soff: e.dma_start(
                    out=wd_slab.rearrange("p (q f) -> p q f", q=4),
                    in_=sd_dr.rearrange("(q p) f -> p q f", p=128)
                    [:, soff // 128: soff // 128 + 4, :]),
                    waits=(("mm", dn_freed),), inc=dnch, dma=True)
            else:
                s.emit("gpsimd", lambda e, pe=pe, wg_slab=wg_slab: e.dma_start(
                    out=wg_slab.rearrange("p (c h) -> p c h", c=DC),
                    in_=gw_dr[pe * D:(pe + 1) * D, :].rearrange(
                        "(c p) h -> p c h", p=128)),
                    waits=(("mm", gu_freed[b]),), inc=slchb[b], dma=True)
                s.emit("gpsimd", lambda e, pe=pe, wu_slab=wu_slab: e.dma_start(
                    out=wu_slab.rearrange("p (c h) -> p c h", c=DC),
                    in_=uw_dr[pe * D:(pe + 1) * D, :].rearrange(
                        "(c p) h -> p c h", p=128)),
                    waits=(("mm", gu_freed[b]),), inc=slchb[b], dma=True)
                s.emit("gpsimd", lambda e, pe=pe: e.dma_start(
                    out=wd_slab.rearrange("p (q f) -> p q f", q=4),
                    in_=dw_dr[pe * F_EXP:(pe + 1) * F_EXP, :].rearrange(
                        "(q p) f -> p q f", p=128)),
                    waits=(("mm", dn_freed),), inc=dnch, dma=True)
            slch_t = s.c(slchb[b])
            wtd = (dnch, s.c(dnch))
            for w in range(NW):
                if pe >= 0:
                    mm_ctr = s.emit("tensor", lambda e, pe=pe, w=w: [
                        e.transpose(ps[4][0:1, it * 128:(it + 1) * 128],
                                    ct_sb[:, (w * 4 + it) * 16 + pe:
                                          (w * 4 + it) * 16 + pe + 1],
                                    id_sb[:])
                        for it in range(4)][-1],
                        waits=(("ev", coeff_done), ("ev", cb_gate),
                               (pre, pre_t)), inc="mm")
                    ev_cr = s.emit("vector", lambda e: e.tensor_copy(
                        crow_sb[:1, 0:512], ps[4][0:1, :]),
                        waits=(("mm", mm_ctr), ("mm", crow_gate),
                               (dch, dch_t)), inc="ev")
                    mm_cb = s.emit("tensor", lambda e: e.matmul(
                        ps[5][:], ones_row[:1, 0:128], crow_sb[:1, 0:512],
                        start=True, stop=True),
                        waits=(("ev", ev_cr), ("ev", cb_gate)), inc="mm")
                    crow_gate = mm_cb
                for q in range(4):
                    pr = (q * NW + w) % 2
                    pg = ps[pr]
                    pu = ps[2 + pr]
                    def gmm(e, q=q, w=w, pg=pg, wg_slab=wg_slab):
                        last = None
                        for c in range(DC):
                            last = e.matmul(
                                pg[:],
                                wg_slab[:, c * 512 + q * 128: c * 512 + (q + 1) * 128],
                                xn_sb[:, c * T + w * TW: c * T + (w + 1) * TW],
                                start=(c == 0), stop=(c == DC - 1))
                        return last
                    mm_g = s.emit("tensor", gmm,
                                  waits=((slchb[b], slch_t),
                                         ("ev", coeff_done),
                                         ("sc", hid_ps_gate[pr])), inc="mm")
                    def umm(e, q=q, w=w, pu=pu, wu_slab=wu_slab):
                        last = None
                        for c in range(DC):
                            last = e.matmul(
                                pu[:],
                                wu_slab[:, c * 512 + q * 128: c * 512 + (q + 1) * 128],
                                xn_sb[:, c * T + w * TW: c * T + (w + 1) * TW],
                                start=(c == 0), stop=(c == DC - 1))
                        return last
                    mm_u = s.emit("tensor", umm,
                                  waits=((slchb[b], slch_t), ("ev", coeff_done),
                                         ("ev", up_ps_gate[pr])), inc="mm")
                    sv = sq_q[:, pr * 512:(pr + 1) * 512]
                    sc_sil = s.emit("scalar",
                                    lambda e, sv=sv, pg=pg: e.activation(
                                        sv, pg[:], AF.Silu),
                                    waits=(("mm", mm_g), ("mm", inter_used),
                                           ("ev", sil_free[pr])),
                                    inc="sc")
                    hid_ps_gate[pr] = sc_sil
                    hv = g_sb[:, q * T + w * TW: q * T + (w + 1) * TW]
                    if pe < 0:
                        sil_free[pr] = up_ps_gate[pr] = s.emit(
                            "vector",
                            lambda e, hv=hv, sv=sv, pu=pu: e.tensor_mul(
                                hv, sv, pu[:]),
                            waits=(("mm", mm_u), ("sc", sc_sil),
                                   ("mm", inter_used)), inc="ev")
                    else:
                        tv = t_scr[:, pr * 512:(pr + 1) * 512]
                        sil_free[pr] = s.emit(
                            "vector",
                            lambda e, tv=tv, sv=sv, pu=pu: e.tensor_mul(
                                tv, sv, pu[:]),
                            waits=(("mm", mm_u), ("sc", sc_sil),
                                   ("mm", inter_used)), inc="ev")
                        s.drain("vector")
                        up_ps_gate[pr] = s.emit(
                            "vector",
                            lambda e, hv=hv, tv=tv: e.tensor_mul(
                                hv, tv, ps[5][:]),
                            waits=(("mm", crow_gate),), inc="ev")
                if pe >= 0:
                    cb_gate = up_ps_gate[(3 * NW + w) % 2]
            gu_freed[b] = s.c("mm")
            mm_o_last = 0
            for fc in range(DC):
                for w in range(NW):
                    pr = (fc * NW + w) % 2
                    po = ps[6 + pr]
                    def dmm(e, fc=fc, w=w, po=po):
                        last = None
                        for q in range(4):
                            last = e.matmul(
                                po[:],
                                wd_slab[:, q * 1024 + fc * 128:
                                        q * 1024 + (fc + 1) * 128],
                                g_sb[:, q * T + w * TW: q * T + (w + 1) * TW],
                                start=(q == 0), stop=(q == 3))
                        return last
                    mm_o = s.emit("tensor", dmm,
                                  waits=(wtd,
                                         ("ev", up_ps_gate[0]),
                                         ("ev", up_ps_gate[1]),
                                         ("ev", o_ps_gate[pr])), inc="mm")
                    av = a_sb[:, fc * T + w * TW: fc * T + (w + 1) * TW]
                    if pe == -2:
                        extra = (("st", st_x1_t),) if first_acc else ()
                        first_acc = False
                        ev_ffn = s.emit(
                            "vector",
                            lambda e, av=av, po=po: e.tensor_copy(av, po[:]),
                            waits=(("mm", mm_o), *extra), inc="ev")
                    else:
                        ev_ffn = s.emit(
                            "vector",
                            lambda e, av=av, po=po: e.tensor_add(av, av, po[:]),
                            waits=(("mm", mm_o),), inc="ev")
                    o_ps_gate[pr] = ev_ffn
                    mm_o_last = mm_o
                    if pe == 15:
                        acc3_last[(fc, w)] = ev_ffn
                        if fc == DC - 1:
                            hid3_w_last[w] = mm_o
            dn_freed = s.c("mm")
            inter_used = mm_o_last
        ffn_done = s.c("ev")

        sil_done3 = 0
        for fc in range(DC):
            for w in range(NW):
                gv = g_sb[:, fc * T + w * TW: fc * T + (w + 1) * TW]
                av = a_sb[:, fc * T + w * TW: fc * T + (w + 1) * TW]
                sil_done3 = s.emit("scalar",
                                   lambda e, gv=gv, av=av: e.activation(
                                       gv, av, AF.Silu),
                                   waits=(("ev", acc3_last[(fc, w)]),
                                          ("mm", hid3_w_last[w] if fc < 4
                                           else inter_used),
                                          ("ev", kv_ready), (c3i, c3i_t)),
                                   inc="sc")

        s.emit("gpsimd", lambda e: e.dma_start(
            out=w_sb[:, 0:8192].rearrange("p (hp j) -> p hp j", hp=8),
            in_=woP_dr.rearrange("p (hp j) -> p hp j", hp=8)),
            waits=(("mm", gu_freed[0]),), inc=d3g, dma=True)
        d3ch_t = s.c(d3ch)
        s.emit("sync", lambda e: e.dma_start(
            out=a_sb[:].rearrange("p (c t) -> p c t", c=DC),
            in_=x1_dr.rearrange("(c p) t -> p c t", p=128)),
            waits=(("sc", sil_done3), ("st", st_x1_t)), inc=d3chx, dma=True)
        d3chx_t = s.c(d3chx)
        wo_v = w_sb[:, 0:8192]

        out_done = 0
        wq_used = 0
        pq_gate = [0, 0]
        qv_used = [0, 0]
        pf_gate = [0, 0]
        ybank_gate = [0, 0]
        po_gate = [0, 0]
        tv_free = [0, 0]
        oa_done = 0
        pd_gate = 0
        pbc_gate = 0
        py_gate = [0, 0]
        crow_d_gate = [0, 0]
        first_qv = [True, True]
        sil_last = [sil_free[0], sil_free[1]]
        d3gb = [s.new_sem("d3gA"), s.new_sem("d3gB")]
        wqh_used = [gu_freed[1], gu_freed[1]]
        for w in range(NW):
            for H in range(2):
                hb = (w * 2 + H) % 2
                wqh_v = w_sb[:, 8192 + hb * 4096: 8192 + (hb + 1) * 4096]
                s.emit("gpsimd", lambda e, H=H, wqh_v=wqh_v: e.dma_start(
                    out=wqh_v.rearrange("p (c j) -> p c j", c=DC),
                    in_=wq1_dr.rearrange("(c p) j -> p c j", p=128)
                    [:, :, H * 512:(H + 1) * 512]),
                    waits=(("mm", wqh_used[hb]),),
                    inc=d3gb[hb], dma=True)
                d3g_t = s.c(d3gb[hb])
                for hp_l in range(4):
                    hp = H * 4 + hp_l
                    pr2 = hp % 2
                    po = ps[4 + pr2]
                    pq = ps[pr2]
                    qkv_mm = [None, None]
                    def qmm(e, hp_l=hp_l, w=w, pq=pq, wqh_v=wqh_v):
                        last = None
                        for c in range(DC):
                            last = e.matmul(
                                pq[:],
                                wqh_v[:, c * 512 + hp_l * 128:
                                      c * 512 + (hp_l + 1) * 128],
                                xn_sb[:, c * T + w * TW: c * T + (w + 1) * TW],
                                start=(c == 0), stop=(c == DC - 1))
                        return last
                    mm_q = s.emit("tensor", qmm,
                                  waits=((d3ch, d3ch_t), (d3gb[hb], d3g_t),
                                         ("sc", sil_done3),
                                         ("sc", pq_gate[pr2])), inc="mm")
                    wq_used = mm_q
                    sc_qv = [0, 0]
                    for half in range(2):
                        h = 2 * hp + half
                        qv = sq_q[0:64, half * 512:(half + 1) * 512]
                        extra = ()
                        if first_qv[half]:
                            extra = (("ev", max(sil_last)),)
                            first_qv[half] = False
                        sc_qv[half] = s.emit(
                            "scalar",
                            lambda e, qv=qv, pq=pq, h=h, half=half: e.activation(
                                qv, pq[half * 64:(half + 1) * 64, :],
                                AF.Identity, bias=bq1c_sb[:, h:h + 1]),
                            waits=(("mm", mm_q), ("mm", qv_used[half]), *extra),
                            inc="sc")
                    pq_gate[pr2] = sc_qv[1]
                    for half in range(2):
                        h = 2 * hp + half
                        qv = sq_q[0:64, half * 512:(half + 1) * 512]
                        pf = ps[2 + half]
                        mm_f = s.emit("tensor",
                                      lambda e, qv=qv, pf=pf: e.matmul(
                                          pf[0:64, :], ptb_sb[:], qv,
                                          start=True, stop=True),
                                      waits=(("sc", sc_qv[half]),
                                             ("sc", pf_gate[half]),
                                             ("ev", ybank_gate[half])), inc="mm")
                        pf_gate[half] = s.emit(
                            "scalar", lambda e, qv=qv, pf=pf: e.activation(
                                qv, pf[0:64, :], AF.Relu, scale=L1_NORM,
                                bias=eps_col[0:64, :]),
                            waits=(("mm", mm_f),), inc="sc")
                        mm_qkv = s.emit(
                            "tensor",
                            lambda e, h=h, half=half, qv=qv, po=po: e.matmul(
                                po[half * 64:(half + 1) * 64, :],
                                kvb_sb[:, h * 65: h * 65 + 64], qv,
                                start=True, stop=True),
                            waits=(("sc", pf_gate[half]), ("ev", kv_ready),
                                   ("sc", po_gate[pr2])), inc="mm")
                        qkv_mm[half] = mm_qkv
                        mm_dn = s.emit(
                            "tensor",
                            lambda e, h=h, qv=qv: e.matmul(
                                ps[7][0:1, :],
                                kvb_sb[:, h * 65 + 64: h * 65 + 65], qv,
                                start=True, stop=True),
                            waits=(("sc", pd_gate), ("ev", ev_m)), inc="mm")
                        qv_used[half] = mm_dn
                        pd_gate = s.emit(
                            "scalar",
                            lambda e, half=half: e.activation(
                                crow_sb[:1, half * 512:(half + 1) * 512],
                                ps[7][0:1, :], AF.Identity,
                                bias=eps_r6[:1, :1]),
                            waits=(("mm", mm_dn), ("mm", crow_d_gate[half])),
                            inc="sc")

                    ev_rcp = s.emit(
                        "vector",
                        lambda e: e.reciprocal(crow_sb[:1, 0:1024],
                                               crow_sb[:1, 0:1024]),
                        waits=(("sc", pd_gate),), inc="ev")
                    tvp = t_scr[:, pr2 * 512:(pr2 + 1) * 512]
                    sc_oc = s.emit(
                        "scalar",
                        lambda e, tvp=tvp, po=po: e.copy(tvp, po[:]),
                        waits=(("mm", qkv_mm[1]), ("ev", tv_free[pr2])),
                        inc="sc")
                    po_gate[pr2] = sc_oc
                    mm_bc = s.emit(
                        "tensor",
                        lambda e: [
                            e.matmul(ps[6][0:64, :], ones_row[:1, 0:64],
                                     crow_sb[:1, 0:512], start=True, stop=True),
                            e.matmul(ps[6][64:128, :], ones_row[:1, 0:64],
                                     crow_sb[:1, 512:1024], start=True, stop=True),
                        ][-1],
                        waits=(("ev", ev_rcp), ("ev", pbc_gate)), inc="mm")
                    crow_d_gate = [mm_bc, mm_bc]
                    ovp = o_sb[:, hp * 512:(hp + 1) * 512]
                    pbc_gate = oa_done = s.emit(
                        "vector",
                        lambda e, ovp=ovp, tvp=tvp: e.tensor_mul(
                            ovp, tvp, ps[6][:]),
                        waits=(("mm", mm_bc), ("sc", sc_oc)), inc="ev")
                    tv_free[hp % 2] = oa_done
                wqh_used[hb] = wq_used
            o_done = oa_done
            for ic in range(DC):
                pr = ic % 2
                py = ps[2 + pr]
                def ymm(e, ic=ic, py=py):
                    last = None
                    for hp in range(8):
                        last = e.matmul(
                            py[:],
                            wo_v[:, hp * 1024 + ic * 128:
                                 hp * 1024 + (ic + 1) * 128],
                            o_sb[:, hp * 512:(hp + 1) * 512],
                            start=(hp == 0), stop=(hp == 7))
                    return last
                mm_y = s.emit("tensor", ymm,
                              waits=((d3ch, d3ch_t), (d3g, 16), ("ev", o_done),
                                     ("ev", py_gate[pr]),
                                     ("sc", pf_gate[0]), ("sc", pf_gate[1])),
                              inc="mm")
                tv = t_scr[:, pr * 512:(pr + 1) * 512]
                gv = g_sb[:, ic * T + w * TW: ic * T + (w + 1) * TW]
                av = a_sb[:, ic * T + w * TW: ic * T + (w + 1) * TW]
                py_gate[pr] = s.emit(
                    "vector",
                    lambda e, tv=tv, gv=gv, py=py, ic=ic: e.scalar_tensor_tensor(
                        tv, py[:], bo1c_sb[:, ic:ic + 1], gv, ALU.add, ALU.mult),
                    waits=(("mm", mm_y),), inc="ev")
                s.drain("vector")
                out_done = s.emit("vector",
                                  lambda e, av=av, tv=tv: e.tensor_add(av, av, tv),
                                  waits=((d3chx, d3chx_t),), inc="ev")
                tv_free[pr] = out_done
            ybank_gate = [py_gate[0], py_gate[1]]
        s.emit("sync", lambda e: e.dma_start(
            out=out_dr.rearrange("(c p) t -> p c t", p=128),
            in_=a_sb[:].rearrange("p (c t) -> p c t", c=DC)),
            waits=(("ev", out_done),), inc="st", dma=True)
        st_all = s.c("st")
        s.emit("sync", lambda e: e.nop(), waits=(("st", st_all),))

        s.finish()
    return nc



def _prep_inputs(inp):
    d = {}
    x = np.asarray(inp["L0_attn_x"])[0]
    x_fm = np.ascontiguousarray(x.T)
    d["x_slices"] = [np.ascontiguousarray(x_fm[:, c * T:(c + 1) * T])
                     for c in range(NCORES)]

    w_qkv = np.asarray(inp["L0_attn_w_qkv"])
    b_qkv = np.asarray(inp["L0_attn_b_qkv"])
    rows = (np.arange(128)[None, :] * 8 + np.arange(8)[:, None]).reshape(-1)
    for name, lo in (("q", 0), ("k", 1), ("v", 2)):
        w = w_qkv[lo * D:(lo + 1) * D][rows, :]
        d[f"w{name}T"] = np.ascontiguousarray(w.T)
        d[f"b{name}"] = np.ascontiguousarray(
            b_qkv[lo * D:(lo + 1) * D][rows])[None, :]

    c_ = np.ascontiguousarray
    d["w_oT"] = c_(np.asarray(inp["L0_attn_w_o"]).T)
    d["b_o"] = c_(np.asarray(inp["L0_attn_b_o"]))[None, :]
    d["alpha"] = np.asarray(inp["L0_ffn_alpha"], np.float32).reshape(1, 1)
    d["wgate"] = c_(np.asarray(inp["L0_ffn_w_gate"]))
    d["bgate"] = c_(np.asarray(inp["L0_ffn_b_gate"]))[None, :]
    d["W1m"] = c_(np.asarray(inp["L0_ffn_W1m"]))
    d["b1m"] = c_(np.asarray(inp["L0_ffn_b1m"]))[None, :]
    d["W2m"] = c_(np.asarray(inp["L0_ffn_W2m"]))
    d["b2m"] = c_(np.asarray(inp["L0_ffn_b2m"]))[None, :]
    d["W1e"] = c_(np.asarray(inp["L0_ffn_W1e"]).reshape(8 * D, 4096))
    d["b1e"] = c_(np.asarray(inp["L0_ffn_b1e"]))
    d["W2e"] = c_(np.asarray(inp["L0_ffn_W2e"]).reshape(8 * 4096, D))
    d["b2e"] = c_(np.asarray(inp["L0_ffn_b2e"]))
    for nm in ("q", "k", "v", "o"):
        d[f"w{nm}1T"] = c_(np.asarray(inp[f"L1_attn_W{nm}"]).T)
        d[f"b{nm}1"] = c_(np.asarray(inp[f"L1_attn_b{nm}"]))[None, :]
    d["wo1P"] = c_(d["wo1T"].reshape(8, 2, 64, D).transpose(1, 2, 0, 3).reshape(128, 8 * D))
    d["b1m_cols"] = c_(d["b1m"][0].reshape(32, 128).T)
    d["bq_cols"] = c_(d["bq"][0].reshape(8, 128).T)
    d["bo_cols"] = c_(d["b_o"][0].reshape(8, 128).T)
    d["bo1_cols"] = c_(d["bo1"][0].reshape(8, 128).T)
    d["bk1_cols"] = c_(d["bk1"][0].reshape(16, 64).T)
    d["bq1_cols"] = c_(d["bq1"][0].reshape(16, 64).T)
    d["PT"] = c_(np.asarray(inp["L1_attn_P"]).T)
    d["ident"] = np.eye(128, dtype=np.float32)
    d["router_w"] = c_(np.asarray(inp["L1_ffn_router_w"]))
    d["gate_w"] = c_(np.asarray(inp["L1_ffn_gate_w"]).reshape(16 * D, 512))
    d["up_w"] = c_(np.asarray(inp["L1_ffn_up_w"]).reshape(16 * D, 512))
    d["down_w"] = c_(np.asarray(inp["L1_ffn_down_w"]).reshape(16 * 512, D))
    d["sgate_w"] = c_(np.asarray(inp["L1_ffn_shared_gate_w"]))
    d["sup_w"] = c_(np.asarray(inp["L1_ffn_shared_up_w"]))
    d["sdown_w"] = c_(np.asarray(inp["L1_ffn_shared_down_w"]))
    return d


def kernel(**inputs):
    prep = _prep_inputs(inputs)
    cores = list(range(NCORES))
    nc = build_fused()
    shared = {
        "wkT": prep["wkT"], "wvT": prep["wvT"],
        "bk": prep["bk"], "bv": prep["bv"],
        "alpha": prep["alpha"], "wgate": prep["wgate"], "bgate": prep["bgate"],
        "W1m": prep["W1m"], "W2m": prep["W2m"], "b2m": prep["b2m"],
        "W1e": prep["W1e"], "b1e": prep["b1e"],
        "W2e": prep["W2e"], "b2e": prep["b2e"],
        "wqT": prep["wqT"], "woT": prep["w_oT"], "bo_cols": prep["bo_cols"],
        "wk1T": prep["wk1T"], "wv1T": prep["wv1T"], "bv1": prep["bv1"],
        "PT": prep["PT"], "ident": prep["ident"],
        "b1m_cols": prep["b1m_cols"], "bq_cols": prep["bq_cols"],
        "bk1_cols": prep["bk1_cols"], "bq1_cols": prep["bq1_cols"],
        "wq1T": prep["wq1T"], "wo1P": prep["wo1P"], "bo1_cols": prep["bo1_cols"],
        "router_w": prep["router_w"],
        "gate_w": prep["gate_w"], "up_w": prep["up_w"], "down_w": prep["down_w"],
        "sgate_w": prep["sgate_w"], "sup_w": prep["sup_w"],
        "sdown_w": prep["sdown_w"],
    }
    in_maps = [{**shared, "x_fm": prep["x_slices"][c]} for c in range(NCORES)]
    res = run_bass_kernel_spmd(nc, in_maps, core_ids=cores).results
    out = np.empty((1, NCORES * T, D), np.float32)
    for c in range(NCORES):
        out[0, c * T:(c + 1) * T, :] = res[c]["out_fm"].T
    return out


# version 09
# speedup vs baseline: 1.0061x; 1.0061x over previous
import numpy as np
import ml_dtypes
from contextlib import ExitStack

import concourse.bass as bass
import concourse.mybir as mybir
from concourse.bass import IndirectOffsetOnAxis
from concourse.bass_utils import run_bass_kernel_spmd

F32 = mybir.dt.float32
BF16 = mybir.dt.bfloat16
I32 = mybir.dt.int32
AF = mybir.ActivationFunctionType
ALU = mybir.AluOpType
AX = mybir.AxisListType

NCORES = 8
RG = [[0, 1, 2, 3, 4, 5, 6, 7]]
T = 2048
D = 1024
DC = 8
TW = 512
NW = T // TW
NT = T // 128
LN_EPS = 1e-5
FFN_H = 4096
HC = 512
NHC = FFN_H // HC
L1_NORM = 1.0 / (64.0 ** 0.25)
F_EXP = 512
F_SH = 1024



class Sched:

    ENGINES = ("sync", "gpsimd", "tensor", "vector", "scalar")
    SEMS = ("ld", "wt", "mm", "ev", "sc", "gp", "st")

    DEBUG = False

    def __init__(self, nc, stack):
        self.nc = nc
        self.stack = stack
        self.q = {e: [] for e in self.ENGINES}
        self.sems = {n: stack.enter_context(nc.semaphore(n)) for n in self.SEMS}
        self.cnt = {n: 0 for n in self.SEMS}
        self.floor = {e: {n: 0 for n in self.SEMS} for e in self.ENGINES}
        self.log = []

    def new_sem(self, name):
        self.sems[name] = self.stack.enter_context(self.nc.semaphore(name))
        self.cnt[name] = 0
        for e in self.ENGINES:
            self.floor[e][name] = 0
        return name

    def emit(self, eng, fn, waits=(), inc=None, dma=False, inc_n=None):
        n = inc_n if inc_n is not None else (16 if dma else 1)
        fl = self.floor[eng]
        real_waits = []
        for s_, v in waits:
            if v > fl[s_]:
                real_waits.append((s_, v))
                fl[s_] = v
        target = None
        if inc is not None:
            self.cnt[inc] += n
            target = self.cnt[inc]
            if not dma and inc_n is None:
                fl[inc] = max(fl[inc], target)
        if Sched.DEBUG:
            import sys as _s
            self.log.append((eng, _s._getframe(1).f_lineno,
                             tuple(real_waits), inc, target))
        sems = self.sems

        def cl(e, fn=fn, rw=tuple(real_waits), inc=inc, n=n):
            for s_, v in rw:
                e.wait_ge(sems[s_], v)
            ins = fn(e)
            if inc is not None:
                ins.then_inc(sems[inc], n)
        self.q[eng].append(cl)
        return target

    def drain(self, eng):
        self.q[eng].append(lambda e: e.drain())

    def c(self, name):
        return self.cnt[name]

    def finish(self):
        with self.nc.Block() as block:
            for engname in self.ENGINES:
                ops = self.q[engname]
                if not ops:
                    continue
                getattr(block, engname)(
                    lambda e, ops=ops: [f(e) for f in ops] and None
                )


def _alloc(stack, nc, name, shape, dtype):
    return stack.enter_context(nc.sbuf_tensor(name, shape, dtype))


def _psum(stack, nc, name, shape, dtype=F32):
    return stack.enter_context(nc.psum_tensor(name, shape, dtype))



def emit_layernorm(s, x_sb, xn_sb, aux, colsum_to=None):
    ps_sum = aux["ps_stat0"]
    ps_sq = aux["ps_stat1"]
    ps_m = aux["ps_bc0"]
    ps_r = aux["ps_bc1"]
    sq = aux["sq_scr"]
    rows = aux["rows"]
    ones_col = aux["ones_col"]
    ones_row = aux["ones_row"]
    x_ready = aux["x_ready"]
    pre = tuple(aux.get("pre_waits", ()))

    sq_gate = [0, 0]
    st = dict(stats_read=0,
              bc_read=0,
              sc_sd=0)
    ev_r = [0, 0]
    bc_done = [0, 0]

    def stage_a(w):
        p = w % 2
        last = None
        for c in range(DC):
            xin = x_sb[:, c * T + w * TW: c * T + (w + 1) * TW]
            sqv = sq[:, (c % 2) * TW: (c % 2 + 1) * TW]
            sc_done = s.emit(
                "scalar",
                lambda e, xin=xin, sqv=sqv: e.square(sqv, xin),
                waits=(*pre, x_ready, ("mm", sq_gate[c % 2])),
                inc="sc")
            s.emit(
                "tensor",
                lambda e, xin=xin, c=c: e.matmul(
                    ps_sum, ones_col, xin, start=(c == 0), stop=(c == DC - 1)),
                waits=(*pre, x_ready, ("ev", st["stats_read"])))
            mm_i = s.emit(
                "tensor",
                lambda e, sqv=sqv, c=c: e.matmul(
                    ps_sq, ones_col, sqv, start=(c == 0), stop=(c == DC - 1)),
                waits=(("sc", sc_done),),
                inc="mm")
            sq_gate[c % 2] = mm_i
            last = mm_i
        m_row = rows[:, p * TW: (p + 1) * TW]
        r_row = rows[:, (2 + p) * TW: (3 + p) * TW]
        t4 = rows[:, 4 * TW: 5 * TW]
        t5 = rows[:, 5 * TW: 6 * TW]
        s.emit("vector",
               lambda e, m_row=m_row: e.tensor_scalar_mul(m_row, ps_sum, 1.0 / D),
               waits=(("mm", last), ("mm", bc_done[p])))
        st["stats_read"] = s.emit(
            "vector", lambda e: e.tensor_scalar_mul(t4, ps_sq, 1.0 / D),
            waits=(("sc", st["sc_sd"]),), inc="ev")
        s.drain("vector")
        s.emit("vector", lambda e, m_row=m_row: e.tensor_mul(t5, m_row, m_row))
        s.drain("vector")
        ev_var = s.emit("vector", lambda e: e.tensor_sub(t4, t4, t5),
                        inc="ev")
        st["sc_sd"] = s.emit(
            "scalar",
            lambda e: e.activation(t4, t4, AF.Sqrt, bias=aux["eps_row"]),
            waits=(("ev", ev_var),), inc="sc")
        ev_r[p] = s.emit("vector",
                         lambda e, r_row=r_row: e.reciprocal(r_row, t4),
                         waits=(("sc", st["sc_sd"]),), inc="ev")

    def stage_b(w):
        p = w % 2
        m_row = rows[:, p * TW: (p + 1) * TW]
        r_row = rows[:, (2 + p) * TW: (3 + p) * TW]
        mm_bc = s.emit(
            "tensor",
            lambda e, m_row=m_row, r_row=r_row: [
                e.matmul(ps_m, ones_row[:1, 0:128], m_row, start=True, stop=True),
                e.matmul(ps_r, ones_row[:1, 0:128], r_row, start=True, stop=True),
            ][-1],
            waits=(("ev", ev_r[p]), ("ev", st["bc_read"])), inc="mm")
        bc_done[p] = mm_bc
        diff = aux["diff_scr"]
        ev_xn = 0

        def em_diff(c):
            xin = x_sb[:, c * T + w * TW: c * T + (w + 1) * TW]
            dv = diff[:, (c % 2) * TW: (c % 2 + 1) * TW]
            s.emit("vector",
                   lambda e, xin=xin, dv=dv: e.tensor_sub(dv, xin, ps_m),
                   waits=(("mm", mm_bc),))

        def em_xn(c):
            nonlocal ev_xn
            xno = xn_sb[:, c * T + w * TW: c * T + (w + 1) * TW]
            dv = diff[:, (c % 2) * TW: (c % 2 + 1) * TW]
            if colsum_to is not None:
                cs = colsum_to[:, c * NW + w: c * NW + w + 1]
                ev_xn = s.emit(
                    "vector",
                    lambda e, dv=dv, xno=xno, cs=cs: e.scalar_tensor_tensor(
                        xno, dv, 1.0, ps_r, ALU.mult, ALU.mult, accum_out=cs),
                    inc="ev")
            else:
                ev_xn = s.emit(
                    "vector",
                    lambda e, dv=dv, xno=xno: e.scalar_tensor_tensor(
                        xno, dv, 1.0, ps_r, ALU.mult, ALU.mult),
                    inc="ev")

        em_diff(0)
        em_diff(1)
        for c in range(DC):
            s.drain("vector")
            em_xn(c)
            if c + 2 < DC:
                em_diff(c + 2)
        st["bc_read"] = ev_xn
        aux["xn_done"][w] = ev_xn

    stage_a(0)
    for w in range(NW):
        if w + 1 < NW:
            stage_a(w + 1)
        stage_b(w)



def build_fused():
    nc = bass.Bass("TRN2", target_bir_lowering=False)
    dr = {}

    def din(name, shape, dtype=F32):
        dr[name] = nc.dram_tensor(name, shape, dtype, kind="ExternalInput").ap()
        return dr[name]

    x_dr = din("x_fm", [D, T])
    wkT_dr = din("wkT", [D, D])
    wvT_dr = din("wvT", [D, D])
    bk_dr = din("bk", [1, D])
    bv_dr = din("bv", [1, D])
    alpha_dr = din("alpha", [1, 1])
    wgate_dr = din("wgate", [D, 8])
    bgate_dr = din("bgate", [1, 8])
    w1m_dr = din("W1m", [D, FFN_H])
    w2m_dr = din("W2m", [FFN_H, D])
    b2m_dr = din("b2m", [1, D])
    w1e_dr = din("W1e", [8 * D, FFN_H])
    b1e_dr = din("b1e", [8, FFN_H])
    w2e_dr = din("W2e", [8 * FFN_H, D])
    b2e_dr = din("b2e", [8, D])
    wq_dr = din("wqT", [D, D])
    wo_dr = din("woT", [D, D])
    boc_dr = din("bo_cols", [128, 8])
    wk1_dr = din("wk1T", [D, D])
    wv1_dr = din("wv1T", [D, D])
    bv1_dr = din("bv1", [1, D])
    pt_dr = din("PT", [64, 64])
    id_dr = din("ident", [128, 128])
    b1mc_dr = din("b1m_cols", [128, 32])
    bqc_dr = din("bq_cols", [128, 8])
    bk1c_dr = din("bk1_cols", [64, 16])
    wq1_dr = din("wq1T", [D, D])
    woP_dr = din("wo1P", [128, 8 * D])
    bo1c_dr = din("bo1_cols", [128, 8])
    bq1c_dr = din("bq1_cols", [64, 16])
    rw_dr = din("router_w", [D, 16])
    gw_dr = din("gate_w", [16 * D, F_EXP])
    uw_dr = din("up_w", [16 * D, F_EXP])
    dw_dr = din("down_w", [16 * F_EXP, D])
    sg_dr = din("sgate_w", [D, F_SH])
    su_dr = din("sup_w", [D, F_SH])
    sd_dr = din("sdown_w", [F_SH, D])

    out_dr = nc.dram_tensor("out_fm", [D, T], F32, kind="ExternalOutput").ap()

    cc_cs_i = nc.dram_tensor("cc_cs_i", [128, 8], F32)
    cc_cs_o = nc.dram_tensor("cc_cs_o", [128, 8], F32)
    cc_m_i = nc.dram_tensor("cc_m_i", [128, 1024], F32)
    cc_m_o = nc.dram_tensor("cc_m_o", [128, 1024], F32)
    cc_kv_i = nc.dram_tensor("cc_kv_i", [64, 16 * 65], F32)
    cc_kv_o = nc.dram_tensor("cc_kv_o", [64, 16 * 65], F32)
    x1_dram = nc.dram_tensor("x1b", [D, T], F32)
    x1_dr = x1_dram.ap()

    with ExitStack() as stack:
        s = Sched(nc, stack)
        a_sb = _alloc(stack, nc, "a_sb", [128, DC * T], F32)
        xn_sb = _alloc(stack, nc, "xn_sb", [128, DC * T], BF16)
        g_sb = _alloc(stack, nc, "g_sb", [128, DC * T], BF16)
        w_sb = _alloc(stack, nc, "w_sb", [128, DC * T], BF16)
        gf = g_sb[:].bitcast(F32)
        m32_sb = _alloc(stack, nc, "m32_sb", [128, 8 * 128], F32)
        m_sb = _alloc(stack, nc, "m_sb", [128, 8 * 128], BF16)
        o_sb = _alloc(stack, nc, "o_sb", [128, 8 * 512], BF16)
        of32 = o_sb[:].bitcast(F32)
        sq_q = _alloc(stack, nc, "sq_q", [128, 2 * 512], BF16)
        t_scr = _alloc(stack, nc, "t_scr", [128, 2 * 512], F32)
        brow_sb = _alloc(stack, nc, "brow_sb", [1, 4096], F32)
        bv_row_sb = _alloc(stack, nc, "bv_row", [1, D], F32)
        kvb_sb = _alloc(stack, nc, "kvb_sb", [64, 16 * 65], BF16)
        cs_w = _alloc(stack, nc, "cs_w", [128, DC * NW], F32)
        cs8 = _alloc(stack, nc, "cs8", [128, 8], F32)
        wg_sb = _alloc(stack, nc, "wg_sb", [128, 8 * 8], F32)
        cols_sb = _alloc(stack, nc, "cols_sb", [128, 16], F32)
        idx_sb = _alloc(stack, nc, "idx_sb", [128, 8], I32)
        bidx_sb = _alloc(stack, nc, "bidx_sb", [1, 2], I32)
        iota_sb = _alloc(stack, nc, "iota_sb", [128, 1], F32)
        cvec_sb = _alloc(stack, nc, "cvec_sb", [128, 8], F32)
        rrow_sb = _alloc(stack, nc, "rrow_sb", [1, 64], F32)
        b1mc_sb = _alloc(stack, nc, "b1mc_sb", [128, 32], F32)
        b1c_sb = _alloc(stack, nc, "b1c_sb", [128, 8], F32)
        bqc_sb = _alloc(stack, nc, "bqc_sb", [128, 8], F32)
        boc_sb = _alloc(stack, nc, "boc_sb", [128, 8], F32)
        bo1c_sb = _alloc(stack, nc, "bo1c_sb", [128, 8], F32)
        bk1c_sb = _alloc(stack, nc, "bk1c_sb", [64, 16], F32)
        bq1c_sb = _alloc(stack, nc, "bq1c_sb", [64, 16], F32)
        eps_ln = _alloc(stack, nc, "eps_ln", [1, 1], F32)
        eps_r6 = _alloc(stack, nc, "eps_r6", [1, 1], F32)
        eps_col = _alloc(stack, nc, "eps_col", [128, 1], F32)
        ones_row = _alloc(stack, nc, "ones_row", [1, TW], F32)
        ones_col = _alloc(stack, nc, "ones_col", [128, 1], F32)
        onesb_col = _alloc(stack, nc, "onesb_col", [128, 1], BF16)
        ptb_sb = _alloc(stack, nc, "ptb_sb", [64, 64], BF16)
        id_sb = _alloc(stack, nc, "id_sb", [128, 128], F32)
        ct_sb = _alloc(stack, nc, "ct_sb", [128, 16 * 16], F32)
        pr_sb = _alloc(stack, nc, "pr_sb", [128, 16], F32)
        rcol_sb = _alloc(stack, nc, "rcol_sb", [128, 4], F32)
        rw_sb = _alloc(stack, nc, "rw_sb", [128, 8 * 16], BF16)

        b1st_sb = gf[0:2, 4096:5120]
        b2st_sb = gf[0:2, 5120:6144]
        kvf_sb = gf[0:64, 7152:8192]
        crow_sb = brow_sb[:1, 2048:3072]

        ps = [_psum(stack, nc, f"ps{i}", [128, 512], F32) for i in range(8)]

        pre = s.new_sem("pre")
        preg = s.new_sem("preg")
        c1i = s.new_sem("c1i"); c1c = s.new_sem("c1c"); c1o = s.new_sem("c1o")
        c2i = s.new_sem("c2i"); c2c = s.new_sem("c2c"); c2o = s.new_sem("c2o")
        c3i = s.new_sem("c3i"); c3c = s.new_sem("c3c"); c3o = s.new_sem("c3o")
        w1s = [s.new_sem("w1A"), s.new_sem("w1B")]
        w2s = [s.new_sem("w2A"), s.new_sem("w2B")]
        b2ch = s.new_sem("b2ch")
        b2g = s.new_sem("b2g")
        dch = s.new_sem("dch")
        dchx = s.new_sem("dchx")
        dg = s.new_sem("dg")
        fch = s.new_sem("fch")
        fg = s.new_sem("fg")
        slch = s.new_sem("slch")
        d3ch = s.new_sem("d3ch")
        d3chx = s.new_sem("d3chx")
        d3g = s.new_sem("d3g")

        init = s.emit("vector", lambda e: [
            e.memset(ones_row[:], 1.0),
            e.memset(ones_col[:], 1.0),
            e.memset(onesb_col[:], 1.0),
            e.memset(eps_ln[:], LN_EPS),
            e.memset(eps_r6[:], 1e-6),
            e.memset(eps_col[:], 1e-6),
        ][-1], inc="ev")
        gp_iota = s.emit("gpsimd", lambda e: [
            e.iota(iota_sb[:, 0:1], [[0, 1]], channel_multiplier=1,
                   allow_small_or_imprecise_dtypes=True),
            e.iota(rrow_sb[:1, 48:56], [[1, 8]], channel_multiplier=0,
                   allow_small_or_imprecise_dtypes=True),
        ][-1], inc="gp")

        s.emit("sync", lambda e: e.dma_start(
            out=a_sb[:].rearrange("p (c t) -> p c t", c=DC),
            in_=x_dr.rearrange("(c p) t -> p c t", p=128)), inc=pre, dma=True)
        s.emit("gpsimd", lambda e: e.dma_start(
            out=w_sb[:, 0:8192].rearrange("p (c j) -> p c j", c=DC),
            in_=wkT_dr.rearrange("(c p) j -> p c j", p=128)), inc=preg, dma=True)
        s.emit("gpsimd", lambda e: e.dma_start(
            out=w_sb[:, 8192:16384].rearrange("p (c j) -> p c j", c=DC),
            in_=wvT_dr.rearrange("(c p) j -> p c j", p=128)), inc=preg, dma=True)
        s.emit("sync", lambda e: e.dma_start(out=brow_sb[:1, 3072:4096], in_=bk_dr),
               inc=pre, dma=True)
        s.emit("sync", lambda e: e.dma_start(out=bv_row_sb[:], in_=bv_dr),
               inc=pre, dma=True)
        s.emit("sync", lambda e: e.dma_start(
            out=wg_sb[:].rearrange("p (c g) -> p c g", c=DC),
            in_=wgate_dr.rearrange("(c p) g -> p c g", p=128)), inc=pre, dma=True)
        s.emit("sync", lambda e: e.dma_start(out=rrow_sb[:1, 32:40], in_=bgate_dr),
               inc=pre, dma=True)
        s.emit("sync", lambda e: e.dma_start(out=rrow_sb[:1, 40:41], in_=alpha_dr),
               inc=pre, dma=True)
        s.emit("sync", lambda e: e.dma_start(out=b1mc_sb[:], in_=b1mc_dr),
               inc=pre, dma=True)
        s.emit("sync", lambda e: e.dma_start(out=bqc_sb[:], in_=bqc_dr),
               inc=pre, dma=True)
        s.emit("sync", lambda e: e.dma_start(out=boc_sb[:], in_=boc_dr),
               inc=pre, dma=True)
        s.emit("sync", lambda e: e.dma_start(out=bo1c_sb[:], in_=bo1c_dr),
               inc=pre, dma=True)
        s.emit("sync", lambda e: e.dma_start(out=bk1c_sb[:], in_=bk1c_dr),
               inc=pre, dma=True)
        s.emit("sync", lambda e: e.dma_start(out=bq1c_sb[:], in_=bq1c_dr),
               inc=pre, dma=True)
        s.emit("sync", lambda e: e.dma_start(out=id_sb[:], in_=id_dr),
               inc=pre, dma=True)
        s.emit("gpsimd", lambda e: e.dma_start(out=ptb_sb[:], in_=pt_dr),
               inc=preg, dma=True)
        s.emit("gpsimd", lambda e: e.dma_start(
            out=rw_sb[:].rearrange("p (c g) -> p c g", c=DC),
            in_=rw_dr.rearrange("(c p) g -> p c g", p=128)), inc=preg, dma=True)
        pre_t = s.c(pre)
        preg_t = s.c(preg)

        aux0 = dict(
            eps_row=eps_ln[:1, :1],
            ps_stat0=ps[0][0:1, :], ps_stat1=ps[1][0:1, :],
            ps_bc0=ps[2][:], ps_bc1=ps[3][:],
            sq_scr=of32[:, 0:1024], diff_scr=of32[:, 1024:2048],
            rows=brow_sb[:1, 0:3072],
            ones_col=ones_col[:], ones_row=ones_row,
            x_ready=(pre, pre_t),
            pre_waits=(("ev", init),),
            xn_done=[0] * NW,
        )
        emit_layernorm(s, a_sb, xn_sb, aux0, colsum_to=cs_w)
        ln0_done = aux0["xn_done"]

        ev_cs = 0
        for c in range(DC):
            ev_cs = s.emit("vector", lambda e, c=c: e.reduce_sum(
                cs8[:, c:c + 1],
                cs_w[:].rearrange("p (c w) -> p c w", c=DC)[:, c, :],
                axis=AX.X), inc="ev" if c == DC - 1 else None)
        s.emit("sync", lambda e: e.dma_start(out=cc_cs_i[:, :], in_=cs8[:]),
               waits=(("ev", ev_cs),), inc=c1i, dma=True)
        s.emit("gpsimd", lambda e: e.collective_compute(
            "AllReduce", ALU.add, replica_groups=RG,
            ins=[cc_cs_i.ap().opt()], outs=[cc_cs_o.ap().opt()]),
            waits=((c1i, 16),), inc=c1c, dma=True, inc_n=1)
        s.emit("sync", lambda e: e.dma_start(out=cs8[:], in_=cc_cs_o[:, :]),
               waits=((c1c, 1),), inc=c1o, dma=True)
        c1o_t = s.c(c1o)

        kg_sb = g_sb[:, 0:8192]
        vg_sb = g_sb[:, 8192:16384]
        wk_v = w_sb[:, 0:8192]
        wv_v = w_sb[:, 8192:16384]
        bk_row = brow_sb[:1, 3072:4096]
        evk_gate = [0, 0]
        scv_last = 0
        kv_used = 0
        ev_m = 0
        m_ps_free = 0
        mm_p1w = 0
        ev_kg_last = 0
        scv_p1_last = 0
        for g in range(2):
            ev_k = sc_v = 0
            for tt in range(NT):
                slot = tt % 2
                kg = kg_sb[:, tt * 512:(tt + 1) * 512]
                vg = vg_sb[:, tt * 512:(tt + 1) * 512]
                kgp = ps[4 + slot][:, 0:512]
                vgp = ps[6][:, 0:512]
                w = tt // (NT // NW)
                xw = ("ev", ln0_done[w])

                def proj(e, g=g, tt=tt, kgp=kgp, vgp=vgp):
                    for c in range(DC):
                        xsl = xn_sb[:, c * T + tt * 128: c * T + (tt + 1) * 128]
                        wks = wk_v[:, c * D + g * 512: c * D + (g + 1) * 512]
                        wvs = wv_v[:, c * D + g * 512: c * D + (g + 1) * 512]
                        e.matmul(kgp, xsl, wks, start=(c == 0), stop=False)
                        e.matmul(vgp, xsl, wvs, start=(c == 0), stop=False)
                    e.matmul(kgp, ones_row[:1, 0:128],
                             bk_row[:1, g * 512:(g + 1) * 512], start=False, stop=True)
                    return e.matmul(vgp, ones_row[:1, 0:128],
                                    bv_row_sb[:1, g * 512:(g + 1) * 512],
                                    start=False, stop=True)
                mm_proj = s.emit(
                    "tensor", proj,
                    waits=(xw, (pre, pre_t), (preg, preg_t),
                           ("ev", evk_gate[slot]), ("sc", scv_last),
                           ("mm", kv_used)),
                    inc="mm")
                mm_p1w = mm_proj
                ev_k = s.emit("vector",
                              lambda e, kg=kg, kgp=kgp: e.tensor_copy(kg, kgp),
                              waits=(("mm", mm_proj),), inc="ev")
                sc_v = s.emit("scalar",
                              lambda e, vg=vg, vgp=vgp: e.copy(vg, vgp),
                              waits=(("mm", mm_proj),), inc="sc")
                evk_gate[slot] = ev_k
                scv_last = sc_v
            ev_kg_last = ev_k
            scv_p1_last = sc_v
            mm_m = 0
            for hh in range(4):
                def macc(e, hh=hh):
                    last = None
                    for tt in range(NT):
                        last = e.matmul(
                            ps[7][:, hh * 128:(hh + 1) * 128],
                            kg_sb[:, tt * 512 + hh * 128: tt * 512 + (hh + 1) * 128],
                            vg_sb[:, tt * 512 + hh * 128: tt * 512 + (hh + 1) * 128],
                            start=(tt == 0), stop=(tt == NT - 1))
                    return last
                mm_m = s.emit("tensor", macc,
                              waits=(("ev", ev_k), ("sc", sc_v),
                                     ("ev", m_ps_free)), inc="mm")
            for hh in range(4):
                h = g * 4 + hh
                ev_m = s.emit("vector", lambda e, h=h, hh=hh: e.tensor_copy(
                    m32_sb[:, h * 128:(h + 1) * 128], ps[7][:, hh * 128:(hh + 1) * 128]),
                    waits=(("mm", mm_m),), inc="ev" if hh == 3 else None)
            kv_used = mm_m
            m_ps_free = ev_m
        mm_p1g = kv_used

        s.emit("sync", lambda e: e.dma_start(out=cc_m_i[:, :], in_=m32_sb[:]),
               waits=(("ev", ev_m),), inc=c2i, dma=True)

        def emit_c2_collective():
            s.emit("gpsimd", lambda e: e.collective_compute(
                "AllReduce", ALU.add, replica_groups=RG,
                ins=[cc_m_i.ap().opt()], outs=[cc_m_o.ap().opt()]),
                waits=((c2i, 16),), inc=c2c, dma=True, inc_n=1)
        s.emit("sync", lambda e: e.dma_start(out=m32_sb[:], in_=cc_m_o[:, :]),
               waits=((c2c, 1),), inc=c2o, dma=True)
        c2o_t = s.c(c2o)

        R = rrow_sb
        lg = R[:1, 0:8]
        scr = R[:1, 8:16]
        probs = R[:1, 16:24]
        scr2 = R[:1, 24:32]
        bgate = R[:1, 32:40]
        iota8 = R[:1, 48:56]
        mm_lg = s.emit("tensor", lambda e: [
            e.matmul(ps[0][0:1, 0:8], cs8[:, c:c + 1], wg_sb[:, c * 8:(c + 1) * 8],
                     start=(c == 0), stop=(c == DC - 1))
            for c in range(DC)][-1],
            waits=((c1o, c1o_t), (pre, pre_t), ("ev", ev_cs)), inc="mm")

        def rt(e):
            e.tensor_scalar(lg, ps[0][0:1, 0:8], 1.0 / 16384.0, None, ALU.mult)
            e.drain()
            e.tensor_add(lg, lg, bgate)
            e.drain()
            e.reduce_max(R[:1, 41:42], lg, axis=AX.X)
            e.drain()
            return e.tensor_scalar(R[:1, 42:43], R[:1, 41:42], -1.0, None, ALU.mult)
        ev_rt1 = s.emit("vector", rt, waits=(("mm", mm_lg), (pre, pre_t),
                                             ("ev", init)), inc="ev")
        sc_exp = s.emit(
            "scalar",
            lambda e: e.activation(scr, lg, AF.Exp, bias=R[:1, 42:43]),
            waits=(("ev", ev_rt1),), inc="sc")

        def rt2(e):
            e.reduce_sum(R[:1, 44:45], scr, axis=AX.X)
            e.drain()
            e.reciprocal(R[:1, 45:46], R[:1, 44:45])
            e.drain()
            e.tensor_scalar(probs, scr, R[:1, 45:46], None, ALU.mult)
            e.drain()
            e.reduce_max(R[:1, 56:57], probs, axis=AX.X)
            e.drain()
            e.tensor_scalar(scr2, probs, R[:1, 56:57], None, ALU.is_ge)
            e.drain()
            e.tensor_scalar(scr, scr2, -9.0, 9.0, ALU.mult, ALU.add)
            e.drain()
            e.tensor_add(scr, scr, iota8)
            e.drain()
            e.tensor_reduce(R[:1, 57:58], scr, AX.X, ALU.min)
            e.drain()
            e.scalar_tensor_tensor(scr, scr2, -2.0, probs, ALU.mult, ALU.add)
            e.drain()
            e.reduce_max(R[:1, 58:59], scr, axis=AX.X)
            e.drain()
            e.tensor_scalar(scr2, scr, R[:1, 58:59], None, ALU.is_ge)
            e.drain()
            e.tensor_scalar(scr2, scr2, -9.0, 9.0, ALU.mult, ALU.add)
            e.drain()
            e.tensor_add(scr2, scr2, iota8)
            e.drain()
            e.tensor_reduce(R[:1, 59:60], scr2, AX.X, ALU.min)
            e.drain()
            e.tensor_add(R[:1, 60:61], R[:1, 56:57], R[:1, 58:59])
            e.drain()
            e.reciprocal(R[:1, 61:62], R[:1, 60:61])
            e.drain()
            e.tensor_mul(R[:1, 62:63], R[:1, 56:57], R[:1, 61:62])
            e.tensor_mul(R[:1, 63:64], R[:1, 58:59], R[:1, 61:62])
            e.drain()
            e.tensor_scalar(R[:1, 8:9], R[:1, 40:41], 0.1, 1.0, ALU.max, ALU.min)
            e.drain()
            e.tensor_scalar(R[:1, 24:25], R[:1, 8:9], -1.0, 1.0, ALU.mult, ALU.add)
            e.drain()
            e.tensor_mul(R[:1, 9:10], R[:1, 24:25], R[:1, 62:63])
            e.tensor_mul(R[:1, 10:11], R[:1, 24:25], R[:1, 63:64])
            e.tensor_scalar(R[:1, 11:12], R[:1, 57:58], 1024.0, None, ALU.mult)
            e.tensor_scalar(R[:1, 12:13], R[:1, 59:60], 1024.0, None, ALU.mult)
            e.tensor_scalar(R[:1, 13:14], R[:1, 57:58], 4096.0, None, ALU.mult)
            e.tensor_scalar(R[:1, 14:15], R[:1, 59:60], 4096.0, None, ALU.mult)
            e.tensor_copy(R[:1, 15:16], R[:1, 57:58])
            e.tensor_copy(R[:1, 16:17], R[:1, 59:60])
            e.drain()
            e.tensor_copy(bidx_sb[:1, 0:1], R[:1, 57:58])
            return e.tensor_copy(bidx_sb[:1, 1:2], R[:1, 59:60])
        ev_rt2 = s.emit("vector", rt2, waits=(("sc", sc_exp), ("gp", gp_iota)),
                        inc="ev")
        mm_bcr = s.emit("tensor", lambda e: e.matmul(
            ps[2][:, 0:9], ones_row[:1, 0:128], R[:1, 8:17], start=True, stop=True),
            waits=(("ev", ev_rt2),), inc="mm")

        def mkcols(e):
            e.tensor_copy(cols_sb[:, 0:9], ps[2][:, 0:9])
            e.drain()
            e.tensor_scalar(cols_sb[:, 9:13], cols_sb[:, 3:7], iota_sb[:, 0:1],
                            None, ALU.add)
            e.drain()
            e.tensor_copy(idx_sb[:, 0:4], cols_sb[:, 9:13])
            return e.tensor_copy(idx_sb[:, 4:6], cols_sb[:, 7:9])
        routing_done = s.emit("vector", mkcols,
                              waits=(("mm", mm_bcr), ("gp", gp_iota)), inc="ev")
        a_col = cols_sb[:, 0:1]
        c1_col = cols_sb[:, 1:2]
        c2_col = cols_sb[:, 2:3]

        s.emit("sync", lambda e: e.dma_start(
            out=brow_sb[:1, D:2 * D], in_=b2m_dr),
            waits=(("mm", mm_bcr),), inc=b2ch, dma=True)
        s.emit("gpsimd", lambda e: e.indirect_dma_start(
            out=b2st_sb[0:2, :], out_offset=None, in_=b2e_dr,
            in_offset=IndirectOffsetOnAxis(idx_sb[0:2, 4:5], 0)),
            waits=(("ev", routing_done), ("mm", mm_p1g)), inc=b2g, dma=True)
        wt_b2a = s.c(b2g)
        ev_b2a = s.emit("vector", lambda e: e.tensor_copy(
            brow_sb[:1, 2 * D:3 * D], b2st_sb[0:1, :]),
            waits=((b2g, wt_b2a), ("mm", mm_bcr)), inc="ev")
        s.emit("gpsimd", lambda e: e.indirect_dma_start(
            out=b2st_sb[0:2, :], out_offset=None, in_=b2e_dr,
            in_offset=IndirectOffsetOnAxis(idx_sb[0:2, 5:6], 0)),
            waits=(("ev", ev_b2a),), inc=b2g, dma=True)
        wt_b2b = s.c(b2g)
        ev_b2b = s.emit("vector", lambda e: e.tensor_copy(
            brow_sb[:1, 3 * D:4 * D], b2st_sb[0:1, :]),
            waits=((b2g, wt_b2b), (b2ch, s.c(b2ch)), ("mm", mm_p1w)), inc="ev")

        def cvec(e):
            cv = brow_sb[:1, 0:D]
            tv = brow_sb[:1, D:2 * D]
            e.tensor_scalar(cv, brow_sb[:1, D:2 * D], R[:1, 8:9], None, ALU.mult)
            e.drain()
            e.tensor_scalar(tv, brow_sb[:1, 2 * D:3 * D], R[:1, 9:10], None, ALU.mult)
            e.drain()
            e.tensor_add(cv, cv, tv)
            e.drain()
            e.tensor_scalar(tv, brow_sb[:1, 3 * D:4 * D], R[:1, 10:11], None, ALU.mult)
            e.drain()
            return e.tensor_add(cv, cv, tv)
        ev_cvec = s.emit("vector", cvec, waits=(("ev", ev_b2b),), inc="ev")

        slab = 0
        w1_freed = [mm_p1w, mm_p1w]
        w2_freed = [mm_p1w, mm_p1w]
        b1_freed = [mm_p1g, mm_p1g]
        hid_ps_gate = [0, 0]
        b1c_gate = [scv_p1_last, scv_p1_last]
        o_ps_gate = [0, 0]
        hid_used = mm_p1g
        sc_gelu = 0
        ev_ffn = 0
        first_hid = [True, True]
        first_out = [True, True]
        acc_last = {}
        hid_w_last = {}

        for p in range(3):
            sc_col = (a_col, c1_col, c2_col)[p]
            for hc in range(NHC):
                sl = slab % 2
                w1v = w_sb[:, sl * 4096:(sl + 1) * 4096]
                w2v = w_sb[:, 8192 + sl * 4096: 8192 + (sl + 1) * 4096]
                b1v = b1st_sb[0:1, sl * HC:(sl + 1) * HC]
                if p == 0:
                    s.emit("gpsimd", lambda e, hc=hc, w1v=w1v: e.dma_start(
                        out=w1v.rearrange("p (c h) -> p c h", c=DC),
                        in_=w1m_dr.rearrange("(c p) h -> p c h", p=128)
                        [:, :, hc * HC:(hc + 1) * HC]),
                        waits=(("mm", w1_freed[sl]),), inc=w1s[sl], dma=True)
                    wt1 = (w1s[sl], s.c(w1s[sl]))
                    s.emit("gpsimd", lambda e, hc=hc, w2v=w2v: e.dma_start(
                        out=w2v.rearrange("p (q f) -> p q f", q=4),
                        in_=w2m_dr.rearrange("(q p) f -> p q f", p=128)
                        [:, hc * 4:(hc + 1) * 4, :]),
                        waits=(("mm", w2_freed[sl]),), inc=w2s[sl], dma=True)
                    wt2 = (w2s[sl], s.c(w2s[sl]))
                    bt = (pre, pre_t)
                    if hc == 0:
                        emit_c2_collective()
                else:
                    iw1 = idx_sb[:, (p - 1):p]
                    iw2 = idx_sb[:, (1 + p):(2 + p)]
                    for c in range(DC):
                        s.emit("gpsimd", lambda e, c=c, hc=hc, w1v=w1v, iw1=iw1: e.indirect_dma_start(
                            out=w1v.rearrange("p (c h) -> p c h", c=DC)[:, c, :],
                            out_offset=None, in_=w1e_dr,
                            in_offset=IndirectOffsetOnAxis(iw1, 0),
                            element_offset=(c * 128) * FFN_H + hc * HC),
                            waits=(("mm", w1_freed[sl]), ("ev", routing_done)),
                            inc=w1s[sl], dma=True)
                    s.emit("gpsimd", lambda e, p=p, hc=hc, sl=sl: e.indirect_dma_start(
                        out=b1st_sb[0:2, sl * HC:(sl + 1) * HC], out_offset=None,
                        in_=b1e_dr,
                        in_offset=IndirectOffsetOnAxis(idx_sb[0:2, (3 + p):(4 + p)], 0),
                        element_offset=hc * HC),
                        waits=(("ev", routing_done), ("mm", b1_freed[sl])),
                        inc=w1s[sl], dma=True)
                    wt1 = (w1s[sl], s.c(w1s[sl]))
                    bt = wt1
                    for qq in range(4):
                        s.emit("gpsimd", lambda e, qq=qq, hc=hc, w2v=w2v, iw2=iw2: e.indirect_dma_start(
                            out=w2v.rearrange("p (q f) -> p q f", q=4)[:, qq, :],
                            out_offset=None, in_=w2e_dr,
                            in_offset=IndirectOffsetOnAxis(iw2, 0),
                            element_offset=(hc * 4 + qq) * 128 * D),
                            waits=(("mm", w2_freed[sl]), ("ev", routing_done)),
                            inc=w2s[sl], dma=True)
                    wt2 = (w2s[sl], s.c(w2s[sl]))
                if p == 0:
                    b1cols = b1mc_sb[:, hc * 4: hc * 4 + 4]
                    b1_ready = bt
                else:
                    mm_b1t = s.emit("tensor", lambda e, b1v=b1v: [
                        e.transpose(ps[6][:, 8 + it: 9 + it],
                                    b1v[:1, it * 128:(it + 1) * 128], ones_row[:1, 0:1])
                        for it in range(4)][-1],
                        waits=(bt, (pre, pre_t), ("sc", b1c_gate[sl])),
                        inc="mm")
                    ev_b1c = s.emit("vector", lambda e, sl=sl: e.tensor_copy(
                        b1c_sb[:, sl * 4:(sl + 1) * 4], ps[6][:, 8:12]),
                        waits=(("mm", mm_b1t),), inc="ev")
                    b1cols = b1c_sb[:, sl * 4: sl * 4 + 4]
                    b1_ready = ("ev", ev_b1c)
                for q in range(4):
                    for w in range(NW):
                        pr = (q * NW + w) % 2
                        pi = ps[pr]
                        def hmm(e, q=q, w=w, pi=pi, w1v=w1v):
                            last = None
                            for c in range(DC):
                                last = e.matmul(
                                    pi[:],
                                    w1v[:, c * HC + q * 128: c * HC + (q + 1) * 128],
                                    xn_sb[:, c * T + w * TW: c * T + (w + 1) * TW],
                                    start=(c == 0), stop=(c == DC - 1))
                            return last
                        extra = ()
                        if first_hid[pr]:
                            extra = (("ev", routing_done),)
                            first_hid[pr] = False
                        mm_h = s.emit("tensor", hmm,
                                      waits=(wt1, ("ev", ln0_done[w]),
                                             ("sc", hid_ps_gate[pr]), *extra),
                                      inc="mm")
                        hv = g_sb[:, q * T + w * TW: q * T + (w + 1) * TW]
                        sc_gelu = s.emit(
                            "scalar",
                            lambda e, hv=hv, pi=pi, b1cols=b1cols, q=q:
                            e.activation(hv, pi[:], AF.Gelu_apprx_tanh,
                                         bias=b1cols[:, q:q + 1]),
                            waits=(("mm", mm_h), ("mm", hid_used), b1_ready),
                            inc="sc")
                        hid_ps_gate[pr] = sc_gelu
                if p != 0:
                    b1c_gate[sl] = sc_gelu
                mm_o_last = 0
                for fc in range(DC):
                    for w in range(NW):
                        pr = (fc * NW + w) % 2
                        po = ps[2 + pr]
                        def omm(e, fc=fc, w=w, po=po, w2v=w2v):
                            last = None
                            for q in range(4):
                                last = e.matmul(
                                    po[:],
                                    w2v[:, q * 1024 + fc * 128: q * 1024 + (fc + 1) * 128],
                                    g_sb[:, q * T + w * TW: q * T + (w + 1) * TW],
                                    start=(q == 0), stop=(q == 3))
                            return last
                        extra = ()
                        if first_out[pr]:
                            extra = (("ev", routing_done),)
                            first_out[pr] = False
                        mm_o = s.emit("tensor", omm,
                                      waits=(wt2, ("sc", sc_gelu),
                                             ("ev", o_ps_gate[pr]), *extra),
                                      inc="mm")
                        av = a_sb[:, fc * T + w * TW: fc * T + (w + 1) * TW]
                        if p == 0 and hc == 0:
                            ev_ffn = s.emit(
                                "vector",
                                lambda e, av=av, po=po: e.tensor_scalar(
                                    av, po[:], a_col, None, ALU.mult),
                                waits=(("mm", mm_o), ("ev", routing_done)),
                                inc="ev")
                        else:
                            ev_ffn = s.emit(
                                "vector",
                                lambda e, av=av, po=po, sc_col=sc_col: e.scalar_tensor_tensor(
                                    av, po[:], sc_col, av, ALU.mult, ALU.add),
                                waits=(("mm", mm_o),), inc="ev")
                        o_ps_gate[pr] = ev_ffn
                        mm_o_last = mm_o
                        if p == 2 and hc == NHC - 1:
                            acc_last[(fc, w)] = ev_ffn
                            if fc == DC - 1:
                                hid_w_last[w] = mm_o
                w1_freed[sl] = s.c("mm")
                w2_freed[sl] = s.c("mm")
                b1_freed[sl] = s.c("mm")
                hid_used = mm_o_last
                slab += 1
        ffn_done_ev = s.c("ev")

        mm_cvt = s.emit("tensor", lambda e: [
            e.transpose(ps[6][:, fc:fc + 1], brow_sb[:1, fc * 128:(fc + 1) * 128],
                        ones_row[:1, 0:1])
            for fc in range(DC)][-1],
            waits=(("ev", ev_cvec), ("sc", scv_p1_last), (pre, pre_t)), inc="mm")
        ev_cvc = s.emit("vector",
                        lambda e: e.tensor_copy(cvec_sb[:, 0:8], ps[6][:, 0:8]),
                        waits=(("mm", mm_cvt),), inc="ev")
        sil_done = 0
        for fc in range(DC):
            for w in range(NW):
                gv = g_sb[:, fc * T + w * TW: fc * T + (w + 1) * TW]
                av = a_sb[:, fc * T + w * TW: fc * T + (w + 1) * TW]
                sil_done = s.emit(
                    "scalar",
                    lambda e, gv=gv, av=av, fc=fc: e.activation(
                        gv, av, AF.Silu, bias=cvec_sb[:, fc:fc + 1]),
                    waits=(("ev", ev_cvc), ("ev", acc_last[(fc, w)]),
                           ("mm", hid_w_last[w] if fc < 4 else mm_p1g)),
                    inc="sc")

        dgq = s.new_sem("dgq")
        for hh in range(2):
            s.emit("gpsimd", lambda e, hh=hh: e.dma_start(
                out=w_sb[:, hh * 4096:(hh + 1) * 4096].rearrange(
                    "p (c j) -> p c j", c=4),
                in_=wq_dr[hh * 512:(hh + 1) * 512, :].rearrange(
                    "(c p) j -> p c j", p=128)),
                waits=(("mm", w1_freed[hh]),), inc=dgq, dma=True)
            s.emit("gpsimd", lambda e, hh=hh: e.dma_start(
                out=w_sb[:, 8192 + hh * 4096: 8192 + (hh + 1) * 4096].rearrange(
                    "p (c j) -> p c j", c=4),
                in_=wo_dr[hh * 512:(hh + 1) * 512, :].rearrange(
                    "(c p) j -> p c j", p=128)),
                waits=(("mm", w2_freed[hh]),), inc=dg, dma=True)
        dgq_t = s.c(dgq)
        s.emit("sync", lambda e: e.dma_start(
            out=a_sb[:].rearrange("p (c t) -> p c t", c=DC),
            in_=x_dr.rearrange("(c p) t -> p c t", p=128)),
            waits=(("sc", sil_done),), inc=dchx, dma=True)
        dch_t = s.c(dch)
        dchx_t = s.c(dchx)
        dg_t = s.c(dg)

        ev_mbf = s.emit("vector", lambda e: e.tensor_copy(m_sb[:], m32_sb[:]),
                        waits=((c2o, c2o_t),), inc="ev")

        x1_done = 0
        o_used = 0
        q_used = [0, 0]
        y_evac = [ev_kg_last, ev_kg_last]
        for w in range(NW):
            sc_o = 0
            for h in range(8):
                pq = ps[h % 2]
                def qmm(e, h=h, w=w, pq=pq):
                    last = None
                    for c in range(DC):
                        last = e.matmul(
                            pq[:],
                            w_sb[:, c * 1024 + h * 128: c * 1024 + (h + 1) * 128],
                            xn_sb[:, c * T + w * TW: c * T + (w + 1) * TW],
                            start=(c == 0), stop=(c == DC - 1))
                    return last
                mm_q = s.emit("tensor", qmm,
                              waits=((dch, dch_t), (dgq, dgq_t),
                                     ("sc", sil_done),
                                     ("mm", q_used[h % 2])),
                              inc="mm")
                qv = sq_q[:, (h % 2) * 512:(h % 2 + 1) * 512]
                ev_q = s.emit("scalar",
                              lambda e, qv=qv, pq=pq, h=h: e.activation(
                                  qv, pq[:], AF.Identity, bias=bqc_sb[:, h:h + 1]),
                              waits=(("mm", mm_q),), inc="sc")
                po = ps[2 + h % 2]
                mm_oh = s.emit("tensor",
                               lambda e, h=h, qv=qv, po=po: e.matmul(
                                   po[:], m_sb[:, h * 128:(h + 1) * 128], qv,
                                   start=True, stop=True),
                               waits=(("sc", ev_q), ("ev", ev_mbf),
                                      ("sc", sc_o), ("mm", o_used)), inc="mm")
                q_used[h % 2] = mm_oh
                ov = o_sb[:, h * 512:(h + 1) * 512]
                sc_o = s.emit("scalar",
                              lambda e, ov=ov, po=po: e.copy(ov, po[:]),
                              waits=(("mm", mm_oh),), inc="sc")
            for ic in range(DC):
                pr = ic % 2
                py = ps[4 + pr]
                def ymm(e, ic=ic, py=py):
                    last = None
                    for cc in range(DC):
                        last = e.matmul(
                            py[:],
                            w_sb[:, 8192 + cc * 1024 + ic * 128:
                                 8192 + cc * 1024 + (ic + 1) * 128],
                            o_sb[:, cc * 512:(cc + 1) * 512],
                            start=(cc == 0), stop=(cc == DC - 1))
                    return last
                mm_y = s.emit("tensor", ymm,
                              waits=((dch, dch_t), (dg, dg_t), ("sc", sc_o),
                                     ("ev", y_evac[pr])), inc="mm")
                o_used = mm_y
                tv = t_scr[:, pr * 512:(pr + 1) * 512]
                gv = g_sb[:, ic * T + w * TW: ic * T + (w + 1) * TW]
                av = a_sb[:, ic * T + w * TW: ic * T + (w + 1) * TW]
                y_evac[pr] = s.emit(
                    "vector",
                    lambda e, tv=tv, gv=gv, py=py, ic=ic: e.scalar_tensor_tensor(
                        tv, py[:], boc_sb[:, ic:ic + 1], gv, ALU.add, ALU.mult),
                    waits=(("mm", mm_y),), inc="ev")
                s.drain("vector")
                x1_done = s.emit("vector",
                                 lambda e, av=av, tv=tv: e.tensor_add(av, av, tv),
                                 waits=((dchx, dchx_t),), inc="ev")
        st_x1 = s.emit("sync", lambda e: e.dma_start(
            out=x1_dr.rearrange("(c p) t -> p c t", p=128),
            in_=a_sb[:].rearrange("p (c t) -> p c t", c=DC)),
            waits=(("ev", x1_done),), inc="st", dma=True)
        st_x1_t = s.c("st")

        aux1 = dict(
            eps_row=eps_ln[:1, :1],
            ps_stat0=ps[0][0:1, :], ps_stat1=ps[1][0:1, :],
            ps_bc0=ps[2][:], ps_bc1=ps[3][:],
            sq_scr=gf[:, 0:1024], diff_scr=gf[:, 1024:2048],
            rows=gf[0:1, 2048:5120],
            ones_col=ones_col[:], ones_row=ones_row,
            x_ready=("ev", x1_done),
            pre_waits=(("ev", x1_done),),
            xn_done=[0] * NW,
        )
        emit_layernorm(s, a_sb, xn_sb, aux1, colsum_to=None)
        xn1_done = aux1["xn_done"]

        mm_all_d = s.c("mm")
        fgk = s.new_sem("fgk")
        s.emit("gpsimd", lambda e: e.dma_start(
            out=w_sb[:, 0:8192].rearrange("p (c j) -> p c j", c=DC),
            in_=wk1_dr.rearrange("(c p) j -> p c j", p=128)),
            waits=(("mm", max(q_used)),), inc=fgk, dma=True)
        s.emit("gpsimd", lambda e: e.dma_start(
            out=w_sb[:, 8192:16384].rearrange("p (c j) -> p c j", c=DC),
            in_=wv1_dr.rearrange("(c p) j -> p c j", p=128)),
            waits=(("mm", mm_all_d),), inc=fg, dma=True)
        s.emit("sync",
               lambda e: e.dma_start(out=brow_sb[:1, 1024:2048], in_=bv1_dr),
               waits=(("mm", mm_all_d),), inc=fch, dma=True)
        fch_t = s.c(fch)
        fg_t = s.c(fg)
        fgk_t = s.c(fgk)
        bv1_row = brow_sb[:1, 1024:2048]
        v_all = o_sb[:, 0:2048]
        kf_all = o_sb[:, 2048:3072]
        kv_sb = kvf_sb
        ln1_all = ("ev", xn1_done[NW - 1])

        kv_done = 0
        kv_done_par = [0, 0]
        v_used = 0
        kf_used = 0
        pv_gate = [0, 0]
        pk_gate = [0, 0]
        pf_gate = [0, 0]
        first_kv = True
        sc_bvt = 0
        ev_v_prev = x1_done
        bvt_sb = t_scr[:, 0:128]
        for hp in range(8):
            mm_bv = s.emit("tensor", lambda e, hp=hp: e.matmul(
                ps[7][:, 0:128], ones_row[:1, 0:128],
                bv1_row[:1, hp * 128:(hp + 1) * 128], start=True, stop=True),
                waits=((fch, fch_t), ("sc", sc_bvt), ("ev", ev_m)), inc="mm")
            sc_bvt = s.emit("scalar", lambda e: e.copy(bvt_sb, ps[7][:, 0:128]),
                            waits=(("mm", mm_bv), ("ev", ev_v_prev)), inc="sc")
            ev_v = 0
            for tt in range(NT):
                pv = ps[2 + tt % 2]
                def vmm(e, hp=hp, tt=tt, pv=pv):
                    last = None
                    for c in range(DC):
                        last = e.matmul(
                            pv[:, 0:128],
                            xn_sb[:, c * T + tt * 128: c * T + (tt + 1) * 128],
                            w_sb[:, 8192 + c * 1024 + hp * 128:
                                 8192 + c * 1024 + (hp + 1) * 128],
                            start=(c == 0), stop=(c == DC - 1))
                    return last
                mm_v = s.emit("tensor", vmm,
                              waits=((fg, fg_t), ln1_all,
                                     ("mm", v_used), ("ev", pv_gate[tt % 2])),
                              inc="mm")
                ev_v = s.emit("vector", lambda e, tt=tt, pv=pv: e.tensor_add(
                    v_all[:, tt * 128:(tt + 1) * 128], pv[:, 0:128], bvt_sb),
                    waits=(("mm", mm_v), ("mm", v_used), ("sc", sc_bvt)),
                    inc="ev")
                pv_gate[tt % 2] = ev_v
                ev_v_prev = ev_v
            for half in range(2):
                h = 2 * hp + half
                ev_kf = 0
                for w in range(NW):
                    pk = ps[w % 2]
                    def kfmm(e, h=h, w=w, pk=pk):
                        last = None
                        for c in range(DC):
                            last = e.matmul(
                                pk[0:64, :],
                                w_sb[:, c * 1024 + h * 64: c * 1024 + (h + 1) * 64],
                                xn_sb[:, c * T + w * TW: c * T + (w + 1) * TW],
                                start=(c == 0), stop=(c == DC - 1))
                        return last
                    mm_kf = s.emit("tensor", kfmm,
                                   waits=((fgk, fgk_t), ln1_all,
                                          ("mm", kf_used), ("sc", pk_gate[w % 2])),
                                   inc="mm")
                    kfv = sq_q[0:64, (w % 2) * 512:(w % 2 + 1) * 512]
                    ev_kfm = s.emit("scalar",
                                    lambda e, kfv=kfv, pk=pk, h=h: e.activation(
                                        kfv, pk[0:64, :], AF.Identity,
                                        bias=bk1c_sb[:, h:h + 1]),
                                    waits=(("mm", mm_kf),), inc="sc")
                    pk_gate[w % 2] = ev_kfm
                    for it in range(4):
                        tt = w * 4 + it
                        pf = ps[4 + tt % 2]
                        mm_f = s.emit(
                            "tensor",
                            lambda e, it=it, kfv=kfv, pf=pf: e.matmul(
                                pf[:, 0:64], kfv[:, it * 128:(it + 1) * 128],
                                ptb_sb[0:64, :], start=True, stop=True),
                            waits=(("sc", ev_kfm),
                                   ("sc", pf_gate[tt % 2])), inc="mm")
                        ev_kf = s.emit(
                            "scalar", lambda e, tt=tt, pf=pf: e.activation(
                                kf_all[:, tt * 64:(tt + 1) * 64], pf[:, 0:64],
                                AF.Relu, scale=L1_NORM, bias=eps_col[:, 0:1]),
                            waits=(("mm", mm_f), ("mm", kf_used)), inc="sc")
                        pf_gate[tt % 2] = ev_kf
                cb = (h % 2) * 128
                def kvmm(e, half=half, cb=cb):
                    base = half * 64
                    for tt in range(NT):
                        e.matmul(ps[6][0:64, cb:cb + 64],
                                 kf_all[:, tt * 64:(tt + 1) * 64],
                                 v_all[:, tt * 128 + base: tt * 128 + base + 64],
                                 start=(tt == 0), stop=(tt == NT - 1))
                    for tt in range(NT):
                        last = e.matmul(ps[6][0:64, cb + 64:cb + 65],
                                        kf_all[:, tt * 64:(tt + 1) * 64],
                                        onesb_col[:],
                                        start=(tt == 0), stop=(tt == NT - 1))
                    return last
                extra = ()
                if first_kv:
                    extra = (("ev", ev_cvc),)
                    first_kv = False
                else:
                    extra = (("ev", kv_done_par[h % 2]),)
                mm_kv = s.emit("tensor", kvmm,
                               waits=(("sc", ev_kf), ("ev", ev_v), *extra),
                               inc="mm")
                kf_used = mm_kv
                if half == 1:
                    v_used = mm_kv
                kv_done = s.emit("vector", lambda e, h=h, cb=cb: e.tensor_copy(
                    kv_sb[0:64, h * 65:(h + 1) * 65], ps[6][0:64, cb:cb + 65]),
                    waits=(("mm", mm_kv),), inc="ev")
                kv_done_par[h % 2] = kv_done

        s.emit("sync", lambda e: e.dma_start(out=cc_kv_i[:, :], in_=kv_sb[0:64, :]),
               waits=(("ev", kv_done),), inc=c3i, dma=True)
        c3i_t = s.c(c3i)
        s.emit("gpsimd", lambda e: e.collective_compute(
            "AllReduce", ALU.add, replica_groups=RG,
            ins=[cc_kv_i.ap().opt()], outs=[cc_kv_o.ap().opt()]),
            waits=((c3i, 16),), inc=c3c, dma=True, inc_n=1)
        s.emit("sync", lambda e: e.dma_start(out=kv_sb[0:64, :], in_=cc_kv_o[:, :]),
               waits=((c3c, 1),), inc=c3o, dma=True)
        c3o_t = s.c(c3o)

        B3mm = s.c("mm")
        B3ev = s.c("ev")
        B3sc = s.c("sc")
        s.emit("tensor", lambda e: e.nop(), waits=(("ev", B3ev), ("sc", B3sc)))
        s.emit("vector", lambda e: e.nop(), waits=(("mm", B3mm), ("sc", B3sc)))
        s.emit("scalar", lambda e: e.nop(), waits=(("mm", B3mm), ("ev", B3ev)))

        lg_read = 0
        ev_ct = 0
        for w in range(NW):
            mm_lg3 = s.emit("tensor", lambda e, w=w: [
                e.matmul(ps[0][0:16, :], rw_sb[:, c * 16:(c + 1) * 16],
                         xn_sb[:, c * T + w * TW: c * T + (w + 1) * TW],
                         start=(c == 0), stop=(c == DC - 1))
                for c in range(DC)][-1],
                waits=((preg, preg_t), ("ev", xn1_done[w]), ("ev", lg_read)),
                inc="mm")
            ev_lg = s.emit("vector", lambda e: e.tensor_copy(
                t_scr[0:16, 0:512], ps[0][0:16, :]),
                waits=(("mm", mm_lg3),), inc="ev")
            lg_read = ev_lg
            for it in range(4):
                tt = w * 4 + it
                mm_tr = s.emit("tensor", lambda e, it=it: e.transpose(
                    ps[1][:, 0:16], t_scr[0:16, it * 128:(it + 1) * 128],
                    id_sb[0:16, 0:16]),
                    waits=(("ev", ev_lg), ("ev", ev_ct)), inc="mm")

                def sfm(e):
                    e.reduce_max(rcol_sb[:, 0:1], ps[1][:, 0:16], axis=AX.X)
                    e.drain()
                    return e.tensor_scalar(rcol_sb[:, 1:2], rcol_sb[:, 0:1],
                                           -1.0, None, ALU.mult)
                ev_s1 = s.emit("vector", sfm, waits=(("mm", mm_tr),), inc="ev")
                sc_e = s.emit("scalar", lambda e: e.activation(
                    pr_sb[:, 0:16], ps[1][:, 0:16], AF.Exp,
                    bias=rcol_sb[:, 1:2]),
                    waits=(("ev", ev_s1),), inc="sc")

                def top2(e, tt=tt):
                    ctv = ct_sb[:, tt * 16:(tt + 1) * 16]
                    e.reduce_sum(rcol_sb[:, 0:1], pr_sb[:, 0:16], axis=AX.X)
                    e.drain()
                    e.reciprocal(rcol_sb[:, 1:2], rcol_sb[:, 0:1])
                    e.drain()
                    e.tensor_scalar(pr_sb[:, 0:16], pr_sb[:, 0:16],
                                    rcol_sb[:, 1:2], None, ALU.mult)
                    e.drain()
                    e.reduce_max(rcol_sb[:, 0:1], pr_sb[:, 0:16], axis=AX.X)
                    e.drain()
                    e.tensor_scalar(ctv, pr_sb[:, 0:16],
                                    rcol_sb[:, 0:1], None, ALU.is_ge)
                    e.drain()
                    e.scalar_tensor_tensor(ctv, ctv, -2.0, pr_sb[:, 0:16],
                                           ALU.mult, ALU.add)
                    e.drain()
                    e.reduce_max(rcol_sb[:, 2:3], ctv, axis=AX.X)
                    e.drain()
                    e.tensor_scalar(ctv, pr_sb[:, 0:16],
                                    rcol_sb[:, 2:3], None, ALU.is_ge)
                    e.drain()
                    return e.tensor_mul(ctv, ctv, pr_sb[:, 0:16])
                ev_ct = s.emit("vector", top2, waits=(("sc", sc_e),), inc="ev")
        coeff_done = ev_ct

        kv_ready = s.emit("vector", lambda e: e.tensor_copy(
            kvb_sb[:], kv_sb[0:64, :]), waits=((c3o, c3o_t),), inc="ev")

        wd_slab = g_sb[:, 8192:12288]
        gu_freed = [B3mm, B3mm]
        dn_freed = B3mm
        slchb = [s.new_sem("slA3"), s.new_sem("slB3")]
        dnch = s.new_sem("dnch")
        hid_ps_gate = [0, 0]
        up_ps_gate = [0, 0]
        o_ps_gate = [0, 0]
        cb_gate = 0
        sil_free = [0, 0]
        crow_gate = 0
        inter_used = 0
        ev_ffn = 0
        sc_sil = 0
        first_acc = True
        acc3_last = {}
        hid3_w_last = {}
        for pe in range(-2, 16):
            soff = (pe + 2) * 512
            b = (pe + 2) % 2
            wg_slab = w_sb[:, b * 8192: b * 8192 + 4096]
            wu_slab = w_sb[:, b * 8192 + 4096: (b + 1) * 8192]
            if pe < 0:
                s.emit("gpsimd", lambda e, soff=soff, wg_slab=wg_slab: e.dma_start(
                    out=wg_slab.rearrange("p (c h) -> p c h", c=DC),
                    in_=sg_dr.rearrange("(c p) h -> p c h", p=128)
                    [:, :, soff:soff + 512]),
                    waits=(("mm", gu_freed[b]),), inc=slchb[b], dma=True)
                s.emit("gpsimd", lambda e, soff=soff, wu_slab=wu_slab: e.dma_start(
                    out=wu_slab.rearrange("p (c h) -> p c h", c=DC),
                    in_=su_dr.rearrange("(c p) h -> p c h", p=128)
                    [:, :, soff:soff + 512]),
                    waits=(("mm", gu_freed[b]),), inc=slchb[b], dma=True)
                s.emit("gpsimd", lambda e, soff=soff: e.dma_start(
                    out=wd_slab.rearrange("p (q f) -> p q f", q=4),
                    in_=sd_dr.rearrange("(q p) f -> p q f", p=128)
                    [:, soff // 128: soff // 128 + 4, :]),
                    waits=(("mm", dn_freed),), inc=dnch, dma=True)
            else:
                s.emit("gpsimd", lambda e, pe=pe, wg_slab=wg_slab: e.dma_start(
                    out=wg_slab.rearrange("p (c h) -> p c h", c=DC),
                    in_=gw_dr[pe * D:(pe + 1) * D, :].rearrange(
                        "(c p) h -> p c h", p=128)),
                    waits=(("mm", gu_freed[b]),), inc=slchb[b], dma=True)
                s.emit("gpsimd", lambda e, pe=pe, wu_slab=wu_slab: e.dma_start(
                    out=wu_slab.rearrange("p (c h) -> p c h", c=DC),
                    in_=uw_dr[pe * D:(pe + 1) * D, :].rearrange(
                        "(c p) h -> p c h", p=128)),
                    waits=(("mm", gu_freed[b]),), inc=slchb[b], dma=True)
                s.emit("gpsimd", lambda e, pe=pe: e.dma_start(
                    out=wd_slab.rearrange("p (q f) -> p q f", q=4),
                    in_=dw_dr[pe * F_EXP:(pe + 1) * F_EXP, :].rearrange(
                        "(q p) f -> p q f", p=128)),
                    waits=(("mm", dn_freed),), inc=dnch, dma=True)
            slch_t = s.c(slchb[b])
            wtd = (dnch, s.c(dnch))
            for w in range(NW):
                if pe >= 0:
                    mm_ctr = s.emit("tensor", lambda e, pe=pe, w=w: [
                        e.transpose(ps[4][0:1, it * 128:(it + 1) * 128],
                                    ct_sb[:, (w * 4 + it) * 16 + pe:
                                          (w * 4 + it) * 16 + pe + 1],
                                    id_sb[:])
                        for it in range(4)][-1],
                        waits=(("ev", coeff_done), ("ev", cb_gate),
                               (pre, pre_t)), inc="mm")
                    ev_cr = s.emit("vector", lambda e: e.tensor_copy(
                        crow_sb[:1, 0:512], ps[4][0:1, :]),
                        waits=(("mm", mm_ctr), ("mm", crow_gate),
                               (dch, dch_t)), inc="ev")
                    mm_cb = s.emit("tensor", lambda e: e.matmul(
                        ps[5][:], ones_row[:1, 0:128], crow_sb[:1, 0:512],
                        start=True, stop=True),
                        waits=(("ev", ev_cr), ("ev", cb_gate)), inc="mm")
                    crow_gate = mm_cb
                for q in range(4):
                    pr = (q * NW + w) % 2
                    pg = ps[pr]
                    pu = ps[2 + pr]
                    def gmm(e, q=q, w=w, pg=pg, wg_slab=wg_slab):
                        last = None
                        for c in range(DC):
                            last = e.matmul(
                                pg[:],
                                wg_slab[:, c * 512 + q * 128: c * 512 + (q + 1) * 128],
                                xn_sb[:, c * T + w * TW: c * T + (w + 1) * TW],
                                start=(c == 0), stop=(c == DC - 1))
                        return last
                    mm_g = s.emit("tensor", gmm,
                                  waits=((slchb[b], slch_t),
                                         ("ev", coeff_done),
                                         ("sc", hid_ps_gate[pr])), inc="mm")
                    def umm(e, q=q, w=w, pu=pu, wu_slab=wu_slab):
                        last = None
                        for c in range(DC):
                            last = e.matmul(
                                pu[:],
                                wu_slab[:, c * 512 + q * 128: c * 512 + (q + 1) * 128],
                                xn_sb[:, c * T + w * TW: c * T + (w + 1) * TW],
                                start=(c == 0), stop=(c == DC - 1))
                        return last
                    mm_u = s.emit("tensor", umm,
                                  waits=((slchb[b], slch_t), ("ev", coeff_done),
                                         ("ev", up_ps_gate[pr])), inc="mm")
                    sv = sq_q[:, pr * 512:(pr + 1) * 512]
                    sc_sil = s.emit("scalar",
                                    lambda e, sv=sv, pg=pg: e.activation(
                                        sv, pg[:], AF.Silu),
                                    waits=(("mm", mm_g), ("mm", inter_used),
                                           ("ev", sil_free[pr])),
                                    inc="sc")
                    hid_ps_gate[pr] = sc_sil
                    hv = g_sb[:, q * T + w * TW: q * T + (w + 1) * TW]
                    if pe < 0:
                        sil_free[pr] = up_ps_gate[pr] = s.emit(
                            "vector",
                            lambda e, hv=hv, sv=sv, pu=pu: e.tensor_mul(
                                hv, sv, pu[:]),
                            waits=(("mm", mm_u), ("sc", sc_sil),
                                   ("mm", inter_used)), inc="ev")
                    else:
                        tv = t_scr[:, pr * 512:(pr + 1) * 512]
                        sil_free[pr] = s.emit(
                            "vector",
                            lambda e, tv=tv, sv=sv, pu=pu: e.tensor_mul(
                                tv, sv, pu[:]),
                            waits=(("mm", mm_u), ("sc", sc_sil),
                                   ("mm", inter_used)), inc="ev")
                        s.drain("vector")
                        up_ps_gate[pr] = s.emit(
                            "vector",
                            lambda e, hv=hv, tv=tv: e.tensor_mul(
                                hv, tv, ps[5][:]),
                            waits=(("mm", crow_gate),), inc="ev")
                if pe >= 0:
                    cb_gate = up_ps_gate[(3 * NW + w) % 2]
            gu_freed[b] = s.c("mm")
            mm_o_last = 0
            for fc in range(DC):
                for w in range(NW):
                    pr = (fc * NW + w) % 2
                    po = ps[6 + pr]
                    def dmm(e, fc=fc, w=w, po=po):
                        last = None
                        for q in range(4):
                            last = e.matmul(
                                po[:],
                                wd_slab[:, q * 1024 + fc * 128:
                                        q * 1024 + (fc + 1) * 128],
                                g_sb[:, q * T + w * TW: q * T + (w + 1) * TW],
                                start=(q == 0), stop=(q == 3))
                        return last
                    mm_o = s.emit("tensor", dmm,
                                  waits=(wtd,
                                         ("ev", up_ps_gate[0]),
                                         ("ev", up_ps_gate[1]),
                                         ("ev", o_ps_gate[pr])), inc="mm")
                    av = a_sb[:, fc * T + w * TW: fc * T + (w + 1) * TW]
                    if pe == -2:
                        extra = (("st", st_x1_t),) if first_acc else ()
                        first_acc = False
                        ev_ffn = s.emit(
                            "vector",
                            lambda e, av=av, po=po: e.tensor_copy(av, po[:]),
                            waits=(("mm", mm_o), *extra), inc="ev")
                    else:
                        ev_ffn = s.emit(
                            "vector",
                            lambda e, av=av, po=po: e.tensor_add(av, av, po[:]),
                            waits=(("mm", mm_o),), inc="ev")
                    o_ps_gate[pr] = ev_ffn
                    mm_o_last = mm_o
                    if pe == 15:
                        acc3_last[(fc, w)] = ev_ffn
                        if fc == DC - 1:
                            hid3_w_last[w] = mm_o
            dn_freed = s.c("mm")
            inter_used = mm_o_last
        ffn_done = s.c("ev")

        sil_done3 = 0
        for fc in range(DC):
            for w in range(NW):
                gv = g_sb[:, fc * T + w * TW: fc * T + (w + 1) * TW]
                av = a_sb[:, fc * T + w * TW: fc * T + (w + 1) * TW]
                sil_done3 = s.emit("scalar",
                                   lambda e, gv=gv, av=av: e.activation(
                                       gv, av, AF.Silu),
                                   waits=(("ev", acc3_last[(fc, w)]),
                                          ("mm", hid3_w_last[w] if fc < 4
                                           else inter_used),
                                          ("ev", kv_ready), (c3i, c3i_t)),
                                   inc="sc")

        s.emit("gpsimd", lambda e: e.dma_start(
            out=w_sb[:, 0:8192].rearrange("p (hp j) -> p hp j", hp=8),
            in_=woP_dr.rearrange("p (hp j) -> p hp j", hp=8)),
            waits=(("mm", gu_freed[0]),), inc=d3g, dma=True)
        d3ch_t = s.c(d3ch)
        s.emit("sync", lambda e: e.dma_start(
            out=a_sb[:].rearrange("p (c t) -> p c t", c=DC),
            in_=x1_dr.rearrange("(c p) t -> p c t", p=128)),
            waits=(("sc", sil_done3), ("st", st_x1_t)), inc=d3chx, dma=True)
        d3chx_t = s.c(d3chx)
        wo_v = w_sb[:, 0:8192]

        out_done = 0
        wq_used = 0
        pq_gate = [0, 0]
        qv_used = [[0, 0], [0, 0]]
        pf_gate = [0, 0]
        ybank_gate = [0, 0]
        po_free = [0, 0]
        tv_free = [0, 0]
        oa_done = 0
        dn_gate = [0, 0]
        py_gate = [0, 0]
        crow_d_gate = [B3mm, B3mm]
        first_qv = [True, True]
        sil_last = [sil_free[0], sil_free[1]]
        d3gb = [s.new_sem("d3gA"), s.new_sem("d3gB")]
        wqh_used = [gu_freed[1], gu_freed[1]]
        for w in range(NW):
            for H in range(2):
                hb = (w * 2 + H) % 2
                wqh_v = w_sb[:, 8192 + hb * 4096: 8192 + (hb + 1) * 4096]
                s.emit("gpsimd", lambda e, H=H, wqh_v=wqh_v: e.dma_start(
                    out=wqh_v.rearrange("p (c j) -> p c j", c=DC),
                    in_=wq1_dr.rearrange("(c p) j -> p c j", p=128)
                    [:, :, H * 512:(H + 1) * 512]),
                    waits=(("mm", wqh_used[hb]),),
                    inc=d3gb[hb], dma=True)
                d3g_t = s.c(d3gb[hb])
                for hp_l in range(4):
                    hp = H * 4 + hp_l
                    pr2 = hp % 2
                    po = ps[4 + pr2]
                    pq = ps[pr2]
                    qkv_mm = [None, None]
                    def qmm(e, hp_l=hp_l, w=w, pq=pq, wqh_v=wqh_v):
                        last = None
                        for c in range(DC):
                            last = e.matmul(
                                pq[:],
                                wqh_v[:, c * 512 + hp_l * 128:
                                      c * 512 + (hp_l + 1) * 128],
                                xn_sb[:, c * T + w * TW: c * T + (w + 1) * TW],
                                start=(c == 0), stop=(c == DC - 1))
                        return last
                    mm_q = s.emit("tensor", qmm,
                                  waits=((d3ch, d3ch_t), (d3gb[hb], d3g_t),
                                         ("sc", sil_done3),
                                         ("sc", pq_gate[pr2])), inc="mm")
                    wq_used = mm_q
                    crow_p = brow_sb[:1, 2048 + pr2 * 1024: 3072 + pr2 * 1024]
                    psd = ps[6 + pr2]
                    qv_arena = sq_q if pr2 == 0 else m_sb
                    sc_qv = [0, 0]
                    for half in range(2):
                        h = 2 * hp + half
                        qv = qv_arena[0:64, half * 512:(half + 1) * 512]
                        extra = ()
                        if first_qv[half]:
                            extra = (("ev", max(sil_last)),)
                            first_qv[half] = False
                        sc_qv[half] = s.emit(
                            "scalar",
                            lambda e, qv=qv, pq=pq, h=h, half=half: e.activation(
                                qv, pq[half * 64:(half + 1) * 64, :],
                                AF.Identity, bias=bq1c_sb[:, h:h + 1]),
                            waits=(("mm", mm_q), ("mm", qv_used[pr2][half]),
                                   *extra),
                            inc="sc")
                    pq_gate[pr2] = sc_qv[1]
                    pd_last = 0
                    for half in range(2):
                        h = 2 * hp + half
                        qv = qv_arena[0:64, half * 512:(half + 1) * 512]
                        pf = ps[2 + half]
                        mm_f = s.emit("tensor",
                                      lambda e, qv=qv, pf=pf: e.matmul(
                                          pf[0:64, :], ptb_sb[:], qv,
                                          start=True, stop=True),
                                      waits=(("sc", sc_qv[half]),
                                             ("sc", pf_gate[half]),
                                             ("ev", ybank_gate[half])), inc="mm")
                        pf_gate[half] = s.emit(
                            "scalar", lambda e, qv=qv, pf=pf: e.activation(
                                qv, pf[0:64, :], AF.Relu, scale=L1_NORM,
                                bias=eps_col[0:64, :]),
                            waits=(("mm", mm_f),), inc="sc")
                        mm_qkv = s.emit(
                            "tensor",
                            lambda e, h=h, half=half, qv=qv, po=po: e.matmul(
                                po[half * 64:(half + 1) * 64, :],
                                kvb_sb[:, h * 65: h * 65 + 64], qv,
                                start=True, stop=True),
                            waits=(("sc", pf_gate[half]), ("ev", kv_ready),
                                   ("ev", po_free[pr2])), inc="mm")
                        qkv_mm[half] = mm_qkv
                        mm_dn = s.emit(
                            "tensor",
                            lambda e, h=h, qv=qv, psd=psd: e.matmul(
                                psd[0:1, :],
                                kvb_sb[:, h * 65 + 64: h * 65 + 65], qv,
                                start=True, stop=True),
                            waits=(("sc", dn_gate[pr2]), ("ev", ev_m)),
                            inc="mm")
                        qv_used[pr2][half] = mm_dn
                        pd_last = s.emit(
                            "scalar",
                            lambda e, half=half, psd=psd, crow_p=crow_p:
                            e.activation(
                                crow_p[:1, half * 512:(half + 1) * 512],
                                psd[0:1, :], AF.Identity,
                                bias=eps_r6[:1, :1]),
                            waits=(("mm", mm_dn), ("mm", crow_d_gate[pr2])),
                            inc="sc")
                    dn_gate[pr2] = pd_last
                    ev_rcp = s.emit(
                        "vector",
                        lambda e, crow_p=crow_p: e.reciprocal(
                            crow_p[:1, 0:1024], crow_p[:1, 0:1024]),
                        waits=(("sc", pd_last),), inc="ev")
                    tvp = t_scr[:, pr2 * 512:(pr2 + 1) * 512]
                    sc_oc = s.emit(
                        "scalar",
                        lambda e, tvp=tvp, po=po: e.copy(tvp, po[:]),
                        waits=(("mm", qkv_mm[1]), ("ev", tv_free[pr2])),
                        inc="sc")
                    mm_bc = s.emit(
                        "tensor",
                        lambda e, po=po, crow_p=crow_p: [
                            e.matmul(po[0:64, :], ones_row[:1, 0:64],
                                     crow_p[:1, 0:512], start=True, stop=True),
                            e.matmul(po[64:128, :], ones_row[:1, 0:64],
                                     crow_p[:1, 512:1024], start=True, stop=True),
                        ][-1],
                        waits=(("ev", ev_rcp), ("sc", sc_oc)), inc="mm")
                    crow_d_gate[pr2] = mm_bc
                    ovp = o_sb[:, hp * 512:(hp + 1) * 512]
                    oa_done = s.emit(
                        "vector",
                        lambda e, ovp=ovp, tvp=tvp, po=po: e.tensor_mul(
                            ovp, tvp, po[:]),
                        waits=(("mm", mm_bc),), inc="ev")
                    po_free[pr2] = oa_done
                    tv_free[pr2] = oa_done
                wqh_used[hb] = wq_used
            o_done = oa_done
            for ic in range(DC):
                pr = ic % 2
                py = ps[2 + pr]
                def ymm(e, ic=ic, py=py):
                    last = None
                    for hp in range(8):
                        last = e.matmul(
                            py[:],
                            wo_v[:, hp * 1024 + ic * 128:
                                 hp * 1024 + (ic + 1) * 128],
                            o_sb[:, hp * 512:(hp + 1) * 512],
                            start=(hp == 0), stop=(hp == 7))
                    return last
                mm_y = s.emit("tensor", ymm,
                              waits=((d3ch, d3ch_t), (d3g, 16), ("ev", o_done),
                                     ("ev", py_gate[pr]),
                                     ("sc", pf_gate[0]), ("sc", pf_gate[1])),
                              inc="mm")
                tv = t_scr[:, pr * 512:(pr + 1) * 512]
                gv = g_sb[:, ic * T + w * TW: ic * T + (w + 1) * TW]
                av = a_sb[:, ic * T + w * TW: ic * T + (w + 1) * TW]
                py_gate[pr] = s.emit(
                    "vector",
                    lambda e, tv=tv, gv=gv, py=py, ic=ic: e.scalar_tensor_tensor(
                        tv, py[:], bo1c_sb[:, ic:ic + 1], gv, ALU.add, ALU.mult),
                    waits=(("mm", mm_y),), inc="ev")
                s.drain("vector")
                out_done = s.emit("vector",
                                  lambda e, av=av, tv=tv: e.tensor_add(av, av, tv),
                                  waits=((d3chx, d3chx_t),), inc="ev")
                tv_free[pr] = out_done
            ybank_gate = [py_gate[0], py_gate[1]]
        s.emit("sync", lambda e: e.dma_start(
            out=out_dr.rearrange("(c p) t -> p c t", p=128),
            in_=a_sb[:].rearrange("p (c t) -> p c t", c=DC)),
            waits=(("ev", out_done),), inc="st", dma=True)
        st_all = s.c("st")
        s.emit("sync", lambda e: e.nop(), waits=(("st", st_all),))

        s.finish()
    return nc



def _prep_inputs(inp):
    d = {}
    x = np.asarray(inp["L0_attn_x"])[0]
    x_fm = np.ascontiguousarray(x.T)
    d["x_slices"] = [np.ascontiguousarray(x_fm[:, c * T:(c + 1) * T])
                     for c in range(NCORES)]

    w_qkv = np.asarray(inp["L0_attn_w_qkv"])
    b_qkv = np.asarray(inp["L0_attn_b_qkv"])
    rows = (np.arange(128)[None, :] * 8 + np.arange(8)[:, None]).reshape(-1)
    for name, lo in (("q", 0), ("k", 1), ("v", 2)):
        w = w_qkv[lo * D:(lo + 1) * D][rows, :]
        d[f"w{name}T"] = np.ascontiguousarray(w.T)
        d[f"b{name}"] = np.ascontiguousarray(
            b_qkv[lo * D:(lo + 1) * D][rows])[None, :]

    c_ = np.ascontiguousarray
    d["w_oT"] = c_(np.asarray(inp["L0_attn_w_o"]).T)
    d["b_o"] = c_(np.asarray(inp["L0_attn_b_o"]))[None, :]
    d["alpha"] = np.asarray(inp["L0_ffn_alpha"], np.float32).reshape(1, 1)
    d["wgate"] = c_(np.asarray(inp["L0_ffn_w_gate"]))
    d["bgate"] = c_(np.asarray(inp["L0_ffn_b_gate"]))[None, :]
    d["W1m"] = c_(np.asarray(inp["L0_ffn_W1m"]))
    d["b1m"] = c_(np.asarray(inp["L0_ffn_b1m"]))[None, :]
    d["W2m"] = c_(np.asarray(inp["L0_ffn_W2m"]))
    d["b2m"] = c_(np.asarray(inp["L0_ffn_b2m"]))[None, :]
    d["W1e"] = c_(np.asarray(inp["L0_ffn_W1e"]).reshape(8 * D, 4096))
    d["b1e"] = c_(np.asarray(inp["L0_ffn_b1e"]))
    d["W2e"] = c_(np.asarray(inp["L0_ffn_W2e"]).reshape(8 * 4096, D))
    d["b2e"] = c_(np.asarray(inp["L0_ffn_b2e"]))
    for nm in ("q", "k", "v", "o"):
        d[f"w{nm}1T"] = c_(np.asarray(inp[f"L1_attn_W{nm}"]).T)
        d[f"b{nm}1"] = c_(np.asarray(inp[f"L1_attn_b{nm}"]))[None, :]
    d["wo1P"] = c_(d["wo1T"].reshape(8, 2, 64, D).transpose(1, 2, 0, 3).reshape(128, 8 * D))
    d["b1m_cols"] = c_(d["b1m"][0].reshape(32, 128).T)
    d["bq_cols"] = c_(d["bq"][0].reshape(8, 128).T)
    d["bo_cols"] = c_(d["b_o"][0].reshape(8, 128).T)
    d["bo1_cols"] = c_(d["bo1"][0].reshape(8, 128).T)
    d["bk1_cols"] = c_(d["bk1"][0].reshape(16, 64).T)
    d["bq1_cols"] = c_(d["bq1"][0].reshape(16, 64).T)
    d["PT"] = c_(np.asarray(inp["L1_attn_P"]).T)
    d["ident"] = np.eye(128, dtype=np.float32)
    d["router_w"] = c_(np.asarray(inp["L1_ffn_router_w"]))
    d["gate_w"] = c_(np.asarray(inp["L1_ffn_gate_w"]).reshape(16 * D, 512))
    d["up_w"] = c_(np.asarray(inp["L1_ffn_up_w"]).reshape(16 * D, 512))
    d["down_w"] = c_(np.asarray(inp["L1_ffn_down_w"]).reshape(16 * 512, D))
    d["sgate_w"] = c_(np.asarray(inp["L1_ffn_shared_gate_w"]))
    d["sup_w"] = c_(np.asarray(inp["L1_ffn_shared_up_w"]))
    d["sdown_w"] = c_(np.asarray(inp["L1_ffn_shared_down_w"]))
    return d


def kernel(**inputs):
    prep = _prep_inputs(inputs)
    cores = list(range(NCORES))
    nc = build_fused()
    shared = {
        "wkT": prep["wkT"], "wvT": prep["wvT"],
        "bk": prep["bk"], "bv": prep["bv"],
        "alpha": prep["alpha"], "wgate": prep["wgate"], "bgate": prep["bgate"],
        "W1m": prep["W1m"], "W2m": prep["W2m"], "b2m": prep["b2m"],
        "W1e": prep["W1e"], "b1e": prep["b1e"],
        "W2e": prep["W2e"], "b2e": prep["b2e"],
        "wqT": prep["wqT"], "woT": prep["w_oT"], "bo_cols": prep["bo_cols"],
        "wk1T": prep["wk1T"], "wv1T": prep["wv1T"], "bv1": prep["bv1"],
        "PT": prep["PT"], "ident": prep["ident"],
        "b1m_cols": prep["b1m_cols"], "bq_cols": prep["bq_cols"],
        "bk1_cols": prep["bk1_cols"], "bq1_cols": prep["bq1_cols"],
        "wq1T": prep["wq1T"], "wo1P": prep["wo1P"], "bo1_cols": prep["bo1_cols"],
        "router_w": prep["router_w"],
        "gate_w": prep["gate_w"], "up_w": prep["up_w"], "down_w": prep["down_w"],
        "sgate_w": prep["sgate_w"], "sup_w": prep["sup_w"],
        "sdown_w": prep["sdown_w"],
    }
    in_maps = [{**shared, "x_fm": prep["x_slices"][c]} for c in range(NCORES)]
    res = run_bass_kernel_spmd(nc, in_maps, core_ids=cores).results
    out = np.empty((1, NCORES * T, D), np.float32)
    for c in range(NCORES):
        out[0, c * T:(c + 1) * T, :] = res[c]["out_fm"].T
    return out


# version 11
# speedup vs baseline: 1.0292x; 1.0230x over previous
import numpy as np
import ml_dtypes
from contextlib import ExitStack

import concourse.bass as bass
import concourse.mybir as mybir
from concourse.bass import IndirectOffsetOnAxis
from concourse.bass_utils import run_bass_kernel_spmd

F32 = mybir.dt.float32
BF16 = mybir.dt.bfloat16
I32 = mybir.dt.int32
AF = mybir.ActivationFunctionType
ALU = mybir.AluOpType
AX = mybir.AxisListType

NCORES = 8
RG = [[0, 1, 2, 3, 4, 5, 6, 7]]
T = 2048
D = 1024
DC = 8
TW = 512
NW = T // TW
NT = T // 128
LN_EPS = 1e-5
FFN_H = 4096
HC = 512
NHC = FFN_H // HC
L1_NORM = 1.0 / (64.0 ** 0.25)
F_EXP = 512
F_SH = 1024



class Sched:

    ENGINES = ("sync", "gpsimd", "tensor", "vector", "scalar")
    SEMS = ("ld", "wt", "mm", "ev", "sc", "gp", "st")

    DEBUG = False

    def __init__(self, nc, stack):
        self.nc = nc
        self.stack = stack
        self.q = {e: [] for e in self.ENGINES}
        self.sems = {n: stack.enter_context(nc.semaphore(n)) for n in self.SEMS}
        self.cnt = {n: 0 for n in self.SEMS}
        self.floor = {e: {n: 0 for n in self.SEMS} for e in self.ENGINES}
        self.log = []

    def new_sem(self, name):
        self.sems[name] = self.stack.enter_context(self.nc.semaphore(name))
        self.cnt[name] = 0
        for e in self.ENGINES:
            self.floor[e][name] = 0
        return name

    def emit(self, eng, fn, waits=(), inc=None, dma=False, inc_n=None):
        n = inc_n if inc_n is not None else (16 if dma else 1)
        fl = self.floor[eng]
        real_waits = []
        for s_, v in waits:
            if v > fl[s_]:
                real_waits.append((s_, v))
                fl[s_] = v
        target = None
        if inc is not None:
            self.cnt[inc] += n
            target = self.cnt[inc]
            if not dma and inc_n is None:
                fl[inc] = max(fl[inc], target)
        if Sched.DEBUG:
            import sys as _s
            self.log.append((eng, _s._getframe(1).f_lineno,
                             tuple(real_waits), inc, target))
        sems = self.sems

        def cl(e, fn=fn, rw=tuple(real_waits), inc=inc, n=n):
            for s_, v in rw:
                e.wait_ge(sems[s_], v)
            ins = fn(e)
            if inc is not None:
                ins.then_inc(sems[inc], n)
        self.q[eng].append(cl)
        return target

    def drain(self, eng):
        self.q[eng].append(lambda e: e.drain())

    def c(self, name):
        return self.cnt[name]

    def finish(self):
        with self.nc.Block() as block:
            for engname in self.ENGINES:
                ops = self.q[engname]
                if not ops:
                    continue
                getattr(block, engname)(
                    lambda e, ops=ops: [f(e) for f in ops] and None
                )


def _alloc(stack, nc, name, shape, dtype):
    return stack.enter_context(nc.sbuf_tensor(name, shape, dtype))


def _psum(stack, nc, name, shape, dtype=F32):
    return stack.enter_context(nc.psum_tensor(name, shape, dtype))



def emit_layernorm(s, x_sb, xn_sb, aux, colsum_to=None):
    ps_sum = aux["ps_stat0"]
    ps_sq = aux["ps_stat1"]
    ps_m = aux["ps_bc0"]
    ps_r = aux["ps_bc1"]
    sq = aux["sq_scr"]
    rows = aux["rows"]
    ones_col = aux["ones_col"]
    ones_row = aux["ones_row"]
    x_ready = aux["x_ready"]
    pre = tuple(aux.get("pre_waits", ()))

    sq_gate = [0, 0]
    st = dict(stats_read=0,
              bc_read=0,
              sc_sd=0)
    ev_r = [0, 0]
    bc_done = [0, 0]

    def stage_a(w):
        p = w % 2
        last = None
        for c in range(DC):
            xin = x_sb[:, c * T + w * TW: c * T + (w + 1) * TW]
            sqv = sq[:, (c % 2) * TW: (c % 2 + 1) * TW]
            sc_done = s.emit(
                "scalar",
                lambda e, xin=xin, sqv=sqv: e.square(sqv, xin),
                waits=(*pre, x_ready, ("mm", sq_gate[c % 2])),
                inc="sc")
            s.emit(
                "tensor",
                lambda e, xin=xin, c=c: e.matmul(
                    ps_sum, ones_col, xin, start=(c == 0), stop=(c == DC - 1)),
                waits=(*pre, x_ready, ("ev", st["stats_read"])))
            mm_i = s.emit(
                "tensor",
                lambda e, sqv=sqv, c=c: e.matmul(
                    ps_sq, ones_col, sqv, start=(c == 0), stop=(c == DC - 1)),
                waits=(("sc", sc_done),),
                inc="mm")
            sq_gate[c % 2] = mm_i
            last = mm_i
        m_row = rows[:, p * TW: (p + 1) * TW]
        r_row = rows[:, (2 + p) * TW: (3 + p) * TW]
        t4 = rows[:, 4 * TW: 5 * TW]
        t5 = rows[:, 5 * TW: 6 * TW]
        s.emit("vector",
               lambda e, m_row=m_row: e.tensor_scalar_mul(m_row, ps_sum, 1.0 / D),
               waits=(("mm", last), ("mm", bc_done[p])))
        st["stats_read"] = s.emit(
            "vector", lambda e: e.tensor_scalar_mul(t4, ps_sq, 1.0 / D),
            waits=(("sc", st["sc_sd"]),), inc="ev")
        s.drain("vector")
        s.emit("vector", lambda e, m_row=m_row: e.tensor_mul(t5, m_row, m_row))
        s.drain("vector")
        ev_var = s.emit("vector", lambda e: e.tensor_sub(t4, t4, t5),
                        inc="ev")
        st["sc_sd"] = s.emit(
            "scalar",
            lambda e: e.activation(t4, t4, AF.Sqrt, bias=aux["eps_row"]),
            waits=(("ev", ev_var),), inc="sc")
        ev_r[p] = s.emit("vector",
                         lambda e, r_row=r_row: e.reciprocal(r_row, t4),
                         waits=(("sc", st["sc_sd"]),), inc="ev")

    def stage_b(w):
        p = w % 2
        m_row = rows[:, p * TW: (p + 1) * TW]
        r_row = rows[:, (2 + p) * TW: (3 + p) * TW]
        mm_bc = s.emit(
            "tensor",
            lambda e, m_row=m_row, r_row=r_row: [
                e.matmul(ps_m, ones_row[:1, 0:128], m_row, start=True, stop=True),
                e.matmul(ps_r, ones_row[:1, 0:128], r_row, start=True, stop=True),
            ][-1],
            waits=(("ev", ev_r[p]), ("ev", st["bc_read"])), inc="mm")
        bc_done[p] = mm_bc
        diff = aux["diff_scr"]
        ev_xn = 0

        def em_diff(c):
            xin = x_sb[:, c * T + w * TW: c * T + (w + 1) * TW]
            dv = diff[:, (c % 2) * TW: (c % 2 + 1) * TW]
            s.emit("vector",
                   lambda e, xin=xin, dv=dv: e.tensor_sub(dv, xin, ps_m),
                   waits=(("mm", mm_bc),))

        def em_xn(c):
            nonlocal ev_xn
            xno = xn_sb[:, c * T + w * TW: c * T + (w + 1) * TW]
            dv = diff[:, (c % 2) * TW: (c % 2 + 1) * TW]
            if colsum_to is not None:
                cs = colsum_to[:, c * NW + w: c * NW + w + 1]
                ev_xn = s.emit(
                    "vector",
                    lambda e, dv=dv, xno=xno, cs=cs: e.scalar_tensor_tensor(
                        xno, dv, 1.0, ps_r, ALU.mult, ALU.mult, accum_out=cs),
                    inc="ev")
            else:
                ev_xn = s.emit(
                    "vector",
                    lambda e, dv=dv, xno=xno: e.scalar_tensor_tensor(
                        xno, dv, 1.0, ps_r, ALU.mult, ALU.mult),
                    inc="ev")

        em_diff(0)
        em_diff(1)
        for c in range(DC):
            s.drain("vector")
            em_xn(c)
            if c + 2 < DC:
                em_diff(c + 2)
        st["bc_read"] = ev_xn
        aux["xn_done"][w] = ev_xn

    stage_a(0)
    for w in range(NW):
        if w + 1 < NW:
            stage_a(w + 1)
        stage_b(w)



def build_fused():
    nc = bass.Bass("TRN2", target_bir_lowering=False)
    dr = {}

    def din(name, shape, dtype=F32):
        dr[name] = nc.dram_tensor(name, shape, dtype, kind="ExternalInput").ap()
        return dr[name]

    x_dr = din("x_fm", [D, T])
    wkT_dr = din("wkT", [D, D])
    wvT_dr = din("wvT", [D, D])
    bk_dr = din("bk", [1, D])
    bv_dr = din("bv", [1, D])
    alpha_dr = din("alpha", [1, 1])
    wgate_dr = din("wgate", [D, 8])
    bgate_dr = din("bgate", [1, 8])
    w1m_dr = din("W1m", [D, FFN_H])
    w2m_dr = din("W2m", [FFN_H, D])
    b2m_dr = din("b2m", [1, D])
    w1e_dr = din("W1e", [8 * D, FFN_H])
    b1e_dr = din("b1e", [8, FFN_H])
    w2e_dr = din("W2e", [8 * FFN_H, D])
    b2e_dr = din("b2e", [8, D])
    wq_dr = din("wqT", [D, D])
    wo_dr = din("woT", [D, D])
    boc_dr = din("bo_cols", [128, 8])
    wk1_dr = din("wk1T", [D, D])
    wv1_dr = din("wv1T", [D, D])
    bv1_dr = din("bv1", [1, D])
    pt_dr = din("PT", [64, 64])
    id_dr = din("ident", [128, 128])
    b1mc_dr = din("b1m_cols", [128, 32])
    bqc_dr = din("bq_cols", [128, 8])
    bk1c_dr = din("bk1_cols", [64, 16])
    wq1_dr = din("wq1T", [D, D])
    woP_dr = din("wo1P", [128, 8 * D])
    bo1c_dr = din("bo1_cols", [128, 8])
    bq1c_dr = din("bq1_cols", [64, 16])
    rw_dr = din("router_w", [D, 16])
    gw_dr = din("gate_w", [16 * D, F_EXP])
    uw_dr = din("up_w", [16 * D, F_EXP])
    dw_dr = din("down_w", [16 * F_EXP, D])
    sg_dr = din("sgate_w", [D, F_SH])
    su_dr = din("sup_w", [D, F_SH])
    sd_dr = din("sdown_w", [F_SH, D])

    out_dr = nc.dram_tensor("out_fm", [D, T], F32, kind="ExternalOutput").ap()

    cc_cs_i = nc.dram_tensor("cc_cs_i", [128, 8], F32)
    cc_cs_o = nc.dram_tensor("cc_cs_o", [128, 8], F32)
    cc_m_i = nc.dram_tensor("cc_m_i", [128, 1024], F32)
    cc_m_o = nc.dram_tensor("cc_m_o", [128, 1024], F32)
    cc_kv_i = nc.dram_tensor("cc_kv_i", [64, 16 * 65], F32)
    cc_kv_o = nc.dram_tensor("cc_kv_o", [64, 16 * 65], F32)
    x1_dram = nc.dram_tensor("x1b", [D, T], F32)
    x1_dr = x1_dram.ap()

    with ExitStack() as stack:
        s = Sched(nc, stack)
        a_sb = _alloc(stack, nc, "a_sb", [128, DC * T], F32)
        xn_sb = _alloc(stack, nc, "xn_sb", [128, DC * T], BF16)
        g_sb = _alloc(stack, nc, "g_sb", [128, DC * T], BF16)
        w_sb = _alloc(stack, nc, "w_sb", [128, DC * T], BF16)
        gf = g_sb[:].bitcast(F32)
        m32_sb = _alloc(stack, nc, "m32_sb", [128, 8 * 128], F32)
        m_sb = _alloc(stack, nc, "m_sb", [128, 8 * 128], BF16)
        o_sb = _alloc(stack, nc, "o_sb", [128, 8 * 512], BF16)
        of32 = o_sb[:].bitcast(F32)
        sq_q = _alloc(stack, nc, "sq_q", [128, 2 * 512], BF16)
        t_scr = _alloc(stack, nc, "t_scr", [128, 2 * 512], F32)
        brow_sb = _alloc(stack, nc, "brow_sb", [1, 4096], F32)
        bv_row_sb = _alloc(stack, nc, "bv_row", [1, D], F32)
        kvb_sb = _alloc(stack, nc, "kvb_sb", [64, 16 * 65], BF16)
        cs_w = _alloc(stack, nc, "cs_w", [128, DC * NW], F32)
        cs8 = _alloc(stack, nc, "cs8", [128, 8], F32)
        wg_sb = _alloc(stack, nc, "wg_sb", [128, 8 * 8], F32)
        cols_sb = _alloc(stack, nc, "cols_sb", [128, 16], F32)
        idx_sb = _alloc(stack, nc, "idx_sb", [128, 8], I32)
        bidx_sb = _alloc(stack, nc, "bidx_sb", [1, 2], I32)
        iota_sb = _alloc(stack, nc, "iota_sb", [128, 1], F32)
        cvec_sb = _alloc(stack, nc, "cvec_sb", [128, 8], F32)
        rrow_sb = _alloc(stack, nc, "rrow_sb", [1, 64], F32)
        b1mc_sb = _alloc(stack, nc, "b1mc_sb", [128, 32], F32)
        b1c_sb = _alloc(stack, nc, "b1c_sb", [128, 8], F32)
        bqc_sb = _alloc(stack, nc, "bqc_sb", [128, 8], F32)
        boc_sb = _alloc(stack, nc, "boc_sb", [128, 8], F32)
        bo1c_sb = _alloc(stack, nc, "bo1c_sb", [128, 8], F32)
        bk1c_sb = _alloc(stack, nc, "bk1c_sb", [64, 16], F32)
        bq1c_sb = _alloc(stack, nc, "bq1c_sb", [64, 16], F32)
        eps_ln = _alloc(stack, nc, "eps_ln", [1, 1], F32)
        eps_r6 = _alloc(stack, nc, "eps_r6", [1, 1], F32)
        eps_col = _alloc(stack, nc, "eps_col", [128, 1], F32)
        ones_row = _alloc(stack, nc, "ones_row", [1, TW], F32)
        ones_col = _alloc(stack, nc, "ones_col", [128, 1], F32)
        onesb_col = _alloc(stack, nc, "onesb_col", [128, 1], BF16)
        ptb_sb = _alloc(stack, nc, "ptb_sb", [64, 64], BF16)
        id_sb = _alloc(stack, nc, "id_sb", [128, 128], F32)
        ct_sb = _alloc(stack, nc, "ct_sb", [128, 16 * 16], F32)
        pr_sb = _alloc(stack, nc, "pr_sb", [128, 16], F32)
        rcol_sb = _alloc(stack, nc, "rcol_sb", [128, 4], F32)
        rw_sb = _alloc(stack, nc, "rw_sb", [128, 8 * 16], BF16)

        b1st_sb = gf[0:2, 4096:5120]
        b2st_sb = gf[0:2, 5120:6144]
        kvf_sb = gf[0:64, 7152:8192]
        crow_sb = brow_sb[:1, 2048:3072]

        ps = [_psum(stack, nc, f"ps{i}", [128, 512], F32) for i in range(8)]

        pre = s.new_sem("pre")
        preg = s.new_sem("preg")
        c1i = s.new_sem("c1i"); c1c = s.new_sem("c1c"); c1o = s.new_sem("c1o")
        c2i = s.new_sem("c2i"); c2c = s.new_sem("c2c"); c2o = s.new_sem("c2o")
        c3i = s.new_sem("c3i"); c3c = s.new_sem("c3c"); c3o = s.new_sem("c3o")
        w1s = [s.new_sem("w1A"), s.new_sem("w1B")]
        w2s = [s.new_sem("w2A"), s.new_sem("w2B")]
        b2ch = s.new_sem("b2ch")
        b2g = s.new_sem("b2g")
        dch = s.new_sem("dch")
        dchx = s.new_sem("dchx")
        dg = s.new_sem("dg")
        fch = s.new_sem("fch")
        fg = s.new_sem("fg")
        slch = s.new_sem("slch")
        d3ch = s.new_sem("d3ch")
        d3chx = s.new_sem("d3chx")
        d3g = s.new_sem("d3g")

        init = s.emit("vector", lambda e: [
            e.memset(ones_row[:], 1.0),
            e.memset(ones_col[:], 1.0),
            e.memset(onesb_col[:], 1.0),
            e.memset(eps_ln[:], LN_EPS),
            e.memset(eps_r6[:], 1e-6),
            e.memset(eps_col[:], 1e-6),
        ][-1], inc="ev")
        gp_iota = s.emit("gpsimd", lambda e: [
            e.iota(iota_sb[:, 0:1], [[0, 1]], channel_multiplier=1,
                   allow_small_or_imprecise_dtypes=True),
            e.iota(rrow_sb[:1, 48:56], [[1, 8]], channel_multiplier=0,
                   allow_small_or_imprecise_dtypes=True),
        ][-1], inc="gp")

        s.emit("sync", lambda e: e.dma_start(
            out=a_sb[:].rearrange("p (c t) -> p c t", c=DC),
            in_=x_dr.rearrange("(c p) t -> p c t", p=128)), inc=pre, dma=True)
        s.emit("gpsimd", lambda e: e.dma_start(
            out=w_sb[:, 0:8192].rearrange("p (c j) -> p c j", c=DC),
            in_=wkT_dr.rearrange("(c p) j -> p c j", p=128)), inc=preg, dma=True)
        s.emit("gpsimd", lambda e: e.dma_start(
            out=w_sb[:, 8192:16384].rearrange("p (c j) -> p c j", c=DC),
            in_=wvT_dr.rearrange("(c p) j -> p c j", p=128)), inc=preg, dma=True)
        s.emit("sync", lambda e: e.dma_start(out=brow_sb[:1, 3072:4096], in_=bk_dr),
               inc=pre, dma=True)
        s.emit("sync", lambda e: e.dma_start(out=bv_row_sb[:], in_=bv_dr),
               inc=pre, dma=True)
        s.emit("sync", lambda e: e.dma_start(
            out=wg_sb[:].rearrange("p (c g) -> p c g", c=DC),
            in_=wgate_dr.rearrange("(c p) g -> p c g", p=128)), inc=pre, dma=True)
        s.emit("sync", lambda e: e.dma_start(out=rrow_sb[:1, 32:40], in_=bgate_dr),
               inc=pre, dma=True)
        s.emit("sync", lambda e: e.dma_start(out=rrow_sb[:1, 40:41], in_=alpha_dr),
               inc=pre, dma=True)
        s.emit("sync", lambda e: e.dma_start(out=b1mc_sb[:], in_=b1mc_dr),
               inc=pre, dma=True)
        s.emit("sync", lambda e: e.dma_start(out=bqc_sb[:], in_=bqc_dr),
               inc=pre, dma=True)
        s.emit("sync", lambda e: e.dma_start(out=boc_sb[:], in_=boc_dr),
               inc=pre, dma=True)
        s.emit("sync", lambda e: e.dma_start(out=bo1c_sb[:], in_=bo1c_dr),
               inc=pre, dma=True)
        s.emit("sync", lambda e: e.dma_start(out=bk1c_sb[:], in_=bk1c_dr),
               inc=pre, dma=True)
        s.emit("sync", lambda e: e.dma_start(out=bq1c_sb[:], in_=bq1c_dr),
               inc=pre, dma=True)
        s.emit("sync", lambda e: e.dma_start(out=id_sb[:], in_=id_dr),
               inc=pre, dma=True)
        s.emit("gpsimd", lambda e: e.dma_start(out=ptb_sb[:], in_=pt_dr),
               inc=preg, dma=True)
        s.emit("gpsimd", lambda e: e.dma_start(
            out=rw_sb[:].rearrange("p (c g) -> p c g", c=DC),
            in_=rw_dr.rearrange("(c p) g -> p c g", p=128)), inc=preg, dma=True)
        pre_t = s.c(pre)
        preg_t = s.c(preg)

        aux0 = dict(
            eps_row=eps_ln[:1, :1],
            ps_stat0=ps[0][0:1, :], ps_stat1=ps[1][0:1, :],
            ps_bc0=ps[2][:], ps_bc1=ps[3][:],
            sq_scr=of32[:, 0:1024], diff_scr=of32[:, 1024:2048],
            rows=brow_sb[:1, 0:3072],
            ones_col=ones_col[:], ones_row=ones_row,
            x_ready=(pre, pre_t),
            pre_waits=(("ev", init),),
            xn_done=[0] * NW,
        )
        emit_layernorm(s, a_sb, xn_sb, aux0, colsum_to=cs_w)
        ln0_done = aux0["xn_done"]

        ev_cs = 0
        for c in range(DC):
            ev_cs = s.emit("vector", lambda e, c=c: e.reduce_sum(
                cs8[:, c:c + 1],
                cs_w[:].rearrange("p (c w) -> p c w", c=DC)[:, c, :],
                axis=AX.X), inc="ev" if c == DC - 1 else None)
        s.emit("sync", lambda e: e.dma_start(out=cc_cs_i[:, :], in_=cs8[:]),
               waits=(("ev", ev_cs),), inc=c1i, dma=True)
        s.emit("gpsimd", lambda e: e.collective_compute(
            "AllReduce", ALU.add, replica_groups=RG,
            ins=[cc_cs_i.ap().opt()], outs=[cc_cs_o.ap().opt()]),
            waits=((c1i, 16),), inc=c1c, dma=True, inc_n=1)
        s.emit("sync", lambda e: e.dma_start(out=cs8[:], in_=cc_cs_o[:, :]),
               waits=((c1c, 1),), inc=c1o, dma=True)
        c1o_t = s.c(c1o)

        kg_sb = g_sb[:, 0:8192]
        vg_sb = g_sb[:, 8192:16384]
        wk_v = w_sb[:, 0:8192]
        wv_v = w_sb[:, 8192:16384]
        bk_row = brow_sb[:1, 3072:4096]
        evk_gate = [0, 0]
        scv_last = 0
        kv_used = 0
        ev_m = 0
        m_ps_free = 0
        mm_p1w = 0
        ev_kg_last = 0
        scv_p1_last = 0
        for g in range(2):
            ev_k = sc_v = 0
            for tt in range(NT):
                slot = tt % 2
                kg = kg_sb[:, tt * 512:(tt + 1) * 512]
                vg = vg_sb[:, tt * 512:(tt + 1) * 512]
                kgp = ps[4 + slot][:, 0:512]
                vgp = ps[6][:, 0:512]
                w = tt // (NT // NW)
                xw = ("ev", ln0_done[w])

                def proj(e, g=g, tt=tt, kgp=kgp, vgp=vgp):
                    for c in range(DC):
                        xsl = xn_sb[:, c * T + tt * 128: c * T + (tt + 1) * 128]
                        wks = wk_v[:, c * D + g * 512: c * D + (g + 1) * 512]
                        wvs = wv_v[:, c * D + g * 512: c * D + (g + 1) * 512]
                        e.matmul(kgp, xsl, wks, start=(c == 0), stop=False)
                        e.matmul(vgp, xsl, wvs, start=(c == 0), stop=False)
                    e.matmul(kgp, ones_row[:1, 0:128],
                             bk_row[:1, g * 512:(g + 1) * 512], start=False, stop=True)
                    return e.matmul(vgp, ones_row[:1, 0:128],
                                    bv_row_sb[:1, g * 512:(g + 1) * 512],
                                    start=False, stop=True)
                mm_proj = s.emit(
                    "tensor", proj,
                    waits=(xw, (pre, pre_t), (preg, preg_t),
                           ("ev", evk_gate[slot]), ("sc", scv_last),
                           ("mm", kv_used)),
                    inc="mm")
                mm_p1w = mm_proj
                ev_k = s.emit("vector",
                              lambda e, kg=kg, kgp=kgp: e.tensor_copy(kg, kgp),
                              waits=(("mm", mm_proj),), inc="ev")
                sc_v = s.emit("scalar",
                              lambda e, vg=vg, vgp=vgp: e.copy(vg, vgp),
                              waits=(("mm", mm_proj),), inc="sc")
                evk_gate[slot] = ev_k
                scv_last = sc_v
            ev_kg_last = ev_k
            scv_p1_last = sc_v
            mm_m = 0
            for hh in range(4):
                def macc(e, hh=hh):
                    last = None
                    for tt in range(NT):
                        last = e.matmul(
                            ps[7][:, hh * 128:(hh + 1) * 128],
                            kg_sb[:, tt * 512 + hh * 128: tt * 512 + (hh + 1) * 128],
                            vg_sb[:, tt * 512 + hh * 128: tt * 512 + (hh + 1) * 128],
                            start=(tt == 0), stop=(tt == NT - 1))
                    return last
                mm_m = s.emit("tensor", macc,
                              waits=(("ev", ev_k), ("sc", sc_v),
                                     ("ev", m_ps_free)), inc="mm")
            for hh in range(4):
                h = g * 4 + hh
                ev_m = s.emit("vector", lambda e, h=h, hh=hh: e.tensor_copy(
                    m32_sb[:, h * 128:(h + 1) * 128], ps[7][:, hh * 128:(hh + 1) * 128]),
                    waits=(("mm", mm_m),), inc="ev" if hh == 3 else None)
            kv_used = mm_m
            m_ps_free = ev_m
        mm_p1g = kv_used

        s.emit("sync", lambda e: e.dma_start(out=cc_m_i[:, :], in_=m32_sb[:]),
               waits=(("ev", ev_m),), inc=c2i, dma=True)

        def emit_c2_collective():
            s.emit("gpsimd", lambda e: e.collective_compute(
                "AllReduce", ALU.add, replica_groups=RG,
                ins=[cc_m_i.ap().opt()], outs=[cc_m_o.ap().opt()]),
                waits=((c2i, 16),), inc=c2c, dma=True, inc_n=1)
        s.emit("sync", lambda e: e.dma_start(out=m32_sb[:], in_=cc_m_o[:, :]),
               waits=((c2c, 1),), inc=c2o, dma=True)
        c2o_t = s.c(c2o)

        R = rrow_sb
        lg = R[:1, 0:8]
        scr = R[:1, 8:16]
        probs = R[:1, 16:24]
        scr2 = R[:1, 24:32]
        bgate = R[:1, 32:40]
        iota8 = R[:1, 48:56]
        mm_lg = s.emit("tensor", lambda e: [
            e.matmul(ps[0][0:1, 0:8], cs8[:, c:c + 1], wg_sb[:, c * 8:(c + 1) * 8],
                     start=(c == 0), stop=(c == DC - 1))
            for c in range(DC)][-1],
            waits=((c1o, c1o_t), (pre, pre_t), ("ev", ev_cs)), inc="mm")

        def rt(e):
            e.tensor_scalar(lg, ps[0][0:1, 0:8], 1.0 / 16384.0, None, ALU.mult)
            e.drain()
            e.tensor_add(lg, lg, bgate)
            e.drain()
            e.reduce_max(R[:1, 41:42], lg, axis=AX.X)
            e.drain()
            return e.tensor_scalar(R[:1, 42:43], R[:1, 41:42], -1.0, None, ALU.mult)
        ev_rt1 = s.emit("vector", rt, waits=(("mm", mm_lg), (pre, pre_t),
                                             ("ev", init)), inc="ev")
        sc_exp = s.emit(
            "scalar",
            lambda e: e.activation(scr, lg, AF.Exp, bias=R[:1, 42:43]),
            waits=(("ev", ev_rt1),), inc="sc")

        def rt2(e):
            e.reduce_sum(R[:1, 44:45], scr, axis=AX.X)
            e.drain()
            e.reciprocal(R[:1, 45:46], R[:1, 44:45])
            e.drain()
            e.tensor_scalar(probs, scr, R[:1, 45:46], None, ALU.mult)
            e.drain()
            e.reduce_max(R[:1, 56:57], probs, axis=AX.X)
            e.drain()
            e.tensor_scalar(scr2, probs, R[:1, 56:57], None, ALU.is_ge)
            e.drain()
            e.tensor_scalar(scr, scr2, -9.0, 9.0, ALU.mult, ALU.add)
            e.drain()
            e.tensor_add(scr, scr, iota8)
            e.drain()
            e.tensor_reduce(R[:1, 57:58], scr, AX.X, ALU.min)
            e.drain()
            e.scalar_tensor_tensor(scr, scr2, -2.0, probs, ALU.mult, ALU.add)
            e.drain()
            e.reduce_max(R[:1, 58:59], scr, axis=AX.X)
            e.drain()
            e.tensor_scalar(scr2, scr, R[:1, 58:59], None, ALU.is_ge)
            e.drain()
            e.tensor_scalar(scr2, scr2, -9.0, 9.0, ALU.mult, ALU.add)
            e.drain()
            e.tensor_add(scr2, scr2, iota8)
            e.drain()
            e.tensor_reduce(R[:1, 59:60], scr2, AX.X, ALU.min)
            e.drain()
            e.tensor_add(R[:1, 60:61], R[:1, 56:57], R[:1, 58:59])
            e.drain()
            e.reciprocal(R[:1, 61:62], R[:1, 60:61])
            e.drain()
            e.tensor_mul(R[:1, 62:63], R[:1, 56:57], R[:1, 61:62])
            e.tensor_mul(R[:1, 63:64], R[:1, 58:59], R[:1, 61:62])
            e.drain()
            e.tensor_scalar(R[:1, 8:9], R[:1, 40:41], 0.1, 1.0, ALU.max, ALU.min)
            e.drain()
            e.tensor_scalar(R[:1, 24:25], R[:1, 8:9], -1.0, 1.0, ALU.mult, ALU.add)
            e.drain()
            e.tensor_mul(R[:1, 9:10], R[:1, 24:25], R[:1, 62:63])
            e.tensor_mul(R[:1, 10:11], R[:1, 24:25], R[:1, 63:64])
            e.tensor_scalar(R[:1, 11:12], R[:1, 57:58], 1024.0, None, ALU.mult)
            e.tensor_scalar(R[:1, 12:13], R[:1, 59:60], 1024.0, None, ALU.mult)
            e.tensor_scalar(R[:1, 13:14], R[:1, 57:58], 4096.0, None, ALU.mult)
            e.tensor_scalar(R[:1, 14:15], R[:1, 59:60], 4096.0, None, ALU.mult)
            e.tensor_copy(R[:1, 15:16], R[:1, 57:58])
            e.tensor_copy(R[:1, 16:17], R[:1, 59:60])
            e.drain()
            e.tensor_copy(bidx_sb[:1, 0:1], R[:1, 57:58])
            return e.tensor_copy(bidx_sb[:1, 1:2], R[:1, 59:60])
        ev_rt2 = s.emit("vector", rt2, waits=(("sc", sc_exp), ("gp", gp_iota)),
                        inc="ev")
        mm_bcr = s.emit("tensor", lambda e: e.matmul(
            ps[2][:, 0:9], ones_row[:1, 0:128], R[:1, 8:17], start=True, stop=True),
            waits=(("ev", ev_rt2),), inc="mm")

        def mkcols(e):
            e.tensor_copy(cols_sb[:, 0:9], ps[2][:, 0:9])
            e.drain()
            e.tensor_scalar(cols_sb[:, 9:13], cols_sb[:, 3:7], iota_sb[:, 0:1],
                            None, ALU.add)
            e.drain()
            e.tensor_copy(idx_sb[:, 0:4], cols_sb[:, 9:13])
            return e.tensor_copy(idx_sb[:, 4:6], cols_sb[:, 7:9])
        routing_done = s.emit("vector", mkcols,
                              waits=(("mm", mm_bcr), ("gp", gp_iota)), inc="ev")
        a_col = cols_sb[:, 0:1]
        c1_col = cols_sb[:, 1:2]
        c2_col = cols_sb[:, 2:3]

        s.emit("sync", lambda e: e.dma_start(
            out=brow_sb[:1, D:2 * D], in_=b2m_dr),
            waits=(("mm", mm_bcr),), inc=b2ch, dma=True)
        s.emit("gpsimd", lambda e: e.indirect_dma_start(
            out=b2st_sb[0:2, :], out_offset=None, in_=b2e_dr,
            in_offset=IndirectOffsetOnAxis(idx_sb[0:2, 4:5], 0)),
            waits=(("ev", routing_done), ("mm", mm_p1g)), inc=b2g, dma=True)
        wt_b2a = s.c(b2g)
        ev_b2a = s.emit("vector", lambda e: e.tensor_copy(
            brow_sb[:1, 2 * D:3 * D], b2st_sb[0:1, :]),
            waits=((b2g, wt_b2a), ("mm", mm_bcr)), inc="ev")
        s.emit("gpsimd", lambda e: e.indirect_dma_start(
            out=b2st_sb[0:2, :], out_offset=None, in_=b2e_dr,
            in_offset=IndirectOffsetOnAxis(idx_sb[0:2, 5:6], 0)),
            waits=(("ev", ev_b2a),), inc=b2g, dma=True)
        wt_b2b = s.c(b2g)
        ev_b2b = s.emit("vector", lambda e: e.tensor_copy(
            brow_sb[:1, 3 * D:4 * D], b2st_sb[0:1, :]),
            waits=((b2g, wt_b2b), (b2ch, s.c(b2ch)), ("mm", mm_p1w)), inc="ev")

        def cvec(e):
            cv = brow_sb[:1, 0:D]
            tv = brow_sb[:1, D:2 * D]
            e.tensor_scalar(cv, brow_sb[:1, D:2 * D], R[:1, 8:9], None, ALU.mult)
            e.drain()
            e.tensor_scalar(tv, brow_sb[:1, 2 * D:3 * D], R[:1, 9:10], None, ALU.mult)
            e.drain()
            e.tensor_add(cv, cv, tv)
            e.drain()
            e.tensor_scalar(tv, brow_sb[:1, 3 * D:4 * D], R[:1, 10:11], None, ALU.mult)
            e.drain()
            return e.tensor_add(cv, cv, tv)
        ev_cvec = s.emit("vector", cvec, waits=(("ev", ev_b2b),), inc="ev")

        slab = 0
        w1_freed = [mm_p1w, mm_p1w]
        w2_freed = [mm_p1w, mm_p1w]
        b1_freed = [mm_p1g, mm_p1g]
        hid_ps_gate = [0, 0]
        b1c_gate = [scv_p1_last, scv_p1_last]
        o_ps_gate = [0, 0]
        hid_used = mm_p1g
        sc_gelu = 0
        ev_ffn = 0
        first_hid = [True, True]
        first_out = [True, True]
        acc_last = {}
        hid_w_last = {}

        for p in range(3):
            sc_col = (a_col, c1_col, c2_col)[p]
            for hc in range(NHC):
                sl = slab % 2
                w1v = w_sb[:, sl * 4096:(sl + 1) * 4096]
                w2v = w_sb[:, 8192 + sl * 4096: 8192 + (sl + 1) * 4096]
                b1v = b1st_sb[0:1, sl * HC:(sl + 1) * HC]
                if p == 0:
                    s.emit("gpsimd", lambda e, hc=hc, w1v=w1v: e.dma_start(
                        out=w1v.rearrange("p (c h) -> p c h", c=DC),
                        in_=w1m_dr.rearrange("(c p) h -> p c h", p=128)
                        [:, :, hc * HC:(hc + 1) * HC]),
                        waits=(("mm", w1_freed[sl]),), inc=w1s[sl], dma=True)
                    wt1 = (w1s[sl], s.c(w1s[sl]))
                    s.emit("gpsimd", lambda e, hc=hc, w2v=w2v: e.dma_start(
                        out=w2v.rearrange("p (q f) -> p q f", q=4),
                        in_=w2m_dr.rearrange("(q p) f -> p q f", p=128)
                        [:, hc * 4:(hc + 1) * 4, :]),
                        waits=(("mm", w2_freed[sl]),), inc=w2s[sl], dma=True)
                    wt2 = (w2s[sl], s.c(w2s[sl]))
                    bt = (pre, pre_t)
                    if hc == 0:
                        emit_c2_collective()
                else:
                    iw1 = idx_sb[:, (p - 1):p]
                    iw2 = idx_sb[:, (1 + p):(2 + p)]
                    for c in range(DC):
                        s.emit("gpsimd", lambda e, c=c, hc=hc, w1v=w1v, iw1=iw1: e.indirect_dma_start(
                            out=w1v.rearrange("p (c h) -> p c h", c=DC)[:, c, :],
                            out_offset=None, in_=w1e_dr,
                            in_offset=IndirectOffsetOnAxis(iw1, 0),
                            element_offset=(c * 128) * FFN_H + hc * HC),
                            waits=(("mm", w1_freed[sl]), ("ev", routing_done)),
                            inc=w1s[sl], dma=True)
                    s.emit("gpsimd", lambda e, p=p, hc=hc, sl=sl: e.indirect_dma_start(
                        out=b1st_sb[0:2, sl * HC:(sl + 1) * HC], out_offset=None,
                        in_=b1e_dr,
                        in_offset=IndirectOffsetOnAxis(idx_sb[0:2, (3 + p):(4 + p)], 0),
                        element_offset=hc * HC),
                        waits=(("ev", routing_done), ("mm", b1_freed[sl])),
                        inc=w1s[sl], dma=True)
                    wt1 = (w1s[sl], s.c(w1s[sl]))
                    bt = wt1
                    for qq in range(4):
                        s.emit("gpsimd", lambda e, qq=qq, hc=hc, w2v=w2v, iw2=iw2: e.indirect_dma_start(
                            out=w2v.rearrange("p (q f) -> p q f", q=4)[:, qq, :],
                            out_offset=None, in_=w2e_dr,
                            in_offset=IndirectOffsetOnAxis(iw2, 0),
                            element_offset=(hc * 4 + qq) * 128 * D),
                            waits=(("mm", w2_freed[sl]), ("ev", routing_done)),
                            inc=w2s[sl], dma=True)
                    wt2 = (w2s[sl], s.c(w2s[sl]))
                if p == 0:
                    b1cols = b1mc_sb[:, hc * 4: hc * 4 + 4]
                    b1_ready = bt
                else:
                    mm_b1t = s.emit("tensor", lambda e, b1v=b1v: [
                        e.transpose(ps[6][:, 8 + it: 9 + it],
                                    b1v[:1, it * 128:(it + 1) * 128], ones_row[:1, 0:1])
                        for it in range(4)][-1],
                        waits=(bt, (pre, pre_t), ("sc", b1c_gate[sl])),
                        inc="mm")
                    ev_b1c = s.emit("vector", lambda e, sl=sl: e.tensor_copy(
                        b1c_sb[:, sl * 4:(sl + 1) * 4], ps[6][:, 8:12]),
                        waits=(("mm", mm_b1t),), inc="ev")
                    b1cols = b1c_sb[:, sl * 4: sl * 4 + 4]
                    b1_ready = ("ev", ev_b1c)
                for q in range(4):
                    for w in range(NW):
                        pr = (q * NW + w) % 2
                        pi = ps[pr]
                        def hmm(e, q=q, w=w, pi=pi, w1v=w1v):
                            last = None
                            for c in range(DC):
                                last = e.matmul(
                                    pi[:],
                                    w1v[:, c * HC + q * 128: c * HC + (q + 1) * 128],
                                    xn_sb[:, c * T + w * TW: c * T + (w + 1) * TW],
                                    start=(c == 0), stop=(c == DC - 1))
                            return last
                        extra = ()
                        if first_hid[pr]:
                            extra = (("ev", routing_done),)
                            first_hid[pr] = False
                        mm_h = s.emit("tensor", hmm,
                                      waits=(wt1, ("ev", ln0_done[w]),
                                             ("sc", hid_ps_gate[pr]), *extra),
                                      inc="mm")
                        hv = g_sb[:, q * T + w * TW: q * T + (w + 1) * TW]
                        sc_gelu = s.emit(
                            "scalar",
                            lambda e, hv=hv, pi=pi, b1cols=b1cols, q=q:
                            e.activation(hv, pi[:], AF.Gelu_apprx_tanh,
                                         bias=b1cols[:, q:q + 1]),
                            waits=(("mm", mm_h), ("mm", hid_used), b1_ready),
                            inc="sc")
                        hid_ps_gate[pr] = sc_gelu
                if p != 0:
                    b1c_gate[sl] = sc_gelu
                mm_o_last = 0
                for fc in range(DC):
                    for w in range(NW):
                        pr = (fc * NW + w) % 2
                        po = ps[2 + pr]
                        def omm(e, fc=fc, w=w, po=po, w2v=w2v):
                            last = None
                            for q in range(4):
                                last = e.matmul(
                                    po[:],
                                    w2v[:, q * 1024 + fc * 128: q * 1024 + (fc + 1) * 128],
                                    g_sb[:, q * T + w * TW: q * T + (w + 1) * TW],
                                    start=(q == 0), stop=(q == 3))
                            return last
                        extra = ()
                        if first_out[pr]:
                            extra = (("ev", routing_done),)
                            first_out[pr] = False
                        mm_o = s.emit("tensor", omm,
                                      waits=(wt2, ("sc", sc_gelu),
                                             ("ev", o_ps_gate[pr]), *extra),
                                      inc="mm")
                        av = a_sb[:, fc * T + w * TW: fc * T + (w + 1) * TW]
                        if p == 0 and hc == 0:
                            ev_ffn = s.emit(
                                "vector",
                                lambda e, av=av, po=po: e.tensor_scalar(
                                    av, po[:], a_col, None, ALU.mult),
                                waits=(("mm", mm_o), ("ev", routing_done)),
                                inc="ev")
                        else:
                            ev_ffn = s.emit(
                                "vector",
                                lambda e, av=av, po=po, sc_col=sc_col: e.scalar_tensor_tensor(
                                    av, po[:], sc_col, av, ALU.mult, ALU.add),
                                waits=(("mm", mm_o),), inc="ev")
                        o_ps_gate[pr] = ev_ffn
                        mm_o_last = mm_o
                        if p == 2 and hc == NHC - 1:
                            acc_last[(fc, w)] = ev_ffn
                            if fc == DC - 1:
                                hid_w_last[w] = mm_o
                w1_freed[sl] = s.c("mm")
                w2_freed[sl] = s.c("mm")
                b1_freed[sl] = s.c("mm")
                hid_used = mm_o_last
                slab += 1
        ffn_done_ev = s.c("ev")

        mm_cvt = s.emit("tensor", lambda e: [
            e.transpose(ps[6][:, fc:fc + 1], brow_sb[:1, fc * 128:(fc + 1) * 128],
                        ones_row[:1, 0:1])
            for fc in range(DC)][-1],
            waits=(("ev", ev_cvec), ("sc", scv_p1_last), (pre, pre_t)), inc="mm")
        ev_cvc = s.emit("vector",
                        lambda e: e.tensor_copy(cvec_sb[:, 0:8], ps[6][:, 0:8]),
                        waits=(("mm", mm_cvt),), inc="ev")
        sil_done = 0
        for fc in range(DC):
            for w in range(NW):
                gv = g_sb[:, fc * T + w * TW: fc * T + (w + 1) * TW]
                av = a_sb[:, fc * T + w * TW: fc * T + (w + 1) * TW]
                sil_done = s.emit(
                    "scalar",
                    lambda e, gv=gv, av=av, fc=fc: e.activation(
                        gv, av, AF.Silu, bias=cvec_sb[:, fc:fc + 1]),
                    waits=(("ev", ev_cvc), ("ev", acc_last[(fc, w)]),
                           ("mm", hid_w_last[w] if fc < 4 else mm_p1g)),
                    inc="sc")

        dgq = s.new_sem("dgq")
        for hh in range(2):
            s.emit("gpsimd", lambda e, hh=hh: e.dma_start(
                out=w_sb[:, hh * 4096:(hh + 1) * 4096].rearrange(
                    "p (c j) -> p c j", c=4),
                in_=wq_dr[hh * 512:(hh + 1) * 512, :].rearrange(
                    "(c p) j -> p c j", p=128)),
                waits=(("mm", w1_freed[hh]),), inc=dgq, dma=True)
            s.emit("gpsimd", lambda e, hh=hh: e.dma_start(
                out=w_sb[:, 8192 + hh * 4096: 8192 + (hh + 1) * 4096].rearrange(
                    "p (c j) -> p c j", c=4),
                in_=wo_dr[hh * 512:(hh + 1) * 512, :].rearrange(
                    "(c p) j -> p c j", p=128)),
                waits=(("mm", w2_freed[hh]),), inc=dg, dma=True)
        dgq_t = s.c(dgq)
        s.emit("sync", lambda e: e.dma_start(
            out=a_sb[:].rearrange("p (c t) -> p c t", c=DC),
            in_=x_dr.rearrange("(c p) t -> p c t", p=128)),
            waits=(("sc", sil_done),), inc=dchx, dma=True)
        dch_t = s.c(dch)
        dchx_t = s.c(dchx)
        dg_t = s.c(dg)

        ev_mbf = s.emit("vector", lambda e: e.tensor_copy(m_sb[:], m32_sb[:]),
                        waits=((c2o, c2o_t),), inc="ev")

        x1_done = 0
        o_used = 0
        q_used = [0, 0]
        y_evac = [ev_kg_last, ev_kg_last]
        for w in range(NW):
            sc_o = 0
            for h in range(8):
                pq = ps[h % 2]
                def qmm(e, h=h, w=w, pq=pq):
                    last = None
                    for c in range(DC):
                        last = e.matmul(
                            pq[:],
                            w_sb[:, c * 1024 + h * 128: c * 1024 + (h + 1) * 128],
                            xn_sb[:, c * T + w * TW: c * T + (w + 1) * TW],
                            start=(c == 0), stop=(c == DC - 1))
                    return last
                mm_q = s.emit("tensor", qmm,
                              waits=((dch, dch_t), (dgq, dgq_t),
                                     ("sc", sil_done),
                                     ("mm", q_used[h % 2])),
                              inc="mm")
                qv = sq_q[:, (h % 2) * 512:(h % 2 + 1) * 512]
                ev_q = s.emit("scalar",
                              lambda e, qv=qv, pq=pq, h=h: e.activation(
                                  qv, pq[:], AF.Identity, bias=bqc_sb[:, h:h + 1]),
                              waits=(("mm", mm_q),), inc="sc")
                po = ps[2 + h % 2]
                mm_oh = s.emit("tensor",
                               lambda e, h=h, qv=qv, po=po: e.matmul(
                                   po[:], m_sb[:, h * 128:(h + 1) * 128], qv,
                                   start=True, stop=True),
                               waits=(("sc", ev_q), ("ev", ev_mbf),
                                      ("sc", sc_o), ("mm", o_used)), inc="mm")
                q_used[h % 2] = mm_oh
                ov = o_sb[:, h * 512:(h + 1) * 512]
                sc_o = s.emit("scalar",
                              lambda e, ov=ov, po=po: e.copy(ov, po[:]),
                              waits=(("mm", mm_oh),), inc="sc")
            for ic in range(DC):
                pr = ic % 2
                py = ps[4 + pr]
                def ymm(e, ic=ic, py=py):
                    last = None
                    for cc in range(DC):
                        last = e.matmul(
                            py[:],
                            w_sb[:, 8192 + cc * 1024 + ic * 128:
                                 8192 + cc * 1024 + (ic + 1) * 128],
                            o_sb[:, cc * 512:(cc + 1) * 512],
                            start=(cc == 0), stop=(cc == DC - 1))
                    return last
                mm_y = s.emit("tensor", ymm,
                              waits=((dch, dch_t), (dg, dg_t), ("sc", sc_o),
                                     ("ev", y_evac[pr])), inc="mm")
                o_used = mm_y
                tv = t_scr[:, pr * 512:(pr + 1) * 512]
                gv = g_sb[:, ic * T + w * TW: ic * T + (w + 1) * TW]
                av = a_sb[:, ic * T + w * TW: ic * T + (w + 1) * TW]
                y_evac[pr] = s.emit(
                    "vector",
                    lambda e, tv=tv, gv=gv, py=py, ic=ic: e.scalar_tensor_tensor(
                        tv, py[:], boc_sb[:, ic:ic + 1], gv, ALU.add, ALU.mult),
                    waits=(("mm", mm_y),), inc="ev")
                s.drain("vector")
                x1_done = s.emit("vector",
                                 lambda e, av=av, tv=tv: e.tensor_add(av, av, tv),
                                 waits=((dchx, dchx_t),), inc="ev")
        st_x1 = s.emit("sync", lambda e: e.dma_start(
            out=x1_dr.rearrange("(c p) t -> p c t", p=128),
            in_=a_sb[:].rearrange("p (c t) -> p c t", c=DC)),
            waits=(("ev", x1_done),), inc="st", dma=True)
        st_x1_t = s.c("st")

        aux1 = dict(
            eps_row=eps_ln[:1, :1],
            ps_stat0=ps[0][0:1, :], ps_stat1=ps[1][0:1, :],
            ps_bc0=ps[2][:], ps_bc1=ps[3][:],
            sq_scr=gf[:, 0:1024], diff_scr=gf[:, 1024:2048],
            rows=gf[0:1, 2048:5120],
            ones_col=ones_col[:], ones_row=ones_row,
            x_ready=("ev", x1_done),
            pre_waits=(("ev", x1_done),),
            xn_done=[0] * NW,
        )
        emit_layernorm(s, a_sb, xn_sb, aux1, colsum_to=None)
        xn1_done = aux1["xn_done"]

        mm_all_d = s.c("mm")
        fgk = s.new_sem("fgk")
        s.emit("gpsimd", lambda e: e.dma_start(
            out=w_sb[:, 0:8192].rearrange("p (c j) -> p c j", c=DC),
            in_=wk1_dr.rearrange("(c p) j -> p c j", p=128)),
            waits=(("mm", max(q_used)),), inc=fgk, dma=True)
        s.emit("gpsimd", lambda e: e.dma_start(
            out=w_sb[:, 8192:16384].rearrange("p (c j) -> p c j", c=DC),
            in_=wv1_dr.rearrange("(c p) j -> p c j", p=128)),
            waits=(("mm", mm_all_d),), inc=fg, dma=True)
        s.emit("sync",
               lambda e: e.dma_start(out=brow_sb[:1, 1024:2048], in_=bv1_dr),
               waits=(("mm", mm_all_d),), inc=fch, dma=True)
        fch_t = s.c(fch)
        fg_t = s.c(fg)
        fgk_t = s.c(fgk)
        bv1_row = brow_sb[:1, 1024:2048]
        v_all = o_sb[:, 0:2048]
        kf_all = o_sb[:, 2048:3072]
        kv_sb = kvf_sb
        ln1_all = ("ev", xn1_done[NW - 1])

        kv_done = 0
        kv_done_par = [0, 0]
        v_used = 0
        kf_used = 0
        pv_gate = [0, 0]
        pk_gate = [0, 0]
        pf_gate = [0, 0]
        first_kv = True
        sc_bvt = 0
        ev_v_prev = x1_done
        bvt_sb = t_scr[:, 0:128]
        for hp in range(8):
            mm_bv = s.emit("tensor", lambda e, hp=hp: e.matmul(
                ps[7][:, 0:128], ones_row[:1, 0:128],
                bv1_row[:1, hp * 128:(hp + 1) * 128], start=True, stop=True),
                waits=((fch, fch_t), ("sc", sc_bvt), ("ev", ev_m)), inc="mm")
            sc_bvt = s.emit("scalar", lambda e: e.copy(bvt_sb, ps[7][:, 0:128]),
                            waits=(("mm", mm_bv), ("ev", ev_v_prev)), inc="sc")
            ev_v = 0
            for tt in range(NT):
                pv = ps[2 + tt % 2]
                def vmm(e, hp=hp, tt=tt, pv=pv):
                    last = None
                    for c in range(DC):
                        last = e.matmul(
                            pv[:, 0:128],
                            xn_sb[:, c * T + tt * 128: c * T + (tt + 1) * 128],
                            w_sb[:, 8192 + c * 1024 + hp * 128:
                                 8192 + c * 1024 + (hp + 1) * 128],
                            start=(c == 0), stop=(c == DC - 1))
                    return last
                mm_v = s.emit("tensor", vmm,
                              waits=((fg, fg_t), ln1_all,
                                     ("mm", v_used), ("ev", pv_gate[tt % 2])),
                              inc="mm")
                ev_v = s.emit("vector", lambda e, tt=tt, pv=pv: e.tensor_add(
                    v_all[:, tt * 128:(tt + 1) * 128], pv[:, 0:128], bvt_sb),
                    waits=(("mm", mm_v), ("mm", v_used), ("sc", sc_bvt)),
                    inc="ev")
                pv_gate[tt % 2] = ev_v
                ev_v_prev = ev_v
            for half in range(2):
                h = 2 * hp + half
                ev_kf = 0
                for w in range(NW):
                    pk = ps[w % 2]
                    def kfmm(e, h=h, w=w, pk=pk):
                        last = None
                        for c in range(DC):
                            last = e.matmul(
                                pk[0:64, :],
                                w_sb[:, c * 1024 + h * 64: c * 1024 + (h + 1) * 64],
                                xn_sb[:, c * T + w * TW: c * T + (w + 1) * TW],
                                start=(c == 0), stop=(c == DC - 1))
                        return last
                    mm_kf = s.emit("tensor", kfmm,
                                   waits=((fgk, fgk_t), ln1_all,
                                          ("mm", kf_used), ("sc", pk_gate[w % 2])),
                                   inc="mm")
                    kfv = sq_q[0:64, (w % 2) * 512:(w % 2 + 1) * 512]
                    ev_kfm = s.emit("scalar",
                                    lambda e, kfv=kfv, pk=pk, h=h: e.activation(
                                        kfv, pk[0:64, :], AF.Identity,
                                        bias=bk1c_sb[:, h:h + 1]),
                                    waits=(("mm", mm_kf),), inc="sc")
                    pk_gate[w % 2] = ev_kfm
                    for it in range(4):
                        tt = w * 4 + it
                        pf = ps[4 + tt % 2]
                        mm_f = s.emit(
                            "tensor",
                            lambda e, it=it, kfv=kfv, pf=pf: e.matmul(
                                pf[:, 0:64], kfv[:, it * 128:(it + 1) * 128],
                                ptb_sb[0:64, :], start=True, stop=True),
                            waits=(("sc", ev_kfm),
                                   ("sc", pf_gate[tt % 2])), inc="mm")
                        ev_kf = s.emit(
                            "scalar", lambda e, tt=tt, pf=pf: e.activation(
                                kf_all[:, tt * 64:(tt + 1) * 64], pf[:, 0:64],
                                AF.Relu, scale=L1_NORM, bias=eps_col[:, 0:1]),
                            waits=(("mm", mm_f), ("mm", kf_used)), inc="sc")
                        pf_gate[tt % 2] = ev_kf
                cb = (h % 2) * 128
                def kvmm(e, half=half, cb=cb):
                    base = half * 64
                    for tt in range(NT):
                        e.matmul(ps[6][0:64, cb:cb + 64],
                                 kf_all[:, tt * 64:(tt + 1) * 64],
                                 v_all[:, tt * 128 + base: tt * 128 + base + 64],
                                 start=(tt == 0), stop=(tt == NT - 1))
                    for tt in range(NT):
                        last = e.matmul(ps[6][0:64, cb + 64:cb + 65],
                                        kf_all[:, tt * 64:(tt + 1) * 64],
                                        onesb_col[:],
                                        start=(tt == 0), stop=(tt == NT - 1))
                    return last
                extra = ()
                if first_kv:
                    extra = (("ev", ev_cvc),)
                    first_kv = False
                else:
                    extra = (("ev", kv_done_par[h % 2]),)
                mm_kv = s.emit("tensor", kvmm,
                               waits=(("sc", ev_kf), ("ev", ev_v), *extra),
                               inc="mm")
                kf_used = mm_kv
                if half == 1:
                    v_used = mm_kv
                kv_done = s.emit("vector", lambda e, h=h, cb=cb: e.tensor_copy(
                    kv_sb[0:64, h * 65:(h + 1) * 65], ps[6][0:64, cb:cb + 65]),
                    waits=(("mm", mm_kv),), inc="ev")
                kv_done_par[h % 2] = kv_done

        s.emit("sync", lambda e: e.dma_start(out=cc_kv_i[:, :], in_=kv_sb[0:64, :]),
               waits=(("ev", kv_done),), inc=c3i, dma=True)
        c3i_t = s.c(c3i)
        s.emit("gpsimd", lambda e: e.collective_compute(
            "AllReduce", ALU.add, replica_groups=RG,
            ins=[cc_kv_i.ap().opt()], outs=[cc_kv_o.ap().opt()]),
            waits=((c3i, 16),), inc=c3c, dma=True, inc_n=1)
        s.emit("sync", lambda e: e.dma_start(out=kv_sb[0:64, :], in_=cc_kv_o[:, :]),
               waits=((c3c, 1),), inc=c3o, dma=True)
        c3o_t = s.c(c3o)

        B3mm = s.c("mm")
        B3ev = s.c("ev")
        B3sc = s.c("sc")
        s.emit("tensor", lambda e: e.nop(), waits=(("ev", B3ev), ("sc", B3sc)))
        s.emit("vector", lambda e: e.nop(), waits=(("mm", B3mm), ("sc", B3sc)))
        s.emit("scalar", lambda e: e.nop(), waits=(("mm", B3mm), ("ev", B3ev)))

        lg_read = 0
        ev_ct = 0
        for w in range(NW):
            mm_lg3 = s.emit("tensor", lambda e, w=w: [
                e.matmul(ps[0][0:16, :], rw_sb[:, c * 16:(c + 1) * 16],
                         xn_sb[:, c * T + w * TW: c * T + (w + 1) * TW],
                         start=(c == 0), stop=(c == DC - 1))
                for c in range(DC)][-1],
                waits=((preg, preg_t), ("ev", xn1_done[w]), ("ev", lg_read)),
                inc="mm")
            ev_lg = s.emit("vector", lambda e: e.tensor_copy(
                t_scr[0:16, 0:512], ps[0][0:16, :]),
                waits=(("mm", mm_lg3),), inc="ev")
            lg_read = ev_lg
            for it in range(4):
                tt = w * 4 + it
                mm_tr = s.emit("tensor", lambda e, it=it: e.transpose(
                    ps[1][:, 0:16], t_scr[0:16, it * 128:(it + 1) * 128],
                    id_sb[0:16, 0:16]),
                    waits=(("ev", ev_lg), ("ev", ev_ct)), inc="mm")

                def sfm(e):
                    e.reduce_max(rcol_sb[:, 0:1], ps[1][:, 0:16], axis=AX.X)
                    e.drain()
                    return e.tensor_scalar(rcol_sb[:, 1:2], rcol_sb[:, 0:1],
                                           -1.0, None, ALU.mult)
                ev_s1 = s.emit("vector", sfm, waits=(("mm", mm_tr),), inc="ev")
                sc_e = s.emit("scalar", lambda e: e.activation(
                    pr_sb[:, 0:16], ps[1][:, 0:16], AF.Exp,
                    bias=rcol_sb[:, 1:2]),
                    waits=(("ev", ev_s1),), inc="sc")

                def top2(e, tt=tt):
                    ctv = ct_sb[:, tt * 16:(tt + 1) * 16]
                    e.reduce_sum(rcol_sb[:, 0:1], pr_sb[:, 0:16], axis=AX.X)
                    e.drain()
                    e.reciprocal(rcol_sb[:, 1:2], rcol_sb[:, 0:1])
                    e.drain()
                    e.tensor_scalar(pr_sb[:, 0:16], pr_sb[:, 0:16],
                                    rcol_sb[:, 1:2], None, ALU.mult)
                    e.drain()
                    e.reduce_max(rcol_sb[:, 0:1], pr_sb[:, 0:16], axis=AX.X)
                    e.drain()
                    e.tensor_scalar(ctv, pr_sb[:, 0:16],
                                    rcol_sb[:, 0:1], None, ALU.is_ge)
                    e.drain()
                    e.scalar_tensor_tensor(ctv, ctv, -2.0, pr_sb[:, 0:16],
                                           ALU.mult, ALU.add)
                    e.drain()
                    e.reduce_max(rcol_sb[:, 2:3], ctv, axis=AX.X)
                    e.drain()
                    e.tensor_scalar(ctv, pr_sb[:, 0:16],
                                    rcol_sb[:, 2:3], None, ALU.is_ge)
                    e.drain()
                    return e.tensor_mul(ctv, ctv, pr_sb[:, 0:16])
                ev_ct = s.emit("vector", top2, waits=(("sc", sc_e),), inc="ev")
        coeff_done = ev_ct

        kv_ready = s.emit("vector", lambda e: e.tensor_copy(
            kvb_sb[:], kv_sb[0:64, :]), waits=((c3o, c3o_t),), inc="ev")

        wd_slab = g_sb[:, 8192:12288]
        gu_freed = [B3mm, B3mm]
        dn_freed = B3mm
        slchb = [s.new_sem("slA3"), s.new_sem("slB3")]
        dnch = s.new_sem("dnch")
        hid_ps_gate = [0, 0]
        up_ps_gate = [0, 0]
        o_ps_gate = [0, 0]
        cb_gate = 0
        sil_free = [0, 0]
        crow_gate = 0
        inter_used = 0
        ev_ffn = 0
        sc_sil = 0
        first_acc = True
        acc3_last = {}
        hid3_w_last = {}
        for pe in range(-2, 16):
            soff = (pe + 2) * 512
            b = (pe + 2) % 2
            wg_slab = w_sb[:, b * 8192: b * 8192 + 4096]
            wu_slab = w_sb[:, b * 8192 + 4096: (b + 1) * 8192]
            if pe < 0:
                s.emit("gpsimd", lambda e, soff=soff, wg_slab=wg_slab: e.dma_start(
                    out=wg_slab.rearrange("p (c h) -> p c h", c=DC),
                    in_=sg_dr.rearrange("(c p) h -> p c h", p=128)
                    [:, :, soff:soff + 512]),
                    waits=(("mm", gu_freed[b]),), inc=slchb[b], dma=True)
                s.emit("gpsimd", lambda e, soff=soff, wu_slab=wu_slab: e.dma_start(
                    out=wu_slab.rearrange("p (c h) -> p c h", c=DC),
                    in_=su_dr.rearrange("(c p) h -> p c h", p=128)
                    [:, :, soff:soff + 512]),
                    waits=(("mm", gu_freed[b]),), inc=slchb[b], dma=True)
                s.emit("gpsimd", lambda e, soff=soff: e.dma_start(
                    out=wd_slab.rearrange("p (q f) -> p q f", q=4),
                    in_=sd_dr.rearrange("(q p) f -> p q f", p=128)
                    [:, soff // 128: soff // 128 + 4, :]),
                    waits=(("mm", dn_freed),), inc=dnch, dma=True)
            else:
                s.emit("gpsimd", lambda e, pe=pe, wg_slab=wg_slab: e.dma_start(
                    out=wg_slab.rearrange("p (c h) -> p c h", c=DC),
                    in_=gw_dr[pe * D:(pe + 1) * D, :].rearrange(
                        "(c p) h -> p c h", p=128)),
                    waits=(("mm", gu_freed[b]),), inc=slchb[b], dma=True)
                s.emit("gpsimd", lambda e, pe=pe, wu_slab=wu_slab: e.dma_start(
                    out=wu_slab.rearrange("p (c h) -> p c h", c=DC),
                    in_=uw_dr[pe * D:(pe + 1) * D, :].rearrange(
                        "(c p) h -> p c h", p=128)),
                    waits=(("mm", gu_freed[b]),), inc=slchb[b], dma=True)
                s.emit("gpsimd", lambda e, pe=pe: e.dma_start(
                    out=wd_slab.rearrange("p (q f) -> p q f", q=4),
                    in_=dw_dr[pe * F_EXP:(pe + 1) * F_EXP, :].rearrange(
                        "(q p) f -> p q f", p=128)),
                    waits=(("mm", dn_freed),), inc=dnch, dma=True)
            slch_t = s.c(slchb[b])
            wtd = (dnch, s.c(dnch))
            for w in range(NW):
                if pe >= 0:
                    mm_ctr = s.emit("tensor", lambda e, pe=pe, w=w: [
                        e.transpose(ps[4][0:1, it * 128:(it + 1) * 128],
                                    ct_sb[:, (w * 4 + it) * 16 + pe:
                                          (w * 4 + it) * 16 + pe + 1],
                                    id_sb[:])
                        for it in range(4)][-1],
                        waits=(("ev", coeff_done), ("ev", cb_gate),
                               (pre, pre_t)), inc="mm")
                    ev_cr = s.emit("vector", lambda e: e.tensor_copy(
                        crow_sb[:1, 0:512], ps[4][0:1, :]),
                        waits=(("mm", mm_ctr), ("mm", crow_gate),
                               (dch, dch_t)), inc="ev")
                    mm_cb = s.emit("tensor", lambda e: e.matmul(
                        ps[5][:], ones_row[:1, 0:128], crow_sb[:1, 0:512],
                        start=True, stop=True),
                        waits=(("ev", ev_cr), ("ev", cb_gate)), inc="mm")
                    crow_gate = mm_cb
                for q in range(4):
                    pr = (q * NW + w) % 2
                    pg = ps[pr]
                    pu = ps[2 + pr]
                    def gmm(e, q=q, w=w, pg=pg, wg_slab=wg_slab):
                        last = None
                        for c in range(DC):
                            last = e.matmul(
                                pg[:],
                                wg_slab[:, c * 512 + q * 128: c * 512 + (q + 1) * 128],
                                xn_sb[:, c * T + w * TW: c * T + (w + 1) * TW],
                                start=(c == 0), stop=(c == DC - 1))
                        return last
                    mm_g = s.emit("tensor", gmm,
                                  waits=((slchb[b], slch_t),
                                         ("ev", coeff_done),
                                         ("sc", hid_ps_gate[pr])), inc="mm")
                    def umm(e, q=q, w=w, pu=pu, wu_slab=wu_slab):
                        last = None
                        for c in range(DC):
                            last = e.matmul(
                                pu[:],
                                wu_slab[:, c * 512 + q * 128: c * 512 + (q + 1) * 128],
                                xn_sb[:, c * T + w * TW: c * T + (w + 1) * TW],
                                start=(c == 0), stop=(c == DC - 1))
                        return last
                    mm_u = s.emit("tensor", umm,
                                  waits=((slchb[b], slch_t), ("ev", coeff_done),
                                         ("ev", up_ps_gate[pr])), inc="mm")
                    sv = sq_q[:, pr * 512:(pr + 1) * 512]
                    sc_sil = s.emit("scalar",
                                    lambda e, sv=sv, pg=pg: e.activation(
                                        sv, pg[:], AF.Silu),
                                    waits=(("mm", mm_g), ("mm", inter_used),
                                           ("ev", sil_free[pr])),
                                    inc="sc")
                    hid_ps_gate[pr] = sc_sil
                    hv = g_sb[:, q * T + w * TW: q * T + (w + 1) * TW]
                    if pe < 0:
                        sil_free[pr] = up_ps_gate[pr] = s.emit(
                            "vector",
                            lambda e, hv=hv, sv=sv, pu=pu: e.tensor_mul(
                                hv, sv, pu[:]),
                            waits=(("mm", mm_u), ("sc", sc_sil),
                                   ("mm", inter_used)), inc="ev")
                    else:
                        tv = t_scr[:, pr * 512:(pr + 1) * 512]
                        sil_free[pr] = s.emit(
                            "vector",
                            lambda e, tv=tv, sv=sv, pu=pu: e.tensor_mul(
                                tv, sv, pu[:]),
                            waits=(("mm", mm_u), ("sc", sc_sil),
                                   ("mm", inter_used)), inc="ev")
                        s.drain("vector")
                        up_ps_gate[pr] = s.emit(
                            "vector",
                            lambda e, hv=hv, tv=tv: e.tensor_mul(
                                hv, tv, ps[5][:]),
                            waits=(("mm", crow_gate),), inc="ev")
                if pe >= 0:
                    cb_gate = up_ps_gate[(3 * NW + w) % 2]
            gu_freed[b] = s.c("mm")
            mm_o_last = 0
            for fc in range(DC):
                for w in range(NW):
                    pr = (fc * NW + w) % 2
                    po = ps[6 + pr]
                    def dmm(e, fc=fc, w=w, po=po):
                        last = None
                        for q in range(4):
                            last = e.matmul(
                                po[:],
                                wd_slab[:, q * 1024 + fc * 128:
                                        q * 1024 + (fc + 1) * 128],
                                g_sb[:, q * T + w * TW: q * T + (w + 1) * TW],
                                start=(q == 0), stop=(q == 3))
                        return last
                    mm_o = s.emit("tensor", dmm,
                                  waits=(wtd,
                                         ("ev", up_ps_gate[0]),
                                         ("ev", up_ps_gate[1]),
                                         ("ev", o_ps_gate[pr])), inc="mm")
                    av = a_sb[:, fc * T + w * TW: fc * T + (w + 1) * TW]
                    if pe == -2:
                        extra = (("st", st_x1_t),) if first_acc else ()
                        first_acc = False
                        ev_ffn = s.emit(
                            "vector",
                            lambda e, av=av, po=po: e.tensor_copy(av, po[:]),
                            waits=(("mm", mm_o), *extra), inc="ev")
                    else:
                        ev_ffn = s.emit(
                            "vector",
                            lambda e, av=av, po=po: e.tensor_add(av, av, po[:]),
                            waits=(("mm", mm_o),), inc="ev")
                    o_ps_gate[pr] = ev_ffn
                    mm_o_last = mm_o
                    if pe == 15:
                        acc3_last[(fc, w)] = ev_ffn
                        if fc == DC - 1:
                            hid3_w_last[w] = mm_o
            dn_freed = s.c("mm")
            inter_used = mm_o_last
        ffn_done = s.c("ev")

        sil_done3 = 0
        for fc in range(DC):
            for w in range(NW):
                gv = g_sb[:, fc * T + w * TW: fc * T + (w + 1) * TW]
                av = a_sb[:, fc * T + w * TW: fc * T + (w + 1) * TW]
                sil_done3 = s.emit("scalar",
                                   lambda e, gv=gv, av=av: e.activation(
                                       gv, av, AF.Silu),
                                   waits=(("ev", acc3_last[(fc, w)]),
                                          ("mm", hid3_w_last[w] if fc < 4
                                           else inter_used),
                                          ("ev", kv_ready), (c3i, c3i_t)),
                                   inc="sc")

        s.emit("gpsimd", lambda e: e.dma_start(
            out=w_sb[:, 0:8192].rearrange("p (hp j) -> p hp j", hp=8),
            in_=woP_dr.rearrange("p (hp j) -> p hp j", hp=8)),
            waits=(("mm", gu_freed[0]),), inc=d3g, dma=True)
        d3ch_t = s.c(d3ch)
        s.emit("sync", lambda e: e.dma_start(
            out=a_sb[:].rearrange("p (c t) -> p c t", c=DC),
            in_=x1_dr.rearrange("(c p) t -> p c t", p=128)),
            waits=(("sc", sil_done3), ("st", st_x1_t)), inc=d3chx, dma=True)
        d3chx_t = s.c(d3chx)
        wo_v = w_sb[:, 0:8192]

        out_done = 0
        wq_used = 0
        pq_gate = [0, 0]
        qv_used = [[0, 0], [0, 0]]
        pf_gate = [0, 0]
        ybank_gate = [0, 0]
        po_free = [0, 0]
        tv_free = [0, 0]
        oa_done = 0
        dn_gate = [0, 0]
        py_gate = [0, 0]
        crow_d_gate = [B3mm, B3mm]
        first_qv = [True, True]
        sil_last = [sil_free[0], sil_free[1]]
        d3gb = [s.new_sem("d3gA"), s.new_sem("d3gB")]
        wqh_used = [gu_freed[1], gu_freed[1]]
        pending_tail = None

        def emit_pair_tail(hp, pr2, po, crow_p, qkv1, pd_last_v):
            nonlocal oa_done
            ev_rcp = s.emit(
                "vector",
                lambda e, crow_p=crow_p: e.reciprocal(
                    crow_p[:1, 0:1024], crow_p[:1, 0:1024]),
                waits=(("sc", pd_last_v),), inc="ev")
            tvp = t_scr[:, pr2 * 512:(pr2 + 1) * 512]
            sc_oc = s.emit(
                "scalar",
                lambda e, tvp=tvp, po=po: e.copy(tvp, po[:]),
                waits=(("mm", qkv1), ("ev", tv_free[pr2])),
                inc="sc")
            mm_bc = s.emit(
                "tensor",
                lambda e, po=po, crow_p=crow_p: [
                    e.matmul(po[0:64, :], ones_row[:1, 0:64],
                             crow_p[:1, 0:512], start=True, stop=True),
                    e.matmul(po[64:128, :], ones_row[:1, 0:64],
                             crow_p[:1, 512:1024], start=True, stop=True),
                ][-1],
                waits=(("ev", ev_rcp), ("sc", sc_oc)), inc="mm")
            crow_d_gate[pr2] = mm_bc
            ovp = o_sb[:, hp * 512:(hp + 1) * 512]
            oa_done = s.emit(
                "vector",
                lambda e, ovp=ovp, tvp=tvp, po=po: e.tensor_mul(
                    ovp, tvp, po[:]),
                waits=(("mm", mm_bc),), inc="ev")
            po_free[pr2] = oa_done
            tv_free[pr2] = oa_done

        for w in range(NW):
            for H in range(2):
                hb = (w * 2 + H) % 2
                wqh_v = w_sb[:, 8192 + hb * 4096: 8192 + (hb + 1) * 4096]
                s.emit("gpsimd", lambda e, H=H, wqh_v=wqh_v: e.dma_start(
                    out=wqh_v.rearrange("p (c j) -> p c j", c=DC),
                    in_=wq1_dr.rearrange("(c p) j -> p c j", p=128)
                    [:, :, H * 512:(H + 1) * 512]),
                    waits=(("mm", wqh_used[hb]),),
                    inc=d3gb[hb], dma=True)
                d3g_t = s.c(d3gb[hb])
                for hp_l in range(4):
                    hp = H * 4 + hp_l
                    pr2 = hp % 2
                    po = ps[4 + pr2]
                    pq = ps[pr2]
                    qkv_mm = [None, None]
                    def qmm(e, hp_l=hp_l, w=w, pq=pq, wqh_v=wqh_v):
                        last = None
                        for c in range(DC):
                            last = e.matmul(
                                pq[:],
                                wqh_v[:, c * 512 + hp_l * 128:
                                      c * 512 + (hp_l + 1) * 128],
                                xn_sb[:, c * T + w * TW: c * T + (w + 1) * TW],
                                start=(c == 0), stop=(c == DC - 1))
                        return last
                    mm_q = s.emit("tensor", qmm,
                                  waits=((d3ch, d3ch_t), (d3gb[hb], d3g_t),
                                         ("sc", sil_done3),
                                         ("sc", pq_gate[pr2])), inc="mm")
                    wq_used = mm_q
                    crow_p = brow_sb[:1, 2048 + pr2 * 1024: 3072 + pr2 * 1024]
                    psd = ps[6 + pr2]
                    qv_arena = sq_q if pr2 == 0 else m_sb
                    sc_qv = [0, 0]
                    for half in range(2):
                        h = 2 * hp + half
                        qv = qv_arena[0:64, half * 512:(half + 1) * 512]
                        extra = ()
                        if first_qv[half]:
                            extra = (("ev", max(sil_last)),)
                            first_qv[half] = False
                        sc_qv[half] = s.emit(
                            "scalar",
                            lambda e, qv=qv, pq=pq, h=h, half=half: e.activation(
                                qv, pq[half * 64:(half + 1) * 64, :],
                                AF.Identity, bias=bq1c_sb[:, h:h + 1]),
                            waits=(("mm", mm_q), ("mm", qv_used[pr2][half]),
                                   *extra),
                            inc="sc")
                    pq_gate[pr2] = sc_qv[1]
                    pd_last = 0
                    for half in range(2):
                        h = 2 * hp + half
                        qv = qv_arena[0:64, half * 512:(half + 1) * 512]
                        pf = ps[2 + half]
                        mm_f = s.emit("tensor",
                                      lambda e, qv=qv, pf=pf: e.matmul(
                                          pf[0:64, :], ptb_sb[:], qv,
                                          start=True, stop=True),
                                      waits=(("sc", sc_qv[half]),
                                             ("sc", pf_gate[half]),
                                             ("ev", ybank_gate[half])), inc="mm")
                        pf_gate[half] = s.emit(
                            "scalar", lambda e, qv=qv, pf=pf: e.activation(
                                qv, pf[0:64, :], AF.Relu, scale=L1_NORM,
                                bias=eps_col[0:64, :]),
                            waits=(("mm", mm_f),), inc="sc")
                        mm_qkv = s.emit(
                            "tensor",
                            lambda e, h=h, half=half, qv=qv, po=po: e.matmul(
                                po[half * 64:(half + 1) * 64, :],
                                kvb_sb[:, h * 65: h * 65 + 64], qv,
                                start=True, stop=True),
                            waits=(("sc", pf_gate[half]), ("ev", kv_ready),
                                   ("ev", po_free[pr2])), inc="mm")
                        qkv_mm[half] = mm_qkv
                        mm_dn = s.emit(
                            "tensor",
                            lambda e, h=h, qv=qv, psd=psd: e.matmul(
                                psd[0:1, :],
                                kvb_sb[:, h * 65 + 64: h * 65 + 65], qv,
                                start=True, stop=True),
                            waits=(("sc", dn_gate[pr2]), ("ev", ev_m)),
                            inc="mm")
                        qv_used[pr2][half] = mm_dn
                        pd_last = s.emit(
                            "scalar",
                            lambda e, half=half, psd=psd, crow_p=crow_p:
                            e.activation(
                                crow_p[:1, half * 512:(half + 1) * 512],
                                psd[0:1, :], AF.Identity,
                                bias=eps_r6[:1, :1]),
                            waits=(("mm", mm_dn), ("mm", crow_d_gate[pr2])),
                            inc="sc")
                    dn_gate[pr2] = pd_last
                    if pending_tail is not None:
                        emit_pair_tail(*pending_tail)
                    pending_tail = (hp, pr2, po, crow_p, qkv_mm[1], pd_last)
                wqh_used[hb] = wq_used
            if pending_tail is not None:
                emit_pair_tail(*pending_tail)
                pending_tail = None
            o_done = oa_done
            for ic in range(DC):
                pr = ic % 2
                py = ps[2 + pr]
                def ymm(e, ic=ic, py=py):
                    last = None
                    for hp in range(8):
                        last = e.matmul(
                            py[:],
                            wo_v[:, hp * 1024 + ic * 128:
                                 hp * 1024 + (ic + 1) * 128],
                            o_sb[:, hp * 512:(hp + 1) * 512],
                            start=(hp == 0), stop=(hp == 7))
                    return last
                mm_y = s.emit("tensor", ymm,
                              waits=((d3ch, d3ch_t), (d3g, 16), ("ev", o_done),
                                     ("ev", py_gate[pr]),
                                     ("sc", pf_gate[0]), ("sc", pf_gate[1])),
                              inc="mm")
                tv = t_scr[:, pr * 512:(pr + 1) * 512]
                gv = g_sb[:, ic * T + w * TW: ic * T + (w + 1) * TW]
                av = a_sb[:, ic * T + w * TW: ic * T + (w + 1) * TW]
                py_gate[pr] = s.emit(
                    "vector",
                    lambda e, tv=tv, gv=gv, py=py, ic=ic: e.scalar_tensor_tensor(
                        tv, py[:], bo1c_sb[:, ic:ic + 1], gv, ALU.add, ALU.mult),
                    waits=(("mm", mm_y),), inc="ev")
                s.drain("vector")
                out_done = s.emit("vector",
                                  lambda e, av=av, tv=tv: e.tensor_add(av, av, tv),
                                  waits=((d3chx, d3chx_t),), inc="ev")
                tv_free[pr] = out_done
            ybank_gate = [py_gate[0], py_gate[1]]
        s.emit("sync", lambda e: e.dma_start(
            out=out_dr.rearrange("(c p) t -> p c t", p=128),
            in_=a_sb[:].rearrange("p (c t) -> p c t", c=DC)),
            waits=(("ev", out_done),), inc="st", dma=True)
        st_all = s.c("st")
        s.emit("sync", lambda e: e.nop(), waits=(("st", st_all),))

        s.finish()
    return nc



def _prep_inputs(inp):
    d = {}
    x = np.asarray(inp["L0_attn_x"])[0]
    x_fm = np.ascontiguousarray(x.T)
    d["x_slices"] = [np.ascontiguousarray(x_fm[:, c * T:(c + 1) * T])
                     for c in range(NCORES)]

    w_qkv = np.asarray(inp["L0_attn_w_qkv"])
    b_qkv = np.asarray(inp["L0_attn_b_qkv"])
    rows = (np.arange(128)[None, :] * 8 + np.arange(8)[:, None]).reshape(-1)
    for name, lo in (("q", 0), ("k", 1), ("v", 2)):
        w = w_qkv[lo * D:(lo + 1) * D][rows, :]
        d[f"w{name}T"] = np.ascontiguousarray(w.T)
        d[f"b{name}"] = np.ascontiguousarray(
            b_qkv[lo * D:(lo + 1) * D][rows])[None, :]

    c_ = np.ascontiguousarray
    d["w_oT"] = c_(np.asarray(inp["L0_attn_w_o"]).T)
    d["b_o"] = c_(np.asarray(inp["L0_attn_b_o"]))[None, :]
    d["alpha"] = np.asarray(inp["L0_ffn_alpha"], np.float32).reshape(1, 1)
    d["wgate"] = c_(np.asarray(inp["L0_ffn_w_gate"]))
    d["bgate"] = c_(np.asarray(inp["L0_ffn_b_gate"]))[None, :]
    d["W1m"] = c_(np.asarray(inp["L0_ffn_W1m"]))
    d["b1m"] = c_(np.asarray(inp["L0_ffn_b1m"]))[None, :]
    d["W2m"] = c_(np.asarray(inp["L0_ffn_W2m"]))
    d["b2m"] = c_(np.asarray(inp["L0_ffn_b2m"]))[None, :]
    d["W1e"] = c_(np.asarray(inp["L0_ffn_W1e"]).reshape(8 * D, 4096))
    d["b1e"] = c_(np.asarray(inp["L0_ffn_b1e"]))
    d["W2e"] = c_(np.asarray(inp["L0_ffn_W2e"]).reshape(8 * 4096, D))
    d["b2e"] = c_(np.asarray(inp["L0_ffn_b2e"]))
    for nm in ("q", "k", "v", "o"):
        d[f"w{nm}1T"] = c_(np.asarray(inp[f"L1_attn_W{nm}"]).T)
        d[f"b{nm}1"] = c_(np.asarray(inp[f"L1_attn_b{nm}"]))[None, :]
    d["wo1P"] = c_(d["wo1T"].reshape(8, 2, 64, D).transpose(1, 2, 0, 3).reshape(128, 8 * D))
    d["b1m_cols"] = c_(d["b1m"][0].reshape(32, 128).T)
    d["bq_cols"] = c_(d["bq"][0].reshape(8, 128).T)
    d["bo_cols"] = c_(d["b_o"][0].reshape(8, 128).T)
    d["bo1_cols"] = c_(d["bo1"][0].reshape(8, 128).T)
    d["bk1_cols"] = c_(d["bk1"][0].reshape(16, 64).T)
    d["bq1_cols"] = c_(d["bq1"][0].reshape(16, 64).T)
    d["PT"] = c_(np.asarray(inp["L1_attn_P"]).T)
    d["ident"] = np.eye(128, dtype=np.float32)
    d["router_w"] = c_(np.asarray(inp["L1_ffn_router_w"]))
    d["gate_w"] = c_(np.asarray(inp["L1_ffn_gate_w"]).reshape(16 * D, 512))
    d["up_w"] = c_(np.asarray(inp["L1_ffn_up_w"]).reshape(16 * D, 512))
    d["down_w"] = c_(np.asarray(inp["L1_ffn_down_w"]).reshape(16 * 512, D))
    d["sgate_w"] = c_(np.asarray(inp["L1_ffn_shared_gate_w"]))
    d["sup_w"] = c_(np.asarray(inp["L1_ffn_shared_up_w"]))
    d["sdown_w"] = c_(np.asarray(inp["L1_ffn_shared_down_w"]))
    return d


def kernel(**inputs):
    prep = _prep_inputs(inputs)
    cores = list(range(NCORES))
    nc = build_fused()
    shared = {
        "wkT": prep["wkT"], "wvT": prep["wvT"],
        "bk": prep["bk"], "bv": prep["bv"],
        "alpha": prep["alpha"], "wgate": prep["wgate"], "bgate": prep["bgate"],
        "W1m": prep["W1m"], "W2m": prep["W2m"], "b2m": prep["b2m"],
        "W1e": prep["W1e"], "b1e": prep["b1e"],
        "W2e": prep["W2e"], "b2e": prep["b2e"],
        "wqT": prep["wqT"], "woT": prep["w_oT"], "bo_cols": prep["bo_cols"],
        "wk1T": prep["wk1T"], "wv1T": prep["wv1T"], "bv1": prep["bv1"],
        "PT": prep["PT"], "ident": prep["ident"],
        "b1m_cols": prep["b1m_cols"], "bq_cols": prep["bq_cols"],
        "bk1_cols": prep["bk1_cols"], "bq1_cols": prep["bq1_cols"],
        "wq1T": prep["wq1T"], "wo1P": prep["wo1P"], "bo1_cols": prep["bo1_cols"],
        "router_w": prep["router_w"],
        "gate_w": prep["gate_w"], "up_w": prep["up_w"], "down_w": prep["down_w"],
        "sgate_w": prep["sgate_w"], "sup_w": prep["sup_w"],
        "sdown_w": prep["sdown_w"],
    }
    in_maps = [{**shared, "x_fm": prep["x_slices"][c]} for c in range(NCORES)]
    res = run_bass_kernel_spmd(nc, in_maps, core_ids=cores).results
    out = np.empty((1, NCORES * T, D), np.float32)
    for c in range(NCORES):
        out[0, c * T:(c + 1) * T, :] = res[c]["out_fm"].T
    return out


# version 12
# speedup vs baseline: 1.0405x; 1.0110x over previous
import numpy as np
import ml_dtypes
from contextlib import ExitStack

import concourse.bass as bass
import concourse.mybir as mybir
from concourse.bass import IndirectOffsetOnAxis
from concourse.bass_utils import run_bass_kernel_spmd

F32 = mybir.dt.float32
BF16 = mybir.dt.bfloat16
I32 = mybir.dt.int32
AF = mybir.ActivationFunctionType
ALU = mybir.AluOpType
AX = mybir.AxisListType

NCORES = 8
RG = [[0, 1, 2, 3, 4, 5, 6, 7]]
T = 2048
D = 1024
DC = 8
TW = 512
NW = T // TW
NT = T // 128
LN_EPS = 1e-5
FFN_H = 4096
HC = 512
NHC = FFN_H // HC
L1_NORM = 1.0 / (64.0 ** 0.25)
F_EXP = 512
F_SH = 1024



class Sched:

    ENGINES = ("sync", "gpsimd", "tensor", "vector", "scalar")
    SEMS = ("ld", "wt", "mm", "ev", "sc", "gp", "st")

    DEBUG = False

    def __init__(self, nc, stack):
        self.nc = nc
        self.stack = stack
        self.q = {e: [] for e in self.ENGINES}
        self.sems = {n: stack.enter_context(nc.semaphore(n)) for n in self.SEMS}
        self.cnt = {n: 0 for n in self.SEMS}
        self.floor = {e: {n: 0 for n in self.SEMS} for e in self.ENGINES}
        self.log = []

    def new_sem(self, name):
        self.sems[name] = self.stack.enter_context(self.nc.semaphore(name))
        self.cnt[name] = 0
        for e in self.ENGINES:
            self.floor[e][name] = 0
        return name

    def emit(self, eng, fn, waits=(), inc=None, dma=False, inc_n=None):
        n = inc_n if inc_n is not None else (16 if dma else 1)
        fl = self.floor[eng]
        real_waits = []
        for s_, v in waits:
            if v > fl[s_]:
                real_waits.append((s_, v))
                fl[s_] = v
        target = None
        if inc is not None:
            self.cnt[inc] += n
            target = self.cnt[inc]
            if not dma and inc_n is None:
                fl[inc] = max(fl[inc], target)
        if Sched.DEBUG:
            import sys as _s
            self.log.append((eng, _s._getframe(1).f_lineno,
                             tuple(real_waits), inc, target))
        sems = self.sems

        def cl(e, fn=fn, rw=tuple(real_waits), inc=inc, n=n):
            for s_, v in rw:
                e.wait_ge(sems[s_], v)
            ins = fn(e)
            if inc is not None:
                ins.then_inc(sems[inc], n)
        self.q[eng].append(cl)
        return target

    def drain(self, eng):
        self.q[eng].append(lambda e: e.drain())

    def c(self, name):
        return self.cnt[name]

    def finish(self):
        with self.nc.Block() as block:
            for engname in self.ENGINES:
                ops = self.q[engname]
                if not ops:
                    continue
                getattr(block, engname)(
                    lambda e, ops=ops: [f(e) for f in ops] and None
                )


def _alloc(stack, nc, name, shape, dtype):
    return stack.enter_context(nc.sbuf_tensor(name, shape, dtype))


def _psum(stack, nc, name, shape, dtype=F32):
    return stack.enter_context(nc.psum_tensor(name, shape, dtype))



def emit_layernorm(s, x_sb, xn_sb, aux, colsum_to=None):
    ps_sum = aux["ps_stat0"]
    ps_sq = aux["ps_stat1"]
    ps_m = aux["ps_bc0"]
    ps_r = aux["ps_bc1"]
    sq = aux["sq_scr"]
    rows = aux["rows"]
    ones_col = aux["ones_col"]
    ones_row = aux["ones_row"]
    x_ready = aux["x_ready"]
    pre = tuple(aux.get("pre_waits", ()))

    sq_gate = [0, 0]
    st = dict(stats_read=0,
              bc_read=0,
              sc_sd=0)
    ev_r = [0, 0]
    bc_done = [0, 0]

    def stage_a(w):
        p = w % 2
        last = None
        for c in range(DC):
            xin = x_sb[:, c * T + w * TW: c * T + (w + 1) * TW]
            sqv = sq[:, (c % 2) * TW: (c % 2 + 1) * TW]
            sc_done = s.emit(
                "scalar",
                lambda e, xin=xin, sqv=sqv: e.square(sqv, xin),
                waits=(*pre, x_ready, ("mm", sq_gate[c % 2])),
                inc="sc")
            s.emit(
                "tensor",
                lambda e, xin=xin, c=c: e.matmul(
                    ps_sum, ones_col, xin, start=(c == 0), stop=(c == DC - 1)),
                waits=(*pre, x_ready, ("ev", st["stats_read"])))
            mm_i = s.emit(
                "tensor",
                lambda e, sqv=sqv, c=c: e.matmul(
                    ps_sq, ones_col, sqv, start=(c == 0), stop=(c == DC - 1)),
                waits=(("sc", sc_done),),
                inc="mm")
            sq_gate[c % 2] = mm_i
            last = mm_i
        m_row = rows[:, p * TW: (p + 1) * TW]
        r_row = rows[:, (2 + p) * TW: (3 + p) * TW]
        t4 = rows[:, 4 * TW: 5 * TW]
        t5 = rows[:, 5 * TW: 6 * TW]
        s.emit("vector",
               lambda e, m_row=m_row: e.tensor_scalar_mul(m_row, ps_sum, 1.0 / D),
               waits=(("mm", last), ("mm", bc_done[p])))
        st["stats_read"] = s.emit(
            "vector", lambda e: e.tensor_scalar_mul(t4, ps_sq, 1.0 / D),
            waits=(("sc", st["sc_sd"]),), inc="ev")
        s.drain("vector")
        s.emit("vector", lambda e, m_row=m_row: e.tensor_mul(t5, m_row, m_row))
        s.drain("vector")
        ev_var = s.emit("vector", lambda e: e.tensor_sub(t4, t4, t5),
                        inc="ev")
        st["sc_sd"] = s.emit(
            "scalar",
            lambda e: e.activation(t4, t4, AF.Sqrt, bias=aux["eps_row"]),
            waits=(("ev", ev_var),), inc="sc")
        ev_r[p] = s.emit("vector",
                         lambda e, r_row=r_row: e.reciprocal(r_row, t4),
                         waits=(("sc", st["sc_sd"]),), inc="ev")

    def stage_b(w):
        p = w % 2
        m_row = rows[:, p * TW: (p + 1) * TW]
        r_row = rows[:, (2 + p) * TW: (3 + p) * TW]
        mm_bc = s.emit(
            "tensor",
            lambda e, m_row=m_row, r_row=r_row: [
                e.matmul(ps_m, ones_row[:1, 0:128], m_row, start=True, stop=True),
                e.matmul(ps_r, ones_row[:1, 0:128], r_row, start=True, stop=True),
            ][-1],
            waits=(("ev", ev_r[p]), ("ev", st["bc_read"])), inc="mm")
        bc_done[p] = mm_bc
        diff = aux["diff_scr"]
        ev_xn = 0

        def em_diff(c):
            xin = x_sb[:, c * T + w * TW: c * T + (w + 1) * TW]
            dv = diff[:, (c % 2) * TW: (c % 2 + 1) * TW]
            s.emit("vector",
                   lambda e, xin=xin, dv=dv: e.tensor_sub(dv, xin, ps_m),
                   waits=(("mm", mm_bc),))

        def em_xn(c):
            nonlocal ev_xn
            xno = xn_sb[:, c * T + w * TW: c * T + (w + 1) * TW]
            dv = diff[:, (c % 2) * TW: (c % 2 + 1) * TW]
            if colsum_to is not None:
                cs = colsum_to[:, c * NW + w: c * NW + w + 1]
                ev_xn = s.emit(
                    "vector",
                    lambda e, dv=dv, xno=xno, cs=cs: e.scalar_tensor_tensor(
                        xno, dv, 1.0, ps_r, ALU.mult, ALU.mult, accum_out=cs),
                    inc="ev")
            else:
                ev_xn = s.emit(
                    "vector",
                    lambda e, dv=dv, xno=xno: e.scalar_tensor_tensor(
                        xno, dv, 1.0, ps_r, ALU.mult, ALU.mult),
                    inc="ev")

        em_diff(0)
        em_diff(1)
        for c in range(DC):
            s.drain("vector")
            em_xn(c)
            if c + 2 < DC:
                em_diff(c + 2)
        st["bc_read"] = ev_xn
        aux["xn_done"][w] = ev_xn

    stage_a(0)
    for w in range(NW):
        if w + 1 < NW:
            stage_a(w + 1)
        stage_b(w)



def build_fused():
    nc = bass.Bass("TRN2", target_bir_lowering=False)
    dr = {}

    def din(name, shape, dtype=F32):
        dr[name] = nc.dram_tensor(name, shape, dtype, kind="ExternalInput").ap()
        return dr[name]

    x_dr = din("x_fm", [D, T])
    wkT_dr = din("wkT", [D, D])
    wvT_dr = din("wvT", [D, D])
    bk_dr = din("bk", [1, D])
    bv_dr = din("bv", [1, D])
    alpha_dr = din("alpha", [1, 1])
    wgate_dr = din("wgate", [D, 8])
    bgate_dr = din("bgate", [1, 8])
    w1m_dr = din("W1m", [D, FFN_H])
    w2m_dr = din("W2m", [FFN_H, D])
    b2m_dr = din("b2m", [1, D])
    w1e_dr = din("W1e", [8 * D, FFN_H])
    b1e_dr = din("b1e", [8, FFN_H])
    w2e_dr = din("W2e", [8 * FFN_H, D])
    b2e_dr = din("b2e", [8, D])
    wq_dr = din("wqT", [D, D])
    wo_dr = din("woT", [D, D])
    boc_dr = din("bo_cols", [128, 8])
    wk1_dr = din("wk1T", [D, D])
    wv1_dr = din("wv1T", [D, D])
    bv1_dr = din("bv1", [1, D])
    pt_dr = din("PT", [64, 64])
    id_dr = din("ident", [128, 128])
    b1mc_dr = din("b1m_cols", [128, 32])
    bqc_dr = din("bq_cols", [128, 8])
    bk1c_dr = din("bk1_cols", [64, 16])
    wq1_dr = din("wq1T", [D, D])
    woP_dr = din("wo1P", [128, 8 * D])
    bo1c_dr = din("bo1_cols", [128, 8])
    bq1c_dr = din("bq1_cols", [64, 16])
    rw_dr = din("router_w", [D, 16])
    gw_dr = din("gate_w", [16 * D, F_EXP])
    uw_dr = din("up_w", [16 * D, F_EXP])
    dw_dr = din("down_w", [16 * F_EXP, D])
    sg_dr = din("sgate_w", [D, F_SH])
    su_dr = din("sup_w", [D, F_SH])
    sd_dr = din("sdown_w", [F_SH, D])

    out_dr = nc.dram_tensor("out_fm", [D, T], F32, kind="ExternalOutput").ap()

    cc_cs_i = nc.dram_tensor("cc_cs_i", [128, 8], F32)
    cc_cs_o = nc.dram_tensor("cc_cs_o", [128, 8], F32)
    cc_m_i = nc.dram_tensor("cc_m_i", [128, 1024], F32)
    cc_m_o = nc.dram_tensor("cc_m_o", [128, 1024], F32)
    cc_kv_i = nc.dram_tensor("cc_kv_i", [64, 16 * 65], F32)
    cc_kv_o = nc.dram_tensor("cc_kv_o", [64, 16 * 65], F32)
    x1_dram = nc.dram_tensor("x1b", [D, T], F32)
    x1_dr = x1_dram.ap()

    with ExitStack() as stack:
        s = Sched(nc, stack)
        a_sb = _alloc(stack, nc, "a_sb", [128, DC * T], F32)
        xn_sb = _alloc(stack, nc, "xn_sb", [128, DC * T], BF16)
        g_sb = _alloc(stack, nc, "g_sb", [128, DC * T], BF16)
        w_sb = _alloc(stack, nc, "w_sb", [128, DC * T], BF16)
        gf = g_sb[:].bitcast(F32)
        m32_sb = _alloc(stack, nc, "m32_sb", [128, 8 * 128], F32)
        m_sb = _alloc(stack, nc, "m_sb", [128, 8 * 128], BF16)
        o_sb = _alloc(stack, nc, "o_sb", [128, 8 * 512], BF16)
        of32 = o_sb[:].bitcast(F32)
        sq_q = _alloc(stack, nc, "sq_q", [128, 2 * 512], BF16)
        t_scr = _alloc(stack, nc, "t_scr", [128, 2 * 512], F32)
        brow_sb = _alloc(stack, nc, "brow_sb", [1, 4096], F32)
        bv_row_sb = _alloc(stack, nc, "bv_row", [1, D], F32)
        kvb_sb = _alloc(stack, nc, "kvb_sb", [64, 16 * 65], BF16)
        cs_w = _alloc(stack, nc, "cs_w", [128, DC * NW], F32)
        cs8 = _alloc(stack, nc, "cs8", [128, 8], F32)
        wg_sb = _alloc(stack, nc, "wg_sb", [128, 8 * 8], F32)
        cols_sb = _alloc(stack, nc, "cols_sb", [128, 16], F32)
        idx_sb = _alloc(stack, nc, "idx_sb", [128, 8], I32)
        bidx_sb = _alloc(stack, nc, "bidx_sb", [1, 2], I32)
        iota_sb = _alloc(stack, nc, "iota_sb", [128, 1], F32)
        cvec_sb = _alloc(stack, nc, "cvec_sb", [128, 8], F32)
        rrow_sb = _alloc(stack, nc, "rrow_sb", [1, 64], F32)
        b1mc_sb = _alloc(stack, nc, "b1mc_sb", [128, 32], F32)
        b1c_sb = _alloc(stack, nc, "b1c_sb", [128, 8], F32)
        bqc_sb = _alloc(stack, nc, "bqc_sb", [128, 8], F32)
        boc_sb = _alloc(stack, nc, "boc_sb", [128, 8], F32)
        bo1c_sb = _alloc(stack, nc, "bo1c_sb", [128, 8], F32)
        bk1c_sb = _alloc(stack, nc, "bk1c_sb", [64, 16], F32)
        bq1c_sb = _alloc(stack, nc, "bq1c_sb", [64, 16], F32)
        eps_ln = _alloc(stack, nc, "eps_ln", [1, 1], F32)
        eps_r6 = _alloc(stack, nc, "eps_r6", [1, 1], F32)
        eps_col = _alloc(stack, nc, "eps_col", [128, 1], F32)
        ones_row = _alloc(stack, nc, "ones_row", [1, TW], F32)
        ones_col = _alloc(stack, nc, "ones_col", [128, 1], F32)
        onesb_col = _alloc(stack, nc, "onesb_col", [128, 1], BF16)
        ptb_sb = _alloc(stack, nc, "ptb_sb", [64, 64], BF16)
        id_sb = _alloc(stack, nc, "id_sb", [128, 128], F32)
        ct_sb = _alloc(stack, nc, "ct_sb", [128, 16 * 16], F32)
        pr_sb = _alloc(stack, nc, "pr_sb", [128, 16], F32)
        rcol_sb = _alloc(stack, nc, "rcol_sb", [128, 4], F32)
        rw_sb = _alloc(stack, nc, "rw_sb", [128, 8 * 16], BF16)

        b1st_sb = gf[0:2, 4096:5120]
        b2st_sb = gf[0:2, 5120:6144]
        kvf_sb = gf[0:64, 7152:8192]
        crow_sb = brow_sb[:1, 2048:3072]

        ps = [_psum(stack, nc, f"ps{i}", [128, 512], F32) for i in range(8)]

        pre = s.new_sem("pre")
        preg = s.new_sem("preg")
        c1i = s.new_sem("c1i"); c1c = s.new_sem("c1c"); c1o = s.new_sem("c1o")
        c2i = s.new_sem("c2i"); c2c = s.new_sem("c2c"); c2o = s.new_sem("c2o")
        c3i = s.new_sem("c3i"); c3c = s.new_sem("c3c"); c3o = s.new_sem("c3o")
        w1s = [s.new_sem("w1A"), s.new_sem("w1B")]
        w2s = [s.new_sem("w2A"), s.new_sem("w2B")]
        b2ch = s.new_sem("b2ch")
        b2g = s.new_sem("b2g")
        dch = s.new_sem("dch")
        dchx = s.new_sem("dchx")
        dg = s.new_sem("dg")
        fch = s.new_sem("fch")
        fg = s.new_sem("fg")
        slch = s.new_sem("slch")
        d3ch = s.new_sem("d3ch")
        d3chx = s.new_sem("d3chx")
        d3g = s.new_sem("d3g")

        init = s.emit("vector", lambda e: [
            e.memset(ones_row[:], 1.0),
            e.memset(ones_col[:], 1.0),
            e.memset(onesb_col[:], 1.0),
            e.memset(eps_ln[:], LN_EPS),
            e.memset(eps_r6[:], 1e-6),
            e.memset(eps_col[:], 1e-6),
        ][-1], inc="ev")
        gp_iota = s.emit("gpsimd", lambda e: [
            e.iota(iota_sb[:, 0:1], [[0, 1]], channel_multiplier=1,
                   allow_small_or_imprecise_dtypes=True),
            e.iota(rrow_sb[:1, 48:56], [[1, 8]], channel_multiplier=0,
                   allow_small_or_imprecise_dtypes=True),
        ][-1], inc="gp")

        s.emit("sync", lambda e: e.dma_start(
            out=a_sb[:].rearrange("p (c t) -> p c t", c=DC),
            in_=x_dr.rearrange("(c p) t -> p c t", p=128)), inc=pre, dma=True)
        s.emit("gpsimd", lambda e: e.dma_start(
            out=w_sb[:, 0:8192].rearrange("p (c j) -> p c j", c=DC),
            in_=wkT_dr.rearrange("(c p) j -> p c j", p=128)), inc=preg, dma=True)
        s.emit("gpsimd", lambda e: e.dma_start(
            out=w_sb[:, 8192:16384].rearrange("p (c j) -> p c j", c=DC),
            in_=wvT_dr.rearrange("(c p) j -> p c j", p=128)), inc=preg, dma=True)
        s.emit("sync", lambda e: e.dma_start(out=brow_sb[:1, 3072:4096], in_=bk_dr),
               inc=pre, dma=True)
        s.emit("sync", lambda e: e.dma_start(out=bv_row_sb[:], in_=bv_dr),
               inc=pre, dma=True)
        s.emit("sync", lambda e: e.dma_start(
            out=wg_sb[:].rearrange("p (c g) -> p c g", c=DC),
            in_=wgate_dr.rearrange("(c p) g -> p c g", p=128)), inc=pre, dma=True)
        s.emit("sync", lambda e: e.dma_start(out=rrow_sb[:1, 32:40], in_=bgate_dr),
               inc=pre, dma=True)
        s.emit("sync", lambda e: e.dma_start(out=rrow_sb[:1, 40:41], in_=alpha_dr),
               inc=pre, dma=True)
        s.emit("sync", lambda e: e.dma_start(out=b1mc_sb[:], in_=b1mc_dr),
               inc=pre, dma=True)
        s.emit("sync", lambda e: e.dma_start(out=bqc_sb[:], in_=bqc_dr),
               inc=pre, dma=True)
        s.emit("sync", lambda e: e.dma_start(out=boc_sb[:], in_=boc_dr),
               inc=pre, dma=True)
        s.emit("sync", lambda e: e.dma_start(out=bo1c_sb[:], in_=bo1c_dr),
               inc=pre, dma=True)
        s.emit("sync", lambda e: e.dma_start(out=bk1c_sb[:], in_=bk1c_dr),
               inc=pre, dma=True)
        s.emit("sync", lambda e: e.dma_start(out=bq1c_sb[:], in_=bq1c_dr),
               inc=pre, dma=True)
        s.emit("sync", lambda e: e.dma_start(out=id_sb[:], in_=id_dr),
               inc=pre, dma=True)
        s.emit("gpsimd", lambda e: e.dma_start(out=ptb_sb[:], in_=pt_dr),
               inc=preg, dma=True)
        s.emit("gpsimd", lambda e: e.dma_start(
            out=rw_sb[:].rearrange("p (c g) -> p c g", c=DC),
            in_=rw_dr.rearrange("(c p) g -> p c g", p=128)), inc=preg, dma=True)
        pre_t = s.c(pre)
        preg_t = s.c(preg)

        aux0 = dict(
            eps_row=eps_ln[:1, :1],
            ps_stat0=ps[0][0:1, :], ps_stat1=ps[1][0:1, :],
            ps_bc0=ps[2][:], ps_bc1=ps[3][:],
            sq_scr=of32[:, 0:1024], diff_scr=of32[:, 1024:2048],
            rows=brow_sb[:1, 0:3072],
            ones_col=ones_col[:], ones_row=ones_row,
            x_ready=(pre, pre_t),
            pre_waits=(("ev", init),),
            xn_done=[0] * NW,
        )
        emit_layernorm(s, a_sb, xn_sb, aux0, colsum_to=cs_w)
        ln0_done = aux0["xn_done"]

        ev_cs = 0
        for c in range(DC):
            ev_cs = s.emit("vector", lambda e, c=c: e.reduce_sum(
                cs8[:, c:c + 1],
                cs_w[:].rearrange("p (c w) -> p c w", c=DC)[:, c, :],
                axis=AX.X), inc="ev" if c == DC - 1 else None)
        s.emit("sync", lambda e: e.dma_start(out=cc_cs_i[:, :], in_=cs8[:]),
               waits=(("ev", ev_cs),), inc=c1i, dma=True)
        s.emit("gpsimd", lambda e: e.collective_compute(
            "AllReduce", ALU.add, replica_groups=RG,
            ins=[cc_cs_i.ap().opt()], outs=[cc_cs_o.ap().opt()]),
            waits=((c1i, 16),), inc=c1c, dma=True, inc_n=1)
        s.emit("sync", lambda e: e.dma_start(out=cs8[:], in_=cc_cs_o[:, :]),
               waits=((c1c, 1),), inc=c1o, dma=True)
        c1o_t = s.c(c1o)

        kg_sb = g_sb[:, 0:8192]
        vg_sb = g_sb[:, 8192:16384]
        wk_v = w_sb[:, 0:8192]
        wv_v = w_sb[:, 8192:16384]
        bk_row = brow_sb[:1, 3072:4096]
        evk_gate = [0, 0]
        scv_last = 0
        kv_used = 0
        ev_m = 0
        m_ps_free = 0
        mm_p1w = 0
        ev_kg_last = 0
        scv_p1_last = 0
        for g in range(2):
            ev_k = sc_v = 0
            for tt in range(NT):
                slot = tt % 2
                kg = kg_sb[:, tt * 512:(tt + 1) * 512]
                vg = vg_sb[:, tt * 512:(tt + 1) * 512]
                kgp = ps[4 + slot][:, 0:512]
                vgp = ps[6][:, 0:512]
                w = tt // (NT // NW)
                xw = ("ev", ln0_done[w])

                def proj(e, g=g, tt=tt, kgp=kgp, vgp=vgp):
                    for c in range(DC):
                        xsl = xn_sb[:, c * T + tt * 128: c * T + (tt + 1) * 128]
                        wks = wk_v[:, c * D + g * 512: c * D + (g + 1) * 512]
                        wvs = wv_v[:, c * D + g * 512: c * D + (g + 1) * 512]
                        e.matmul(kgp, xsl, wks, start=(c == 0), stop=False)
                        e.matmul(vgp, xsl, wvs, start=(c == 0), stop=False)
                    e.matmul(kgp, ones_row[:1, 0:128],
                             bk_row[:1, g * 512:(g + 1) * 512], start=False, stop=True)
                    return e.matmul(vgp, ones_row[:1, 0:128],
                                    bv_row_sb[:1, g * 512:(g + 1) * 512],
                                    start=False, stop=True)
                mm_proj = s.emit(
                    "tensor", proj,
                    waits=(xw, (pre, pre_t), (preg, preg_t),
                           ("ev", evk_gate[slot]), ("sc", scv_last),
                           ("mm", kv_used)),
                    inc="mm")
                mm_p1w = mm_proj
                ev_k = s.emit("vector",
                              lambda e, kg=kg, kgp=kgp: e.tensor_copy(kg, kgp),
                              waits=(("mm", mm_proj),), inc="ev")
                sc_v = s.emit("scalar",
                              lambda e, vg=vg, vgp=vgp: e.copy(vg, vgp),
                              waits=(("mm", mm_proj),), inc="sc")
                evk_gate[slot] = ev_k
                scv_last = sc_v
            ev_kg_last = ev_k
            scv_p1_last = sc_v
            mm_m = 0
            for hh in range(4):
                def macc(e, hh=hh):
                    last = None
                    for tt in range(NT):
                        last = e.matmul(
                            ps[7][:, hh * 128:(hh + 1) * 128],
                            kg_sb[:, tt * 512 + hh * 128: tt * 512 + (hh + 1) * 128],
                            vg_sb[:, tt * 512 + hh * 128: tt * 512 + (hh + 1) * 128],
                            start=(tt == 0), stop=(tt == NT - 1))
                    return last
                mm_m = s.emit("tensor", macc,
                              waits=(("ev", ev_k), ("sc", sc_v),
                                     ("ev", m_ps_free)), inc="mm")
            for hh in range(4):
                h = g * 4 + hh
                ev_m = s.emit("vector", lambda e, h=h, hh=hh: e.tensor_copy(
                    m32_sb[:, h * 128:(h + 1) * 128], ps[7][:, hh * 128:(hh + 1) * 128]),
                    waits=(("mm", mm_m),), inc="ev" if hh == 3 else None)
            kv_used = mm_m
            m_ps_free = ev_m
        mm_p1g = kv_used

        s.emit("sync", lambda e: e.dma_start(out=cc_m_i[:, :], in_=m32_sb[:]),
               waits=(("ev", ev_m),), inc=c2i, dma=True)

        def emit_c2_collective():
            s.emit("gpsimd", lambda e: e.collective_compute(
                "AllReduce", ALU.add, replica_groups=RG,
                ins=[cc_m_i.ap().opt()], outs=[cc_m_o.ap().opt()]),
                waits=((c2i, 16),), inc=c2c, dma=True, inc_n=1)
        s.emit("sync", lambda e: e.dma_start(out=m32_sb[:], in_=cc_m_o[:, :]),
               waits=((c2c, 1),), inc=c2o, dma=True)
        c2o_t = s.c(c2o)

        R = rrow_sb
        lg = R[:1, 0:8]
        scr = R[:1, 8:16]
        probs = R[:1, 16:24]
        scr2 = R[:1, 24:32]
        bgate = R[:1, 32:40]
        iota8 = R[:1, 48:56]
        mm_lg = s.emit("tensor", lambda e: [
            e.matmul(ps[0][0:1, 0:8], cs8[:, c:c + 1], wg_sb[:, c * 8:(c + 1) * 8],
                     start=(c == 0), stop=(c == DC - 1))
            for c in range(DC)][-1],
            waits=((c1o, c1o_t), (pre, pre_t), ("ev", ev_cs)), inc="mm")

        def rt(e):
            e.tensor_scalar(lg, ps[0][0:1, 0:8], 1.0 / 16384.0, None, ALU.mult)
            e.drain()
            e.tensor_add(lg, lg, bgate)
            e.drain()
            e.reduce_max(R[:1, 41:42], lg, axis=AX.X)
            e.drain()
            return e.tensor_scalar(R[:1, 42:43], R[:1, 41:42], -1.0, None, ALU.mult)
        ev_rt1 = s.emit("vector", rt, waits=(("mm", mm_lg), (pre, pre_t),
                                             ("ev", init)), inc="ev")
        sc_exp = s.emit(
            "scalar",
            lambda e: e.activation(scr, lg, AF.Exp, bias=R[:1, 42:43]),
            waits=(("ev", ev_rt1),), inc="sc")

        def rt2(e):
            e.reduce_sum(R[:1, 44:45], scr, axis=AX.X)
            e.drain()
            e.reciprocal(R[:1, 45:46], R[:1, 44:45])
            e.drain()
            e.tensor_scalar(probs, scr, R[:1, 45:46], None, ALU.mult)
            e.drain()
            e.reduce_max(R[:1, 56:57], probs, axis=AX.X)
            e.drain()
            e.tensor_scalar(scr2, probs, R[:1, 56:57], None, ALU.is_ge)
            e.drain()
            e.tensor_scalar(scr, scr2, -9.0, 9.0, ALU.mult, ALU.add)
            e.drain()
            e.tensor_add(scr, scr, iota8)
            e.drain()
            e.tensor_reduce(R[:1, 57:58], scr, AX.X, ALU.min)
            e.drain()
            e.scalar_tensor_tensor(scr, scr2, -2.0, probs, ALU.mult, ALU.add)
            e.drain()
            e.reduce_max(R[:1, 58:59], scr, axis=AX.X)
            e.drain()
            e.tensor_scalar(scr2, scr, R[:1, 58:59], None, ALU.is_ge)
            e.drain()
            e.tensor_scalar(scr2, scr2, -9.0, 9.0, ALU.mult, ALU.add)
            e.drain()
            e.tensor_add(scr2, scr2, iota8)
            e.drain()
            e.tensor_reduce(R[:1, 59:60], scr2, AX.X, ALU.min)
            e.drain()
            e.tensor_add(R[:1, 60:61], R[:1, 56:57], R[:1, 58:59])
            e.drain()
            e.reciprocal(R[:1, 61:62], R[:1, 60:61])
            e.drain()
            e.tensor_mul(R[:1, 62:63], R[:1, 56:57], R[:1, 61:62])
            e.tensor_mul(R[:1, 63:64], R[:1, 58:59], R[:1, 61:62])
            e.drain()
            e.tensor_scalar(R[:1, 8:9], R[:1, 40:41], 0.1, 1.0, ALU.max, ALU.min)
            e.drain()
            e.tensor_scalar(R[:1, 24:25], R[:1, 8:9], -1.0, 1.0, ALU.mult, ALU.add)
            e.drain()
            e.tensor_mul(R[:1, 9:10], R[:1, 24:25], R[:1, 62:63])
            e.tensor_mul(R[:1, 10:11], R[:1, 24:25], R[:1, 63:64])
            e.tensor_scalar(R[:1, 11:12], R[:1, 57:58], 1024.0, None, ALU.mult)
            e.tensor_scalar(R[:1, 12:13], R[:1, 59:60], 1024.0, None, ALU.mult)
            e.tensor_scalar(R[:1, 13:14], R[:1, 57:58], 4096.0, None, ALU.mult)
            e.tensor_scalar(R[:1, 14:15], R[:1, 59:60], 4096.0, None, ALU.mult)
            e.tensor_copy(R[:1, 15:16], R[:1, 57:58])
            e.tensor_copy(R[:1, 16:17], R[:1, 59:60])
            e.drain()
            e.tensor_copy(bidx_sb[:1, 0:1], R[:1, 57:58])
            return e.tensor_copy(bidx_sb[:1, 1:2], R[:1, 59:60])
        ev_rt2 = s.emit("vector", rt2, waits=(("sc", sc_exp), ("gp", gp_iota)),
                        inc="ev")
        mm_bcr = s.emit("tensor", lambda e: e.matmul(
            ps[2][:, 0:9], ones_row[:1, 0:128], R[:1, 8:17], start=True, stop=True),
            waits=(("ev", ev_rt2),), inc="mm")

        def mkcols(e):
            e.tensor_copy(cols_sb[:, 0:9], ps[2][:, 0:9])
            e.drain()
            e.tensor_scalar(cols_sb[:, 9:13], cols_sb[:, 3:7], iota_sb[:, 0:1],
                            None, ALU.add)
            e.drain()
            e.tensor_copy(idx_sb[:, 0:4], cols_sb[:, 9:13])
            return e.tensor_copy(idx_sb[:, 4:6], cols_sb[:, 7:9])
        routing_done = s.emit("vector", mkcols,
                              waits=(("mm", mm_bcr), ("gp", gp_iota)), inc="ev")
        a_col = cols_sb[:, 0:1]
        c1_col = cols_sb[:, 1:2]
        c2_col = cols_sb[:, 2:3]

        s.emit("sync", lambda e: e.dma_start(
            out=brow_sb[:1, D:2 * D], in_=b2m_dr),
            waits=(("mm", mm_bcr),), inc=b2ch, dma=True)
        s.emit("gpsimd", lambda e: e.indirect_dma_start(
            out=b2st_sb[0:2, :], out_offset=None, in_=b2e_dr,
            in_offset=IndirectOffsetOnAxis(idx_sb[0:2, 4:5], 0)),
            waits=(("ev", routing_done), ("mm", mm_p1g)), inc=b2g, dma=True)
        wt_b2a = s.c(b2g)
        ev_b2a = s.emit("vector", lambda e: e.tensor_copy(
            brow_sb[:1, 2 * D:3 * D], b2st_sb[0:1, :]),
            waits=((b2g, wt_b2a), ("mm", mm_bcr)), inc="ev")
        s.emit("gpsimd", lambda e: e.indirect_dma_start(
            out=b2st_sb[0:2, :], out_offset=None, in_=b2e_dr,
            in_offset=IndirectOffsetOnAxis(idx_sb[0:2, 5:6], 0)),
            waits=(("ev", ev_b2a),), inc=b2g, dma=True)
        wt_b2b = s.c(b2g)
        ev_b2b = s.emit("vector", lambda e: e.tensor_copy(
            brow_sb[:1, 3 * D:4 * D], b2st_sb[0:1, :]),
            waits=((b2g, wt_b2b), (b2ch, s.c(b2ch)), ("mm", mm_p1w)), inc="ev")

        def cvec(e):
            cv = brow_sb[:1, 0:D]
            tv = brow_sb[:1, D:2 * D]
            e.tensor_scalar(cv, brow_sb[:1, D:2 * D], R[:1, 8:9], None, ALU.mult)
            e.drain()
            e.tensor_scalar(tv, brow_sb[:1, 2 * D:3 * D], R[:1, 9:10], None, ALU.mult)
            e.drain()
            e.tensor_add(cv, cv, tv)
            e.drain()
            e.tensor_scalar(tv, brow_sb[:1, 3 * D:4 * D], R[:1, 10:11], None, ALU.mult)
            e.drain()
            return e.tensor_add(cv, cv, tv)
        ev_cvec = s.emit("vector", cvec, waits=(("ev", ev_b2b),), inc="ev")

        slab = 0
        w1_freed = [mm_p1w, mm_p1w]
        w2_freed = [mm_p1w, mm_p1w]
        b1_freed = [mm_p1g, mm_p1g]
        hid_ps_gate = [0, 0]
        b1c_gate = [scv_p1_last, scv_p1_last]
        o_ps_gate = [0, 0]
        hid_used = mm_p1g
        sc_gelu = 0
        ev_ffn = 0
        first_hid = [True, True]
        first_out = [True, True]
        acc_last = {}
        hid_w_last = {}

        for p in range(3):
            sc_col = (a_col, c1_col, c2_col)[p]
            for hc in range(NHC):
                sl = slab % 2
                w1v = w_sb[:, sl * 4096:(sl + 1) * 4096]
                w2v = w_sb[:, 8192 + sl * 4096: 8192 + (sl + 1) * 4096]
                b1v = b1st_sb[0:1, sl * HC:(sl + 1) * HC]
                if p == 0:
                    s.emit("gpsimd", lambda e, hc=hc, w1v=w1v: e.dma_start(
                        out=w1v.rearrange("p (c h) -> p c h", c=DC),
                        in_=w1m_dr.rearrange("(c p) h -> p c h", p=128)
                        [:, :, hc * HC:(hc + 1) * HC]),
                        waits=(("mm", w1_freed[sl]),), inc=w1s[sl], dma=True)
                    wt1 = (w1s[sl], s.c(w1s[sl]))
                    s.emit("gpsimd", lambda e, hc=hc, w2v=w2v: e.dma_start(
                        out=w2v.rearrange("p (q f) -> p q f", q=4),
                        in_=w2m_dr.rearrange("(q p) f -> p q f", p=128)
                        [:, hc * 4:(hc + 1) * 4, :]),
                        waits=(("mm", w2_freed[sl]),), inc=w2s[sl], dma=True)
                    wt2 = (w2s[sl], s.c(w2s[sl]))
                    bt = (pre, pre_t)
                    if hc == 0:
                        emit_c2_collective()
                else:
                    iw1 = idx_sb[:, (p - 1):p]
                    iw2 = idx_sb[:, (1 + p):(2 + p)]
                    for c in range(DC):
                        s.emit("gpsimd", lambda e, c=c, hc=hc, w1v=w1v, iw1=iw1: e.indirect_dma_start(
                            out=w1v.rearrange("p (c h) -> p c h", c=DC)[:, c, :],
                            out_offset=None, in_=w1e_dr,
                            in_offset=IndirectOffsetOnAxis(iw1, 0),
                            element_offset=(c * 128) * FFN_H + hc * HC),
                            waits=(("mm", w1_freed[sl]), ("ev", routing_done)),
                            inc=w1s[sl], dma=True)
                    s.emit("gpsimd", lambda e, p=p, hc=hc, sl=sl: e.indirect_dma_start(
                        out=b1st_sb[0:2, sl * HC:(sl + 1) * HC], out_offset=None,
                        in_=b1e_dr,
                        in_offset=IndirectOffsetOnAxis(idx_sb[0:2, (3 + p):(4 + p)], 0),
                        element_offset=hc * HC),
                        waits=(("ev", routing_done), ("mm", b1_freed[sl])),
                        inc=w1s[sl], dma=True)
                    wt1 = (w1s[sl], s.c(w1s[sl]))
                    bt = wt1
                    for qq in range(4):
                        s.emit("gpsimd", lambda e, qq=qq, hc=hc, w2v=w2v, iw2=iw2: e.indirect_dma_start(
                            out=w2v.rearrange("p (q f) -> p q f", q=4)[:, qq, :],
                            out_offset=None, in_=w2e_dr,
                            in_offset=IndirectOffsetOnAxis(iw2, 0),
                            element_offset=(hc * 4 + qq) * 128 * D),
                            waits=(("mm", w2_freed[sl]), ("ev", routing_done)),
                            inc=w2s[sl], dma=True)
                    wt2 = (w2s[sl], s.c(w2s[sl]))
                if p == 0:
                    b1cols = b1mc_sb[:, hc * 4: hc * 4 + 4]
                    b1_ready = bt
                else:
                    mm_b1t = s.emit("tensor", lambda e, b1v=b1v: [
                        e.transpose(ps[6][:, 8 + it: 9 + it],
                                    b1v[:1, it * 128:(it + 1) * 128], ones_row[:1, 0:1])
                        for it in range(4)][-1],
                        waits=(bt, (pre, pre_t), ("sc", b1c_gate[sl])),
                        inc="mm")
                    ev_b1c = s.emit("vector", lambda e, sl=sl: e.tensor_copy(
                        b1c_sb[:, sl * 4:(sl + 1) * 4], ps[6][:, 8:12]),
                        waits=(("mm", mm_b1t),), inc="ev")
                    b1cols = b1c_sb[:, sl * 4: sl * 4 + 4]
                    b1_ready = ("ev", ev_b1c)
                for q in range(4):
                    for w in range(NW):
                        pr = (q * NW + w) % 2
                        pi = ps[pr]
                        def hmm(e, q=q, w=w, pi=pi, w1v=w1v):
                            last = None
                            for c in range(DC):
                                last = e.matmul(
                                    pi[:],
                                    w1v[:, c * HC + q * 128: c * HC + (q + 1) * 128],
                                    xn_sb[:, c * T + w * TW: c * T + (w + 1) * TW],
                                    start=(c == 0), stop=(c == DC - 1))
                            return last
                        extra = ()
                        if first_hid[pr]:
                            extra = (("ev", routing_done),)
                            first_hid[pr] = False
                        mm_h = s.emit("tensor", hmm,
                                      waits=(wt1, ("ev", ln0_done[w]),
                                             ("sc", hid_ps_gate[pr]), *extra),
                                      inc="mm")
                        hv = g_sb[:, q * T + w * TW: q * T + (w + 1) * TW]
                        sc_gelu = s.emit(
                            "scalar",
                            lambda e, hv=hv, pi=pi, b1cols=b1cols, q=q:
                            e.activation(hv, pi[:], AF.Gelu_apprx_tanh,
                                         bias=b1cols[:, q:q + 1]),
                            waits=(("mm", mm_h), ("mm", hid_used), b1_ready),
                            inc="sc")
                        hid_ps_gate[pr] = sc_gelu
                if p != 0:
                    b1c_gate[sl] = sc_gelu
                mm_o_last = 0
                for k in range(16):
                    fc = k // 2
                    w0 = (k % 2) * 2
                    j = k % 2
                    poA = ps[2 + 2 * j]
                    poB = ps[3 + 2 * j]
                    def omm2(e, fc=fc, w0=w0, poA=poA, poB=poB, w2v=w2v):
                        last = None
                        for po, w in ((poA, w0), (poB, w0 + 1)):
                            for q in range(4):
                                last = e.matmul(
                                    po[:],
                                    w2v[:, q * 1024 + fc * 128: q * 1024 + (fc + 1) * 128],
                                    g_sb[:, q * T + w * TW: q * T + (w + 1) * TW],
                                    start=(q == 0), stop=(q == 3))
                        return last
                    extra = ()
                    if first_out[j]:
                        extra = (("ev", routing_done),)
                        first_out[j] = False
                    mm_o = s.emit("tensor", omm2,
                                  waits=(wt2, ("sc", sc_gelu),
                                         ("ev", o_ps_gate[j]), *extra),
                                  inc="mm")
                    for po, w in ((poA, w0), (poB, w0 + 1)):
                        av = a_sb[:, fc * T + w * TW: fc * T + (w + 1) * TW]
                        if p == 0 and hc == 0:
                            ev_ffn = s.emit(
                                "vector",
                                lambda e, av=av, po=po: e.tensor_scalar(
                                    av, po[:], a_col, None, ALU.mult),
                                waits=(("mm", mm_o), ("ev", routing_done)),
                                inc="ev")
                        else:
                            ev_ffn = s.emit(
                                "vector",
                                lambda e, av=av, po=po, sc_col=sc_col: e.scalar_tensor_tensor(
                                    av, po[:], sc_col, av, ALU.mult, ALU.add),
                                waits=(("mm", mm_o),), inc="ev")
                        if p == 2 and hc == NHC - 1:
                            acc_last[(fc, w)] = ev_ffn
                            if fc == DC - 1 and w == 3:
                                for ww in range(NW):
                                    hid_w_last[ww] = mm_o
                    o_ps_gate[j] = ev_ffn
                    mm_o_last = mm_o
                w1_freed[sl] = s.c("mm")
                w2_freed[sl] = s.c("mm")
                b1_freed[sl] = s.c("mm")
                hid_used = mm_o_last
                slab += 1
        ffn_done_ev = s.c("ev")

        mm_cvt = s.emit("tensor", lambda e: [
            e.transpose(ps[6][:, fc:fc + 1], brow_sb[:1, fc * 128:(fc + 1) * 128],
                        ones_row[:1, 0:1])
            for fc in range(DC)][-1],
            waits=(("ev", ev_cvec), ("sc", scv_p1_last), (pre, pre_t)), inc="mm")
        ev_cvc = s.emit("vector",
                        lambda e: e.tensor_copy(cvec_sb[:, 0:8], ps[6][:, 0:8]),
                        waits=(("mm", mm_cvt),), inc="ev")
        sil_done = 0
        for fc in range(DC):
            for w in range(NW):
                gv = g_sb[:, fc * T + w * TW: fc * T + (w + 1) * TW]
                av = a_sb[:, fc * T + w * TW: fc * T + (w + 1) * TW]
                sil_done = s.emit(
                    "scalar",
                    lambda e, gv=gv, av=av, fc=fc: e.activation(
                        gv, av, AF.Silu, bias=cvec_sb[:, fc:fc + 1]),
                    waits=(("ev", ev_cvc), ("ev", acc_last[(fc, w)]),
                           ("mm", hid_w_last[w] if fc < 4 else mm_p1g)),
                    inc="sc")

        dgq = s.new_sem("dgq")
        for hh in range(2):
            s.emit("gpsimd", lambda e, hh=hh: e.dma_start(
                out=w_sb[:, hh * 4096:(hh + 1) * 4096].rearrange(
                    "p (c j) -> p c j", c=4),
                in_=wq_dr[hh * 512:(hh + 1) * 512, :].rearrange(
                    "(c p) j -> p c j", p=128)),
                waits=(("mm", w1_freed[hh]),), inc=dgq, dma=True)
            s.emit("gpsimd", lambda e, hh=hh: e.dma_start(
                out=w_sb[:, 8192 + hh * 4096: 8192 + (hh + 1) * 4096].rearrange(
                    "p (c j) -> p c j", c=4),
                in_=wo_dr[hh * 512:(hh + 1) * 512, :].rearrange(
                    "(c p) j -> p c j", p=128)),
                waits=(("mm", w2_freed[hh]),), inc=dg, dma=True)
        dgq_t = s.c(dgq)
        s.emit("sync", lambda e: e.dma_start(
            out=a_sb[:].rearrange("p (c t) -> p c t", c=DC),
            in_=x_dr.rearrange("(c p) t -> p c t", p=128)),
            waits=(("sc", sil_done),), inc=dchx, dma=True)
        dch_t = s.c(dch)
        dchx_t = s.c(dchx)
        dg_t = s.c(dg)

        ev_mbf = s.emit("vector", lambda e: e.tensor_copy(m_sb[:], m32_sb[:]),
                        waits=((c2o, c2o_t),), inc="ev")

        x1_done = 0
        o_used = 0
        q_used = [0, 0]
        y_evac = [ffn_done_ev, ffn_done_ev]
        for w in range(NW):
            sc_o = 0
            for h in range(8):
                pq = ps[h % 2]
                def qmm(e, h=h, w=w, pq=pq):
                    last = None
                    for c in range(DC):
                        last = e.matmul(
                            pq[:],
                            w_sb[:, c * 1024 + h * 128: c * 1024 + (h + 1) * 128],
                            xn_sb[:, c * T + w * TW: c * T + (w + 1) * TW],
                            start=(c == 0), stop=(c == DC - 1))
                    return last
                mm_q = s.emit("tensor", qmm,
                              waits=((dch, dch_t), (dgq, dgq_t),
                                     ("sc", sil_done),
                                     ("mm", q_used[h % 2])),
                              inc="mm")
                qv = sq_q[:, (h % 2) * 512:(h % 2 + 1) * 512]
                ev_q = s.emit("scalar",
                              lambda e, qv=qv, pq=pq, h=h: e.activation(
                                  qv, pq[:], AF.Identity, bias=bqc_sb[:, h:h + 1]),
                              waits=(("mm", mm_q),), inc="sc")
                po = ps[2 + h % 2]
                mm_oh = s.emit("tensor",
                               lambda e, h=h, qv=qv, po=po: e.matmul(
                                   po[:], m_sb[:, h * 128:(h + 1) * 128], qv,
                                   start=True, stop=True),
                               waits=(("sc", ev_q), ("ev", ev_mbf),
                                      ("sc", sc_o), ("mm", o_used)), inc="mm")
                q_used[h % 2] = mm_oh
                ov = o_sb[:, h * 512:(h + 1) * 512]
                sc_o = s.emit("scalar",
                              lambda e, ov=ov, po=po: e.copy(ov, po[:]),
                              waits=(("mm", mm_oh),), inc="sc")
            for ic in range(DC):
                pr = ic % 2
                py = ps[4 + pr]
                def ymm(e, ic=ic, py=py):
                    last = None
                    for cc in range(DC):
                        last = e.matmul(
                            py[:],
                            w_sb[:, 8192 + cc * 1024 + ic * 128:
                                 8192 + cc * 1024 + (ic + 1) * 128],
                            o_sb[:, cc * 512:(cc + 1) * 512],
                            start=(cc == 0), stop=(cc == DC - 1))
                    return last
                mm_y = s.emit("tensor", ymm,
                              waits=((dch, dch_t), (dg, dg_t), ("sc", sc_o),
                                     ("ev", y_evac[pr])), inc="mm")
                o_used = mm_y
                tv = t_scr[:, pr * 512:(pr + 1) * 512]
                gv = g_sb[:, ic * T + w * TW: ic * T + (w + 1) * TW]
                av = a_sb[:, ic * T + w * TW: ic * T + (w + 1) * TW]
                y_evac[pr] = s.emit(
                    "vector",
                    lambda e, tv=tv, gv=gv, py=py, ic=ic: e.scalar_tensor_tensor(
                        tv, py[:], boc_sb[:, ic:ic + 1], gv, ALU.add, ALU.mult),
                    waits=(("mm", mm_y),), inc="ev")
                s.drain("vector")
                x1_done = s.emit("vector",
                                 lambda e, av=av, tv=tv: e.tensor_add(av, av, tv),
                                 waits=((dchx, dchx_t),), inc="ev")
        st_x1 = s.emit("sync", lambda e: e.dma_start(
            out=x1_dr.rearrange("(c p) t -> p c t", p=128),
            in_=a_sb[:].rearrange("p (c t) -> p c t", c=DC)),
            waits=(("ev", x1_done),), inc="st", dma=True)
        st_x1_t = s.c("st")

        aux1 = dict(
            eps_row=eps_ln[:1, :1],
            ps_stat0=ps[0][0:1, :], ps_stat1=ps[1][0:1, :],
            ps_bc0=ps[2][:], ps_bc1=ps[3][:],
            sq_scr=gf[:, 0:1024], diff_scr=gf[:, 1024:2048],
            rows=gf[0:1, 2048:5120],
            ones_col=ones_col[:], ones_row=ones_row,
            x_ready=("ev", x1_done),
            pre_waits=(("ev", x1_done),),
            xn_done=[0] * NW,
        )
        emit_layernorm(s, a_sb, xn_sb, aux1, colsum_to=None)
        xn1_done = aux1["xn_done"]

        mm_all_d = s.c("mm")
        fgk = s.new_sem("fgk")
        s.emit("gpsimd", lambda e: e.dma_start(
            out=w_sb[:, 0:8192].rearrange("p (c j) -> p c j", c=DC),
            in_=wk1_dr.rearrange("(c p) j -> p c j", p=128)),
            waits=(("mm", max(q_used)),), inc=fgk, dma=True)
        s.emit("gpsimd", lambda e: e.dma_start(
            out=w_sb[:, 8192:16384].rearrange("p (c j) -> p c j", c=DC),
            in_=wv1_dr.rearrange("(c p) j -> p c j", p=128)),
            waits=(("mm", mm_all_d),), inc=fg, dma=True)
        s.emit("sync",
               lambda e: e.dma_start(out=brow_sb[:1, 1024:2048], in_=bv1_dr),
               waits=(("mm", mm_all_d),), inc=fch, dma=True)
        fch_t = s.c(fch)
        fg_t = s.c(fg)
        fgk_t = s.c(fgk)
        bv1_row = brow_sb[:1, 1024:2048]
        v_all = o_sb[:, 0:2048]
        kf_all = o_sb[:, 2048:3072]
        kv_sb = kvf_sb
        ln1_all = ("ev", xn1_done[NW - 1])

        kv_done = 0
        kv_done_par = [0, 0]
        v_used = 0
        kf_used = 0
        pv_gate = [0, 0]
        pk_gate = [0, 0]
        pf_gate = [0, 0]
        first_kv = True
        sc_bvt = 0
        ev_v_prev = x1_done
        bvt_sb = t_scr[:, 0:128]
        for hp in range(8):
            mm_bv = s.emit("tensor", lambda e, hp=hp: e.matmul(
                ps[7][:, 0:128], ones_row[:1, 0:128],
                bv1_row[:1, hp * 128:(hp + 1) * 128], start=True, stop=True),
                waits=((fch, fch_t), ("sc", sc_bvt), ("ev", ev_m)), inc="mm")
            sc_bvt = s.emit("scalar", lambda e: e.copy(bvt_sb, ps[7][:, 0:128]),
                            waits=(("mm", mm_bv), ("ev", ev_v_prev)), inc="sc")
            ev_v = 0
            for tt in range(NT):
                pv = ps[2 + tt % 2]
                def vmm(e, hp=hp, tt=tt, pv=pv):
                    last = None
                    for c in range(DC):
                        last = e.matmul(
                            pv[:, 0:128],
                            xn_sb[:, c * T + tt * 128: c * T + (tt + 1) * 128],
                            w_sb[:, 8192 + c * 1024 + hp * 128:
                                 8192 + c * 1024 + (hp + 1) * 128],
                            start=(c == 0), stop=(c == DC - 1))
                    return last
                mm_v = s.emit("tensor", vmm,
                              waits=((fg, fg_t), ln1_all,
                                     ("mm", v_used), ("ev", pv_gate[tt % 2])),
                              inc="mm")
                ev_v = s.emit("vector", lambda e, tt=tt, pv=pv: e.tensor_add(
                    v_all[:, tt * 128:(tt + 1) * 128], pv[:, 0:128], bvt_sb),
                    waits=(("mm", mm_v), ("mm", v_used), ("sc", sc_bvt)),
                    inc="ev")
                pv_gate[tt % 2] = ev_v
                ev_v_prev = ev_v
            for half in range(2):
                h = 2 * hp + half
                ev_kf = 0
                for w in range(NW):
                    pk = ps[w % 2]
                    def kfmm(e, h=h, w=w, pk=pk):
                        last = None
                        for c in range(DC):
                            last = e.matmul(
                                pk[0:64, :],
                                w_sb[:, c * 1024 + h * 64: c * 1024 + (h + 1) * 64],
                                xn_sb[:, c * T + w * TW: c * T + (w + 1) * TW],
                                start=(c == 0), stop=(c == DC - 1))
                        return last
                    mm_kf = s.emit("tensor", kfmm,
                                   waits=((fgk, fgk_t), ln1_all,
                                          ("mm", kf_used), ("sc", pk_gate[w % 2])),
                                   inc="mm")
                    kfv = sq_q[0:64, (w % 2) * 512:(w % 2 + 1) * 512]
                    ev_kfm = s.emit("scalar",
                                    lambda e, kfv=kfv, pk=pk, h=h: e.activation(
                                        kfv, pk[0:64, :], AF.Identity,
                                        bias=bk1c_sb[:, h:h + 1]),
                                    waits=(("mm", mm_kf),), inc="sc")
                    pk_gate[w % 2] = ev_kfm
                    for it in range(4):
                        tt = w * 4 + it
                        pf = ps[4 + tt % 2]
                        mm_f = s.emit(
                            "tensor",
                            lambda e, it=it, kfv=kfv, pf=pf: e.matmul(
                                pf[:, 0:64], kfv[:, it * 128:(it + 1) * 128],
                                ptb_sb[0:64, :], start=True, stop=True),
                            waits=(("sc", ev_kfm),
                                   ("sc", pf_gate[tt % 2])), inc="mm")
                        ev_kf = s.emit(
                            "scalar", lambda e, tt=tt, pf=pf: e.activation(
                                kf_all[:, tt * 64:(tt + 1) * 64], pf[:, 0:64],
                                AF.Relu, scale=L1_NORM, bias=eps_col[:, 0:1]),
                            waits=(("mm", mm_f), ("mm", kf_used)), inc="sc")
                        pf_gate[tt % 2] = ev_kf
                cb = (h % 2) * 128
                def kvmm(e, half=half, cb=cb):
                    base = half * 64
                    for tt in range(NT):
                        e.matmul(ps[6][0:64, cb:cb + 64],
                                 kf_all[:, tt * 64:(tt + 1) * 64],
                                 v_all[:, tt * 128 + base: tt * 128 + base + 64],
                                 start=(tt == 0), stop=(tt == NT - 1))
                    for tt in range(NT):
                        last = e.matmul(ps[6][0:64, cb + 64:cb + 65],
                                        kf_all[:, tt * 64:(tt + 1) * 64],
                                        onesb_col[:],
                                        start=(tt == 0), stop=(tt == NT - 1))
                    return last
                extra = ()
                if first_kv:
                    extra = (("ev", ev_cvc),)
                    first_kv = False
                else:
                    extra = (("ev", kv_done_par[h % 2]),)
                mm_kv = s.emit("tensor", kvmm,
                               waits=(("sc", ev_kf), ("ev", ev_v), *extra),
                               inc="mm")
                kf_used = mm_kv
                if half == 1:
                    v_used = mm_kv
                kv_done = s.emit("vector", lambda e, h=h, cb=cb: e.tensor_copy(
                    kv_sb[0:64, h * 65:(h + 1) * 65], ps[6][0:64, cb:cb + 65]),
                    waits=(("mm", mm_kv),), inc="ev")
                kv_done_par[h % 2] = kv_done

        s.emit("sync", lambda e: e.dma_start(out=cc_kv_i[:, :], in_=kv_sb[0:64, :]),
               waits=(("ev", kv_done),), inc=c3i, dma=True)
        c3i_t = s.c(c3i)
        s.emit("gpsimd", lambda e: e.collective_compute(
            "AllReduce", ALU.add, replica_groups=RG,
            ins=[cc_kv_i.ap().opt()], outs=[cc_kv_o.ap().opt()]),
            waits=((c3i, 16),), inc=c3c, dma=True, inc_n=1)
        s.emit("sync", lambda e: e.dma_start(out=kv_sb[0:64, :], in_=cc_kv_o[:, :]),
               waits=((c3c, 1),), inc=c3o, dma=True)
        c3o_t = s.c(c3o)

        B3mm = s.c("mm")
        B3ev = s.c("ev")
        B3sc = s.c("sc")
        s.emit("tensor", lambda e: e.nop(), waits=(("ev", B3ev), ("sc", B3sc)))
        s.emit("vector", lambda e: e.nop(), waits=(("mm", B3mm), ("sc", B3sc)))
        s.emit("scalar", lambda e: e.nop(), waits=(("mm", B3mm), ("ev", B3ev)))

        lg_read = 0
        ev_ct = 0
        for w in range(NW):
            mm_lg3 = s.emit("tensor", lambda e, w=w: [
                e.matmul(ps[0][0:16, :], rw_sb[:, c * 16:(c + 1) * 16],
                         xn_sb[:, c * T + w * TW: c * T + (w + 1) * TW],
                         start=(c == 0), stop=(c == DC - 1))
                for c in range(DC)][-1],
                waits=((preg, preg_t), ("ev", xn1_done[w]), ("ev", lg_read)),
                inc="mm")
            ev_lg = s.emit("vector", lambda e: e.tensor_copy(
                t_scr[0:16, 0:512], ps[0][0:16, :]),
                waits=(("mm", mm_lg3),), inc="ev")
            lg_read = ev_lg
            for it in range(4):
                tt = w * 4 + it
                mm_tr = s.emit("tensor", lambda e, it=it: e.transpose(
                    ps[1][:, 0:16], t_scr[0:16, it * 128:(it + 1) * 128],
                    id_sb[0:16, 0:16]),
                    waits=(("ev", ev_lg), ("ev", ev_ct)), inc="mm")

                def sfm(e):
                    e.reduce_max(rcol_sb[:, 0:1], ps[1][:, 0:16], axis=AX.X)
                    e.drain()
                    return e.tensor_scalar(rcol_sb[:, 1:2], rcol_sb[:, 0:1],
                                           -1.0, None, ALU.mult)
                ev_s1 = s.emit("vector", sfm, waits=(("mm", mm_tr),), inc="ev")
                sc_e = s.emit("scalar", lambda e: e.activation(
                    pr_sb[:, 0:16], ps[1][:, 0:16], AF.Exp,
                    bias=rcol_sb[:, 1:2]),
                    waits=(("ev", ev_s1),), inc="sc")

                def top2(e, tt=tt):
                    ctv = ct_sb[:, tt * 16:(tt + 1) * 16]
                    e.reduce_sum(rcol_sb[:, 0:1], pr_sb[:, 0:16], axis=AX.X)
                    e.drain()
                    e.reciprocal(rcol_sb[:, 1:2], rcol_sb[:, 0:1])
                    e.drain()
                    e.tensor_scalar(pr_sb[:, 0:16], pr_sb[:, 0:16],
                                    rcol_sb[:, 1:2], None, ALU.mult)
                    e.drain()
                    e.reduce_max(rcol_sb[:, 0:1], pr_sb[:, 0:16], axis=AX.X)
                    e.drain()
                    e.tensor_scalar(ctv, pr_sb[:, 0:16],
                                    rcol_sb[:, 0:1], None, ALU.is_ge)
                    e.drain()
                    e.scalar_tensor_tensor(ctv, ctv, -2.0, pr_sb[:, 0:16],
                                           ALU.mult, ALU.add)
                    e.drain()
                    e.reduce_max(rcol_sb[:, 2:3], ctv, axis=AX.X)
                    e.drain()
                    e.tensor_scalar(ctv, pr_sb[:, 0:16],
                                    rcol_sb[:, 2:3], None, ALU.is_ge)
                    e.drain()
                    return e.tensor_mul(ctv, ctv, pr_sb[:, 0:16])
                ev_ct = s.emit("vector", top2, waits=(("sc", sc_e),), inc="ev")
        coeff_done = ev_ct

        kv_ready = s.emit("vector", lambda e: e.tensor_copy(
            kvb_sb[:], kv_sb[0:64, :]), waits=((c3o, c3o_t),), inc="ev")

        wd_slab = g_sb[:, 8192:12288]
        gu_freed = [B3mm, B3mm]
        dn_freed = B3mm
        slchb = [s.new_sem("slA3"), s.new_sem("slB3")]
        dnch = s.new_sem("dnch")
        hid_ps_gate = [0, 0]
        up_ps_gate = [0, 0]
        o_ps_gate = [0, 0]
        cb_gate = 0
        sil_free = [0, 0]
        crow_gate = 0
        inter_used = 0
        ev_ffn = 0
        sc_sil = 0
        first_acc = True
        acc3_last = {}
        hid3_w_last = {}
        for pe in range(-2, 16):
            soff = (pe + 2) * 512
            b = (pe + 2) % 2
            wg_slab = w_sb[:, b * 8192: b * 8192 + 4096]
            wu_slab = w_sb[:, b * 8192 + 4096: (b + 1) * 8192]
            if pe < 0:
                s.emit("gpsimd", lambda e, soff=soff, wg_slab=wg_slab: e.dma_start(
                    out=wg_slab.rearrange("p (c h) -> p c h", c=DC),
                    in_=sg_dr.rearrange("(c p) h -> p c h", p=128)
                    [:, :, soff:soff + 512]),
                    waits=(("mm", gu_freed[b]),), inc=slchb[b], dma=True)
                s.emit("gpsimd", lambda e, soff=soff, wu_slab=wu_slab: e.dma_start(
                    out=wu_slab.rearrange("p (c h) -> p c h", c=DC),
                    in_=su_dr.rearrange("(c p) h -> p c h", p=128)
                    [:, :, soff:soff + 512]),
                    waits=(("mm", gu_freed[b]),), inc=slchb[b], dma=True)
                s.emit("gpsimd", lambda e, soff=soff: e.dma_start(
                    out=wd_slab.rearrange("p (q f) -> p q f", q=4),
                    in_=sd_dr.rearrange("(q p) f -> p q f", p=128)
                    [:, soff // 128: soff // 128 + 4, :]),
                    waits=(("mm", dn_freed),), inc=dnch, dma=True)
            else:
                s.emit("gpsimd", lambda e, pe=pe, wg_slab=wg_slab: e.dma_start(
                    out=wg_slab.rearrange("p (c h) -> p c h", c=DC),
                    in_=gw_dr[pe * D:(pe + 1) * D, :].rearrange(
                        "(c p) h -> p c h", p=128)),
                    waits=(("mm", gu_freed[b]),), inc=slchb[b], dma=True)
                s.emit("gpsimd", lambda e, pe=pe, wu_slab=wu_slab: e.dma_start(
                    out=wu_slab.rearrange("p (c h) -> p c h", c=DC),
                    in_=uw_dr[pe * D:(pe + 1) * D, :].rearrange(
                        "(c p) h -> p c h", p=128)),
                    waits=(("mm", gu_freed[b]),), inc=slchb[b], dma=True)
                s.emit("gpsimd", lambda e, pe=pe: e.dma_start(
                    out=wd_slab.rearrange("p (q f) -> p q f", q=4),
                    in_=dw_dr[pe * F_EXP:(pe + 1) * F_EXP, :].rearrange(
                        "(q p) f -> p q f", p=128)),
                    waits=(("mm", dn_freed),), inc=dnch, dma=True)
            slch_t = s.c(slchb[b])
            wtd = (dnch, s.c(dnch))
            for w in range(NW):
                if pe >= 0:
                    mm_ctr = s.emit("tensor", lambda e, pe=pe, w=w: [
                        e.transpose(ps[4][0:1, it * 128:(it + 1) * 128],
                                    ct_sb[:, (w * 4 + it) * 16 + pe:
                                          (w * 4 + it) * 16 + pe + 1],
                                    id_sb[:])
                        for it in range(4)][-1],
                        waits=(("ev", coeff_done), ("ev", cb_gate),
                               (pre, pre_t)), inc="mm")
                    ev_cr = s.emit("vector", lambda e: e.tensor_copy(
                        crow_sb[:1, 0:512], ps[4][0:1, :]),
                        waits=(("mm", mm_ctr), ("mm", crow_gate),
                               (dch, dch_t)), inc="ev")
                    mm_cb = s.emit("tensor", lambda e: e.matmul(
                        ps[5][:], ones_row[:1, 0:128], crow_sb[:1, 0:512],
                        start=True, stop=True),
                        waits=(("ev", ev_cr), ("ev", cb_gate)), inc="mm")
                    crow_gate = mm_cb
                for q in range(4):
                    pr = (q * NW + w) % 2
                    pg = ps[pr]
                    pu = ps[2 + pr]
                    def gmm(e, q=q, w=w, pg=pg, wg_slab=wg_slab):
                        last = None
                        for c in range(DC):
                            last = e.matmul(
                                pg[:],
                                wg_slab[:, c * 512 + q * 128: c * 512 + (q + 1) * 128],
                                xn_sb[:, c * T + w * TW: c * T + (w + 1) * TW],
                                start=(c == 0), stop=(c == DC - 1))
                        return last
                    mm_g = s.emit("tensor", gmm,
                                  waits=((slchb[b], slch_t),
                                         ("ev", coeff_done),
                                         ("sc", hid_ps_gate[pr])), inc="mm")
                    def umm(e, q=q, w=w, pu=pu, wu_slab=wu_slab):
                        last = None
                        for c in range(DC):
                            last = e.matmul(
                                pu[:],
                                wu_slab[:, c * 512 + q * 128: c * 512 + (q + 1) * 128],
                                xn_sb[:, c * T + w * TW: c * T + (w + 1) * TW],
                                start=(c == 0), stop=(c == DC - 1))
                        return last
                    mm_u = s.emit("tensor", umm,
                                  waits=((slchb[b], slch_t), ("ev", coeff_done),
                                         ("ev", up_ps_gate[pr])), inc="mm")
                    sv = sq_q[:, pr * 512:(pr + 1) * 512]
                    sc_sil = s.emit("scalar",
                                    lambda e, sv=sv, pg=pg: e.activation(
                                        sv, pg[:], AF.Silu),
                                    waits=(("mm", mm_g), ("mm", inter_used),
                                           ("ev", sil_free[pr])),
                                    inc="sc")
                    hid_ps_gate[pr] = sc_sil
                    hv = g_sb[:, q * T + w * TW: q * T + (w + 1) * TW]
                    if pe < 0:
                        sil_free[pr] = up_ps_gate[pr] = s.emit(
                            "vector",
                            lambda e, hv=hv, sv=sv, pu=pu: e.tensor_mul(
                                hv, sv, pu[:]),
                            waits=(("mm", mm_u), ("sc", sc_sil),
                                   ("mm", inter_used)), inc="ev")
                    else:
                        tv = t_scr[:, pr * 512:(pr + 1) * 512]
                        sil_free[pr] = s.emit(
                            "vector",
                            lambda e, tv=tv, sv=sv, pu=pu: e.tensor_mul(
                                tv, sv, pu[:]),
                            waits=(("mm", mm_u), ("sc", sc_sil),
                                   ("mm", inter_used)), inc="ev")
                        s.drain("vector")
                        up_ps_gate[pr] = s.emit(
                            "vector",
                            lambda e, hv=hv, tv=tv: e.tensor_mul(
                                hv, tv, ps[5][:]),
                            waits=(("mm", crow_gate),), inc="ev")
                if pe >= 0:
                    cb_gate = up_ps_gate[(3 * NW + w) % 2]
            gu_freed[b] = s.c("mm")
            mm_o_last = 0
            for fc in range(DC):
                for w in range(NW):
                    pr = (fc * NW + w) % 2
                    po = ps[6 + pr]
                    def dmm(e, fc=fc, w=w, po=po):
                        last = None
                        for q in range(4):
                            last = e.matmul(
                                po[:],
                                wd_slab[:, q * 1024 + fc * 128:
                                        q * 1024 + (fc + 1) * 128],
                                g_sb[:, q * T + w * TW: q * T + (w + 1) * TW],
                                start=(q == 0), stop=(q == 3))
                        return last
                    mm_o = s.emit("tensor", dmm,
                                  waits=(wtd,
                                         ("ev", up_ps_gate[0]),
                                         ("ev", up_ps_gate[1]),
                                         ("ev", o_ps_gate[pr])), inc="mm")
                    av = a_sb[:, fc * T + w * TW: fc * T + (w + 1) * TW]
                    if pe == -2:
                        extra = (("st", st_x1_t),) if first_acc else ()
                        first_acc = False
                        ev_ffn = s.emit(
                            "vector",
                            lambda e, av=av, po=po: e.tensor_copy(av, po[:]),
                            waits=(("mm", mm_o), *extra), inc="ev")
                    else:
                        ev_ffn = s.emit(
                            "vector",
                            lambda e, av=av, po=po: e.tensor_add(av, av, po[:]),
                            waits=(("mm", mm_o),), inc="ev")
                    o_ps_gate[pr] = ev_ffn
                    mm_o_last = mm_o
                    if pe == 15:
                        acc3_last[(fc, w)] = ev_ffn
                        if fc == DC - 1:
                            hid3_w_last[w] = mm_o
            dn_freed = s.c("mm")
            inter_used = mm_o_last
        ffn_done = s.c("ev")

        sil_done3 = 0
        for fc in range(DC):
            for w in range(NW):
                gv = g_sb[:, fc * T + w * TW: fc * T + (w + 1) * TW]
                av = a_sb[:, fc * T + w * TW: fc * T + (w + 1) * TW]
                sil_done3 = s.emit("scalar",
                                   lambda e, gv=gv, av=av: e.activation(
                                       gv, av, AF.Silu),
                                   waits=(("ev", acc3_last[(fc, w)]),
                                          ("mm", hid3_w_last[w] if fc < 4
                                           else inter_used),
                                          ("ev", kv_ready), (c3i, c3i_t)),
                                   inc="sc")

        s.emit("gpsimd", lambda e: e.dma_start(
            out=w_sb[:, 0:8192].rearrange("p (hp j) -> p hp j", hp=8),
            in_=woP_dr.rearrange("p (hp j) -> p hp j", hp=8)),
            waits=(("mm", gu_freed[0]),), inc=d3g, dma=True)
        d3ch_t = s.c(d3ch)
        s.emit("sync", lambda e: e.dma_start(
            out=a_sb[:].rearrange("p (c t) -> p c t", c=DC),
            in_=x1_dr.rearrange("(c p) t -> p c t", p=128)),
            waits=(("sc", sil_done3), ("st", st_x1_t)), inc=d3chx, dma=True)
        d3chx_t = s.c(d3chx)
        wo_v = w_sb[:, 0:8192]

        out_done = 0
        wq_used = 0
        pq_gate = [0, 0]
        qv_used = [[0, 0], [0, 0]]
        pf_gate = [0, 0]
        ybank_gate = [0, 0]
        po_free = [0, 0]
        tv_free = [0, 0]
        oa_done = 0
        dn_gate = [0, 0]
        py_gate = [0, 0]
        crow_d_gate = [B3mm, B3mm]
        first_qv = [True, True]
        sil_last = [sil_free[0], sil_free[1]]
        d3gb = [s.new_sem("d3gA"), s.new_sem("d3gB")]
        wqh_used = [gu_freed[1], gu_freed[1]]
        pending_tail = None

        def emit_pair_tail(hp, pr2, po, crow_p, qkv1, pd_last_v):
            nonlocal oa_done
            ev_rcp = s.emit(
                "vector",
                lambda e, crow_p=crow_p: e.reciprocal(
                    crow_p[:1, 0:1024], crow_p[:1, 0:1024]),
                waits=(("sc", pd_last_v),), inc="ev")
            tvp = t_scr[:, pr2 * 512:(pr2 + 1) * 512]
            sc_oc = s.emit(
                "scalar",
                lambda e, tvp=tvp, po=po: e.copy(tvp, po[:]),
                waits=(("mm", qkv1), ("ev", tv_free[pr2])),
                inc="sc")
            mm_bc = s.emit(
                "tensor",
                lambda e, po=po, crow_p=crow_p: [
                    e.matmul(po[0:64, :], ones_row[:1, 0:64],
                             crow_p[:1, 0:512], start=True, stop=True),
                    e.matmul(po[64:128, :], ones_row[:1, 0:64],
                             crow_p[:1, 512:1024], start=True, stop=True),
                ][-1],
                waits=(("ev", ev_rcp), ("sc", sc_oc)), inc="mm")
            crow_d_gate[pr2] = mm_bc
            ovp = o_sb[:, hp * 512:(hp + 1) * 512]
            oa_done = s.emit(
                "vector",
                lambda e, ovp=ovp, tvp=tvp, po=po: e.tensor_mul(
                    ovp, tvp, po[:]),
                waits=(("mm", mm_bc),), inc="ev")
            po_free[pr2] = oa_done
            tv_free[pr2] = oa_done

        for w in range(NW):
            for H in range(2):
                hb = (w * 2 + H) % 2
                wqh_v = w_sb[:, 8192 + hb * 4096: 8192 + (hb + 1) * 4096]
                s.emit("gpsimd", lambda e, H=H, wqh_v=wqh_v: e.dma_start(
                    out=wqh_v.rearrange("p (c j) -> p c j", c=DC),
                    in_=wq1_dr.rearrange("(c p) j -> p c j", p=128)
                    [:, :, H * 512:(H + 1) * 512]),
                    waits=(("mm", wqh_used[hb]),),
                    inc=d3gb[hb], dma=True)
                d3g_t = s.c(d3gb[hb])
                for hp_l in range(4):
                    hp = H * 4 + hp_l
                    pr2 = hp % 2
                    po = ps[4 + pr2]
                    pq = ps[pr2]
                    qkv_mm = [None, None]
                    def qmm(e, hp_l=hp_l, w=w, pq=pq, wqh_v=wqh_v):
                        last = None
                        for c in range(DC):
                            last = e.matmul(
                                pq[:],
                                wqh_v[:, c * 512 + hp_l * 128:
                                      c * 512 + (hp_l + 1) * 128],
                                xn_sb[:, c * T + w * TW: c * T + (w + 1) * TW],
                                start=(c == 0), stop=(c == DC - 1))
                        return last
                    mm_q = s.emit("tensor", qmm,
                                  waits=((d3ch, d3ch_t), (d3gb[hb], d3g_t),
                                         ("sc", sil_done3),
                                         ("sc", pq_gate[pr2])), inc="mm")
                    wq_used = mm_q
                    crow_p = brow_sb[:1, 2048 + pr2 * 1024: 3072 + pr2 * 1024]
                    psd = ps[6 + pr2]
                    qv_arena = sq_q if pr2 == 0 else m_sb
                    sc_qv = [0, 0]
                    for half in range(2):
                        h = 2 * hp + half
                        qv = qv_arena[0:64, half * 512:(half + 1) * 512]
                        extra = ()
                        if first_qv[half]:
                            extra = (("ev", max(sil_last)),)
                            first_qv[half] = False
                        sc_qv[half] = s.emit(
                            "scalar",
                            lambda e, qv=qv, pq=pq, h=h, half=half: e.activation(
                                qv, pq[half * 64:(half + 1) * 64, :],
                                AF.Identity, bias=bq1c_sb[:, h:h + 1]),
                            waits=(("mm", mm_q), ("mm", qv_used[pr2][half]),
                                   *extra),
                            inc="sc")
                    pq_gate[pr2] = sc_qv[1]
                    pd_last = 0
                    for half in range(2):
                        h = 2 * hp + half
                        qv = qv_arena[0:64, half * 512:(half + 1) * 512]
                        pf = ps[2 + half]
                        mm_f = s.emit("tensor",
                                      lambda e, qv=qv, pf=pf: e.matmul(
                                          pf[0:64, :], ptb_sb[:], qv,
                                          start=True, stop=True),
                                      waits=(("sc", sc_qv[half]),
                                             ("sc", pf_gate[half]),
                                             ("ev", ybank_gate[half])), inc="mm")
                        pf_gate[half] = s.emit(
                            "scalar", lambda e, qv=qv, pf=pf: e.activation(
                                qv, pf[0:64, :], AF.Relu, scale=L1_NORM,
                                bias=eps_col[0:64, :]),
                            waits=(("mm", mm_f),), inc="sc")
                        mm_qkv = s.emit(
                            "tensor",
                            lambda e, h=h, half=half, qv=qv, po=po: e.matmul(
                                po[half * 64:(half + 1) * 64, :],
                                kvb_sb[:, h * 65: h * 65 + 64], qv,
                                start=True, stop=True),
                            waits=(("sc", pf_gate[half]), ("ev", kv_ready),
                                   ("ev", po_free[pr2])), inc="mm")
                        qkv_mm[half] = mm_qkv
                        mm_dn = s.emit(
                            "tensor",
                            lambda e, h=h, qv=qv, psd=psd: e.matmul(
                                psd[0:1, :],
                                kvb_sb[:, h * 65 + 64: h * 65 + 65], qv,
                                start=True, stop=True),
                            waits=(("sc", dn_gate[pr2]), ("ev", ev_m)),
                            inc="mm")
                        qv_used[pr2][half] = mm_dn
                        pd_last = s.emit(
                            "scalar",
                            lambda e, half=half, psd=psd, crow_p=crow_p:
                            e.activation(
                                crow_p[:1, half * 512:(half + 1) * 512],
                                psd[0:1, :], AF.Identity,
                                bias=eps_r6[:1, :1]),
                            waits=(("mm", mm_dn), ("mm", crow_d_gate[pr2])),
                            inc="sc")
                    dn_gate[pr2] = pd_last
                    if pending_tail is not None:
                        emit_pair_tail(*pending_tail)
                    pending_tail = (hp, pr2, po, crow_p, qkv_mm[1], pd_last)
                wqh_used[hb] = wq_used
            if pending_tail is not None:
                emit_pair_tail(*pending_tail)
                pending_tail = None
            o_done = oa_done
            for ic in range(DC):
                pr = ic % 2
                py = ps[2 + pr]
                def ymm(e, ic=ic, py=py):
                    last = None
                    for hp in range(8):
                        last = e.matmul(
                            py[:],
                            wo_v[:, hp * 1024 + ic * 128:
                                 hp * 1024 + (ic + 1) * 128],
                            o_sb[:, hp * 512:(hp + 1) * 512],
                            start=(hp == 0), stop=(hp == 7))
                    return last
                mm_y = s.emit("tensor", ymm,
                              waits=((d3ch, d3ch_t), (d3g, 16), ("ev", o_done),
                                     ("ev", py_gate[pr]),
                                     ("sc", pf_gate[0]), ("sc", pf_gate[1])),
                              inc="mm")
                tv = t_scr[:, pr * 512:(pr + 1) * 512]
                gv = g_sb[:, ic * T + w * TW: ic * T + (w + 1) * TW]
                av = a_sb[:, ic * T + w * TW: ic * T + (w + 1) * TW]
                py_gate[pr] = s.emit(
                    "vector",
                    lambda e, tv=tv, gv=gv, py=py, ic=ic: e.scalar_tensor_tensor(
                        tv, py[:], bo1c_sb[:, ic:ic + 1], gv, ALU.add, ALU.mult),
                    waits=(("mm", mm_y),), inc="ev")
                s.drain("vector")
                out_done = s.emit("vector",
                                  lambda e, av=av, tv=tv: e.tensor_add(av, av, tv),
                                  waits=((d3chx, d3chx_t),), inc="ev")
                tv_free[pr] = out_done
            ybank_gate = [py_gate[0], py_gate[1]]
        s.emit("sync", lambda e: e.dma_start(
            out=out_dr.rearrange("(c p) t -> p c t", p=128),
            in_=a_sb[:].rearrange("p (c t) -> p c t", c=DC)),
            waits=(("ev", out_done),), inc="st", dma=True)
        st_all = s.c("st")
        s.emit("sync", lambda e: e.nop(), waits=(("st", st_all),))

        s.finish()
    return nc



def _prep_inputs(inp):
    d = {}
    x = np.asarray(inp["L0_attn_x"])[0]
    x_fm = np.ascontiguousarray(x.T)
    d["x_slices"] = [np.ascontiguousarray(x_fm[:, c * T:(c + 1) * T])
                     for c in range(NCORES)]

    w_qkv = np.asarray(inp["L0_attn_w_qkv"])
    b_qkv = np.asarray(inp["L0_attn_b_qkv"])
    rows = (np.arange(128)[None, :] * 8 + np.arange(8)[:, None]).reshape(-1)
    for name, lo in (("q", 0), ("k", 1), ("v", 2)):
        w = w_qkv[lo * D:(lo + 1) * D][rows, :]
        d[f"w{name}T"] = np.ascontiguousarray(w.T)
        d[f"b{name}"] = np.ascontiguousarray(
            b_qkv[lo * D:(lo + 1) * D][rows])[None, :]

    c_ = np.ascontiguousarray
    d["w_oT"] = c_(np.asarray(inp["L0_attn_w_o"]).T)
    d["b_o"] = c_(np.asarray(inp["L0_attn_b_o"]))[None, :]
    d["alpha"] = np.asarray(inp["L0_ffn_alpha"], np.float32).reshape(1, 1)
    d["wgate"] = c_(np.asarray(inp["L0_ffn_w_gate"]))
    d["bgate"] = c_(np.asarray(inp["L0_ffn_b_gate"]))[None, :]
    d["W1m"] = c_(np.asarray(inp["L0_ffn_W1m"]))
    d["b1m"] = c_(np.asarray(inp["L0_ffn_b1m"]))[None, :]
    d["W2m"] = c_(np.asarray(inp["L0_ffn_W2m"]))
    d["b2m"] = c_(np.asarray(inp["L0_ffn_b2m"]))[None, :]
    d["W1e"] = c_(np.asarray(inp["L0_ffn_W1e"]).reshape(8 * D, 4096))
    d["b1e"] = c_(np.asarray(inp["L0_ffn_b1e"]))
    d["W2e"] = c_(np.asarray(inp["L0_ffn_W2e"]).reshape(8 * 4096, D))
    d["b2e"] = c_(np.asarray(inp["L0_ffn_b2e"]))
    for nm in ("q", "k", "v", "o"):
        d[f"w{nm}1T"] = c_(np.asarray(inp[f"L1_attn_W{nm}"]).T)
        d[f"b{nm}1"] = c_(np.asarray(inp[f"L1_attn_b{nm}"]))[None, :]
    d["wo1P"] = c_(d["wo1T"].reshape(8, 2, 64, D).transpose(1, 2, 0, 3).reshape(128, 8 * D))
    d["b1m_cols"] = c_(d["b1m"][0].reshape(32, 128).T)
    d["bq_cols"] = c_(d["bq"][0].reshape(8, 128).T)
    d["bo_cols"] = c_(d["b_o"][0].reshape(8, 128).T)
    d["bo1_cols"] = c_(d["bo1"][0].reshape(8, 128).T)
    d["bk1_cols"] = c_(d["bk1"][0].reshape(16, 64).T)
    d["bq1_cols"] = c_(d["bq1"][0].reshape(16, 64).T)
    d["PT"] = c_(np.asarray(inp["L1_attn_P"]).T)
    d["ident"] = np.eye(128, dtype=np.float32)
    d["router_w"] = c_(np.asarray(inp["L1_ffn_router_w"]))
    d["gate_w"] = c_(np.asarray(inp["L1_ffn_gate_w"]).reshape(16 * D, 512))
    d["up_w"] = c_(np.asarray(inp["L1_ffn_up_w"]).reshape(16 * D, 512))
    d["down_w"] = c_(np.asarray(inp["L1_ffn_down_w"]).reshape(16 * 512, D))
    d["sgate_w"] = c_(np.asarray(inp["L1_ffn_shared_gate_w"]))
    d["sup_w"] = c_(np.asarray(inp["L1_ffn_shared_up_w"]))
    d["sdown_w"] = c_(np.asarray(inp["L1_ffn_shared_down_w"]))
    return d


def kernel(**inputs):
    prep = _prep_inputs(inputs)
    cores = list(range(NCORES))
    nc = build_fused()
    shared = {
        "wkT": prep["wkT"], "wvT": prep["wvT"],
        "bk": prep["bk"], "bv": prep["bv"],
        "alpha": prep["alpha"], "wgate": prep["wgate"], "bgate": prep["bgate"],
        "W1m": prep["W1m"], "W2m": prep["W2m"], "b2m": prep["b2m"],
        "W1e": prep["W1e"], "b1e": prep["b1e"],
        "W2e": prep["W2e"], "b2e": prep["b2e"],
        "wqT": prep["wqT"], "woT": prep["w_oT"], "bo_cols": prep["bo_cols"],
        "wk1T": prep["wk1T"], "wv1T": prep["wv1T"], "bv1": prep["bv1"],
        "PT": prep["PT"], "ident": prep["ident"],
        "b1m_cols": prep["b1m_cols"], "bq_cols": prep["bq_cols"],
        "bk1_cols": prep["bk1_cols"], "bq1_cols": prep["bq1_cols"],
        "wq1T": prep["wq1T"], "wo1P": prep["wo1P"], "bo1_cols": prep["bo1_cols"],
        "router_w": prep["router_w"],
        "gate_w": prep["gate_w"], "up_w": prep["up_w"], "down_w": prep["down_w"],
        "sgate_w": prep["sgate_w"], "sup_w": prep["sup_w"],
        "sdown_w": prep["sdown_w"],
    }
    in_maps = [{**shared, "x_fm": prep["x_slices"][c]} for c in range(NCORES)]
    res = run_bass_kernel_spmd(nc, in_maps, core_ids=cores).results
    out = np.empty((1, NCORES * T, D), np.float32)
    for c in range(NCORES):
        out[0, c * T:(c + 1) * T, :] = res[c]["out_fm"].T
    return out
